# version 02
# baseline (speedup 1.0000x reference)
import math
import numpy as np
from contextlib import ExitStack

import concourse.bass as bass
import concourse.mybir as mybir
from concourse.bass_utils import run_bass_kernel_spmd

F32 = mybir.dt.float32
BF16 = mybir.dt.bfloat16
I32 = mybir.dt.int32
AF = mybir.ActivationFunctionType
ALU = mybir.AluOpType
AX = mybir.AxisListType

NCORES = 8
D = 1024
TLOC = 2048
SEQ = 8192
BATCH = 2


COMPUTE_STREAMS = ("pe", "dve", "act", "pool")
DMA_STREAMS = ("spq", "actq", "poolq")
ALL_STREAMS = COMPUTE_STREAMS + DMA_STREAMS
STREAM_ENGINE = {
    "pe": "tensor", "dve": "vector", "act": "scalar", "pool": "gpsimd",
    "spq": "sync", "actq": "scalar", "poolq": "gpsimd",
}


def _ap_range(ap):
    name = ap.tensor.name if hasattr(ap.tensor, "name") else str(ap.tensor)
    esz = mybir.dt.size(ap.dtype)
    off = ap.offset
    if not isinstance(off, (int, np.integer)):
        return name, 0, 1 << 60
    span = 1
    for step, count in ap.ap:
        span += abs(int(step)) * (int(count) - 1)
    return name, int(off) * esz, (int(off) + span) * esz


class Op:
    __slots__ = ("idx", "stream", "fn", "reads", "writes", "waits", "tag",
                 "_count")

    def __init__(self, idx, stream, fn, reads, writes, tag):
        self.idx = idx
        self.stream = stream
        self.fn = fn
        self.reads = reads
        self.writes = writes
        self.waits = {}
        self.tag = tag


class P:
    def __init__(self, name="prog"):
        self.nc = bass.Bass("TRN2", target_bir_lowering=False)
        self.name = name
        self.ops = []
        self.counts = {s: 0 for s in ALL_STREAMS}
        self.access = {}
        self.es = ExitStack()
        self._sb_names = 0
        self.handles = []
        self.prefix = ""
        self.wire = {}
        self.alloc_es = self.es
        self.floor = {s: 0 for s in ALL_STREAMS}

    def dram_in(self, name, shape, dtype=F32):
        if name in self.wire:
            return self.wire[name]
        return self.nc.dram_tensor(self.prefix + name, list(shape), dtype,
                                   kind="ExternalInput").ap()

    def dram_out(self, name, shape, dtype=F32):
        if name in self.wire:
            return self.wire[name]
        return self.nc.dram_tensor(self.prefix + name, list(shape), dtype,
                                   kind="ExternalOutput").ap()

    def dram_tmp(self, name, shape, dtype=F32, shared=False):
        return self.nc.dram_tensor(
            self.prefix + name, list(shape), dtype,
            addr_space="Shared" if shared else "Local").ap()

    def sbuf(self, name, shape, dtype=F32):
        h = self.alloc_es.enter_context(
            self.nc.sbuf_tensor(self.prefix + name, list(shape), dtype))
        return h

    def psum(self, name, shape, dtype=F32):
        h = self.alloc_es.enter_context(
            self.nc.psum_tensor(self.prefix + name, list(shape), dtype))
        return h

    def phase(self, name, wire=None):
        from contextlib import contextmanager

        @contextmanager
        def _cm():
            old = (self.prefix, self.wire, self.alloc_es)
            self.prefix = name + "_"
            self.wire = wire or {}
            self.alloc_es = ExitStack()
            try:
                yield
            finally:
                self.alloc_es.close()
                self.prefix, self.wire, self.alloc_es = old
                self.floor = {s: self.counts[s] * (16 if s in DMA_STREAMS else 1)
                              for s in ALL_STREAMS}
        return _cm()

    def rec(self, stream, fn, reads=(), writes=(), tag=""):
        idx = len(self.ops)
        op = Op(idx, stream, fn, [], [], tag)
        deps = {}

        def note_dep(pidx):
            if pidx is not None:
                deps[pidx] = True

        for ap in reads:
            trip = _ap_range(ap)
            op.reads.append(trip)
        for ap in writes:
            trip = _ap_range(ap)
            op.writes.append(trip)

        for (name, lo, hi) in op.reads:
            acc = self.access.get(name)
            if acc:
                for (pidx, plo, phi) in acc[0]:
                    if plo < hi and lo < phi:
                        note_dep(pidx)
        for (name, lo, hi) in op.writes:
            acc = self.access.get(name)
            if acc:
                for (pidx, plo, phi) in acc[0]:
                    if plo < hi and lo < phi:
                        note_dep(pidx)
                for (pidx, plo, phi) in acc[1]:
                    if plo < hi and lo < phi:
                        note_dep(pidx)

        for s, v in self.floor.items():
            if v > 0 and v > op.waits.get(s, 0):
                if s != stream or stream in ("dve", "act", "pool"):
                    op.waits[s] = v

        for pidx in deps:
            pop = self.ops[pidx]
            ps = pop.stream
            if ps == stream and (stream == "pe" or stream in DMA_STREAMS):
                continue
            need = self._op_count_after(pidx)
            if ps == stream:
                if need > op.waits.get(ps, 0):
                    op.waits[ps] = need
            else:
                if need > op.waits.get(ps, 0):
                    op.waits[ps] = need

        for (name, lo, hi) in op.reads:
            self.access.setdefault(name, ([], []))[1].append((idx, lo, hi))
        for (name, lo, hi) in op.writes:
            acc = self.access.setdefault(name, ([], []))
            acc[0][:] = [e for e in acc[0] if not (lo <= e[1] and e[2] <= hi)]
            acc[1][:] = [e for e in acc[1] if not (lo <= e[1] and e[2] <= hi)]
            acc[0].append((idx, lo, hi))

        op._count = self.counts[stream] + 1
        self.counts[stream] += 1
        self.ops.append(op)
        return op

    def _op_count_after(self, idx):
        op = self.ops[idx]
        inc = 16 if op.stream in DMA_STREAMS else 1
        return op._count * inc

    def dma(self, dst, src, q="spq", tag=""):
        def f(eng, sems):
            eng.dma_start(out=dst, in_=src).then_inc(sems[q], 16)
        return self.rec(q, f, reads=[src], writes=[dst], tag=tag or "dma")

    def tt(self, out, a, b, op=ALU.add, eng="dve", tag=""):
        def f(e, sems):
            e.tensor_tensor(out, a, b, op).then_inc(sems[eng], 1)
        return self.rec(eng, f, reads=[a, b], writes=[out], tag=tag or "tt")

    def ts(self, out, in0, s1, s2=None, op0=ALU.mult, op1=ALU.bypass, eng="dve",
           accum_out=None, tag=""):
        reads = [in0] + [x for x in (s1, s2) if isinstance(x, bass.AP)]
        writes = [out] + ([accum_out] if accum_out is not None else [])
        def f(e, sems):
            e.tensor_scalar(out, in0, s1, s2, op0, op1, accum_out=accum_out
                            ).then_inc(sems[eng], 1)
        return self.rec(eng, f, reads=reads, writes=writes, tag=tag or "ts")

    def stt(self, out, in0, scalar, in1, op0=ALU.mult, op1=ALU.add, eng="dve",
            accum_out=None, tag=""):
        reads = [in0, in1] + ([scalar] if isinstance(scalar, bass.AP) else [])
        writes = [out] + ([accum_out] if accum_out is not None else [])
        def f(e, sems):
            e.scalar_tensor_tensor(out, in0, scalar, in1, op0, op1,
                                   accum_out=accum_out).then_inc(sems[eng], 1)
        return self.rec(eng, f, reads=reads, writes=writes, tag=tag or "stt")

    def copy(self, out, in_, eng="dve", tag=""):
        def f(e, sems):
            if eng == "act":
                e.copy(out, in_).then_inc(sems[eng], 1)
            else:
                e.tensor_copy(out, in_).then_inc(sems[eng], 1)
        return self.rec(eng, f, reads=[in_], writes=[out], tag=tag or "copy")

    def memset(self, ap, val, eng="dve", tag=""):
        def f(e, sems):
            e.memset(ap, val).then_inc(sems[eng], 1)
        return self.rec(eng, f, reads=[], writes=[ap], tag=tag or "memset")

    def red(self, out, in_, op=ALU.add, eng="dve", negate=False, tag=""):
        def f(e, sems):
            e.tensor_reduce(out, in_, AX.X, op, negate=negate
                            ).then_inc(sems[eng], 1)
        return self.rec(eng, f, reads=[in_], writes=[out], tag=tag or "red")

    def act(self, out, in_, func, bias=0.0, scale=1.0, accum_out=None, tag=""):
        reads = [in_] + [x for x in (bias, scale) if isinstance(x, bass.AP)]
        writes = [out] + ([accum_out] if accum_out is not None else [])
        def f(e, sems):
            e.activation(out, in_, func, bias=bias, scale=scale,
                         accum_out=accum_out).then_inc(sems["act"], 1)
        return self.rec("act", f, reads=reads, writes=writes, tag=tag or "act")

    def mm(self, out, lhsT, rhs, start=True, stop=True, tag=""):
        def f(e, sems):
            e.matmul(out, lhsT, rhs, start=start, stop=stop).then_inc(sems["pe"], 1)
        return self.rec("pe", f, reads=[lhsT, rhs], writes=[out], tag=tag or "mm")

    def transpose(self, out, in_, ident, tag=""):
        nc = self.nc
        def f(e, sems):
            if out.dtype != F32:
                with nc.allow_low_precision(reason="bf16 transpose"):
                    e.transpose(out, in_, ident).then_inc(sems["pe"], 1)
            else:
                e.transpose(out, in_, ident).then_inc(sems["pe"], 1)
        return self.rec("pe", f, reads=[in_, ident], writes=[out], tag=tag or "tr")

    def scan(self, out, data0, data1, initial, op0=ALU.mult, op1=ALU.add,
             eng="dve", tag=""):
        reads = [data0, data1] + ([initial] if isinstance(initial, bass.AP) else [])
        def f(e, sems):
            e.tensor_tensor_scan(out, data0, data1, initial, op0, op1
                                 ).then_inc(sems[eng], 1)
        return self.rec(eng, f, reads=reads, writes=[out], tag=tag or "scan")

    def iota(self, out, pattern, base=0, channel_multiplier=0, tag=""):
        def f(e, sems):
            e.iota(out, pattern, base=base, channel_multiplier=channel_multiplier,
                   allow_small_or_imprecise_dtypes=True).then_inc(sems["pool"], 1)
        return self.rec("pool", f, reads=[], writes=[out], tag=tag or "iota")

    def affine_select(self, out, in_, compare_op, fill, pattern, base=0,
                      channel_multiplier=0, tag=""):
        def f(e, sems):
            e.affine_select(out=out, in_=in_, compare_op=compare_op, fill=fill,
                            base=base, pattern=pattern,
                            channel_multiplier=channel_multiplier
                            ).then_inc(sems["pool"], 1)
        return self.rec("pool", f, reads=[in_], writes=[out], tag=tag or "asel")

    def reciprocal(self, out, in_, tag=""):
        def f(e, sems):
            e.reciprocal(out, in_).then_inc(sems["dve"], 1)
        return self.rec("dve", f, reads=[in_], writes=[out], tag=tag or "recip")

    def top8(self, out, in_, tag=""):
        def f(e, sems):
            e.max(out, in_).then_inc(sems["dve"], 1)
        return self.rec("dve", f, reads=[in_], writes=[out], tag=tag or "top8")

    def build(self):
        nc = self.nc
        sems = {}
        for s in ALL_STREAMS:
            sems[s] = self.es.enter_context(nc.semaphore("sem_" + s))
        block = self.es.enter_context(nc.Block())

        per_stream = {s: [] for s in ALL_STREAMS}
        for op in self.ops:
            per_stream[op.stream].append(op)

        final = {s: self.counts[s] * (16 if s in DMA_STREAMS else 1)
                 for s in ALL_STREAMS}

        def make_section(streams_here):
            def emit(eng):
                waited = {s: 0 for s in ALL_STREAMS}
                oplist = []
                for s in streams_here:
                    oplist.extend(per_stream[s])
                oplist.sort(key=lambda o: o.idx)
                for op in oplist:
                    for s, v in op.waits.items():
                        if v > waited[s]:
                            eng.wait_ge(sems[s], v)
                            waited[s] = v
                    op.fn(eng, sems)
                return waited
            return emit

        emitted_sections = {}

        @block.tensor
        def _(eng):
            emitted_sections["pe"] = make_section(["pe"])(eng)

        @block.vector
        def _(eng):
            emitted_sections["dve"] = make_section(["dve"])(eng)

        @block.scalar
        def _(eng):
            w = make_section(["act", "actq"])(eng)
            emitted_sections["act"] = w

        @block.gpsimd
        def _(eng):
            w = make_section(["pool", "poolq"])(eng)
            for s in ALL_STREAMS:
                if final[s] > w.get(s, 0):
                    eng.wait_ge(sems[s], final[s])
            emitted_sections["pool"] = w

        @block.sync
        def _(eng):
            emitted_sections["spq"] = make_section(["spq"])(eng)

        return nc

    def run(self, in_maps, core_ids=None, **kw):
        if getattr(self, "_built_nc", None) is None:
            self._built_nc = self.build()
            with self.es:
                pass
        res = run_bass_kernel_spmd(self._built_nc, in_maps,
                                   core_ids=core_ids or list(range(NCORES)),
                                   **kw)
        return res


NCH = D // 128


def ln_stats_fm(p, x_sb, T, ones_col, eps, pref, sq_scratch, psum_stats,
                mean_bc, rstd_bc, bcast_psum, ones_row):
    nt = T // 512
    srow = p.sbuf(pref + "srow", [1, 2048], F32)
    epsc = p.sbuf(pref + "eps", [1, 1], F32)
    p.memset(epsc[:], eps, eng="pool")
    for it in range(nt):
        t0, t1 = it * 512, (it + 1) * 512
        for c in range(NCH):
            p.mm(psum_stats[0:1, 0:512], ones_col[:, 0:1],
                 x_sb[:, c * T + t0: c * T + t1],
                 start=(c == 0), stop=(c == NCH - 1), tag=pref + "sum")
        for c in range(NCH):
            sl = sq_scratch[:, (c % 2) * 512:(c % 2) * 512 + 512]
            p.tt(sl, x_sb[:, c * T + t0: c * T + t1],
                 x_sb[:, c * T + t0: c * T + t1], op=ALU.mult, eng="pool",
                  tag=pref + "sq")
            p.mm(psum_stats[0:1, 512:1024], ones_col[:, 0:1], sl,
                 start=(c == 0), stop=(c == NCH - 1), tag=pref + "sumsq")
        mean_r = srow[0:1, 0:512]
        ms_r = srow[0:1, 512:1024]
        var_r = srow[0:1, 1024:1536]
        rstd_r = srow[0:1, 1536:2048]
        p.act(mean_r, psum_stats[0:1, 0:512], AF.Copy, scale=1.0 / D,
              tag=pref + "mean")
        p.tt(ms_r, mean_r, mean_r, op=ALU.mult,
              tag=pref + "ms")
        p.stt(var_r, psum_stats[0:1, 512:1024], 1.0 / D, ms_r,
              op0=ALU.mult, op1=ALU.subtract, tag=pref + "var")
        p.act(var_r, var_r, AF.Sqrt, bias=epsc[:], tag=pref + "sqrt")
        p.reciprocal(rstd_r, var_r, tag=pref + "recip")
        p.mm(bcast_psum[:, 0:512], ones_row[0:1, :], mean_r, tag=pref + "bm")
        p.copy(mean_bc[:, t0:t1], bcast_psum[:, 0:512], eng="dve",
               tag=pref + "bmc")
        p.mm(bcast_psum[:, 512:1024], ones_row[0:1, :], rstd_r, tag=pref + "br")
        p.copy(rstd_bc[:, t0:t1], bcast_psum[:, 512:1024], eng="dve",
               tag=pref + "brc")


def ln_apply_fm(p, x_sb, out_sb, T, mean_bc, rstd_bc, pref, scratch,
                gain_col=None, bias_col=None, out2_sb=None):
    for c in range(NCH):
        xs = x_sb[:, c * T:(c + 1) * T]
        sc = scratch[:, 0:T]
        p.tt(sc, xs, mean_bc[:, 0:T], op=ALU.subtract, tag=pref + "sub")
        dst = out_sb[:, c * T:(c + 1) * T]
        fdst = out2_sb[:, c * T:(c + 1) * T] if out2_sb is not None else None
        if gain_col is None:
            p.tt(fdst if fdst is not None else dst, sc, rstd_bc[:, 0:T],
                 op=ALU.mult, tag=pref + "mul")
        else:
            p.tt(sc, sc, rstd_bc[:, 0:T], op=ALU.mult, tag=pref + "mul")
            p.stt(fdst if fdst is not None else dst, sc, gain_col[c],
                  bias_col[c].broadcast_to([128, T]),
                  op0=ALU.mult, op1=ALU.add, tag=pref + "aff")
        if fdst is not None:
            p.copy(dst, fdst, eng="pool", tag=pref + "cp2")


def ln_streaming(p, cm, T, x_tile, emit_out, pref, it_done=None):
    nt = T // 512
    srow = cm.srow
    for it in range(nt):
        for c in range(NCH):
            xs = x_tile(c, it)
            p.mm(cm.psum_stats[0:1, 0:512], cm.ones_col[:, 0:1], xs,
                 start=(c == 0), stop=(c == NCH - 1), tag=pref + "sum")
            sl = cm.sq_scratch[:, 0:512]
            p.tt(sl, xs, xs, op=ALU.mult, eng="pool", tag=pref + "sq")
            p.mm(cm.psum_stats[0:1, 512:1024], cm.ones_col[:, 0:1], sl,
                 start=(c == 0), stop=(c == NCH - 1), tag=pref + "sumsq")
        mean_r = srow[0:1, 0:512]
        ms_r = srow[0:1, 512:1024]
        var_r = srow[0:1, 1024:1536]
        p.act(mean_r, cm.psum_stats[0:1, 0:512], AF.Copy, scale=1.0 / D,
              tag=pref + "mean")
        p.tt(ms_r, mean_r, mean_r, op=ALU.mult,
              tag=pref + "ms")
        p.stt(var_r, cm.psum_stats[0:1, 512:1024], 1.0 / D, ms_r,
              op0=ALU.mult, op1=ALU.subtract, tag=pref + "var")
        p.act(var_r, var_r, AF.Sqrt, bias=cm.epsc[0:1, 0:1], tag=pref + "sqrt")
        p.reciprocal(srow[0:1, 1536:2048], var_r, tag=pref + "recip")
        p.mm(cm.bcast_psum[:, 0:512], cm.ones_row[0:1, :], mean_r,
             tag=pref + "bm")
        p.copy(cm.mean_b1[:, 0:512], cm.bcast_psum[:, 0:512], eng="dve",
               tag=pref + "bmc")
        p.mm(cm.bcast_psum[:, 512:1024], cm.ones_row[0:1, :],
             srow[0:1, 1536:2048], tag=pref + "br")
        p.copy(cm.rstd_b1[:, 0:512], cm.bcast_psum[:, 512:1024], eng="dve",
               tag=pref + "brc")
        for c in range(NCH):
            xs = x_tile(c, it)
            sc = cm.scratch[:, 0:512]
            p.tt(sc, xs, cm.mean_b1[:, 0:512], op=ALU.subtract,
                 tag=pref + "asub")
            p.tt(sc, sc, cm.rstd_b1[:, 0:512], op=ALU.mult, tag=pref + "amul")
            emit_out(c, it, sc)
        if it_done is not None:
            it_done(it)


def load_wT_bf16(p, w_dram, d_in, d_out, name, q="poolq", region=None):
    nin = d_in // 128
    h = region if region is not None else p.sbuf(
        name + "_sb", [128, nin * d_out], BF16)
    hap = h if region is not None else h[:, 0:nin * d_out]
    if d_out == 1:
        for c in range(nin):
            p.dma(hap[:, c:c + 1], w_dram[c * 128:(c + 1) * 128, :], q=q,
                  tag=name + "_ld")
    else:
        p.dma(hap.rearrange("p (c n) -> p c n", n=d_out),
              w_dram.rearrange("(c p) n -> p c n", p=128), q=q,
              tag=name + "_ld")
    return [hap[:, c * d_out:(c + 1) * d_out] for c in range(nin)]


def load_wT_f32(p, w_dram, d_in, d_out, name, q="spq"):
    nin = d_in // 128
    h = p.sbuf(name + "_sb", [128, nin * d_out], F32)
    hap = h[:, 0:nin * d_out]
    if d_out == 1:
        for c in range(nin):
            p.dma(hap[:, c:c + 1], w_dram[c * 128:(c + 1) * 128, :], q=q,
                  tag=name + "_ld")
    else:
        p.dma(hap.rearrange("p (c n) -> p c n", n=d_out),
              w_dram.rearrange("(c p) n -> p c n", p=128), q=q,
              tag=name + "_ld")
    return [hap[:, c * d_out:(c + 1) * d_out] for c in range(nin)]


def load_col_f32(p, v_dram, n, name, q="spq"):
    h = p.sbuf(name + "_sb", [128, n // 128], F32)
    cols = []
    for c in range(n // 128):
        sl = h[:, c:c + 1]
        p.dma(sl, v_dram[c * 128:(c + 1) * 128].rearrange("(a b) -> a b", b=1), q=q,
              tag=name + "_ld")
        cols.append(sl)
    return cols


def matmul_fm(p, wT_tiles, x_bf, d_in, d_out, T, psum_pool, evac, pref,
              x_cols=None):
    nin = d_in // 128
    nt = T // 512
    nfo = d_out // 128
    for fo in range(nfo):
        for it in range(nt):
            ps = psum_pool[(fo * nt + it) % len(psum_pool)][:, 0:512]
            for c in range(nin):
                xs = (x_cols[c][:, it * 512:(it + 1) * 512] if x_cols is not None
                      else x_bf[:, c * T + it * 512: c * T + (it + 1) * 512])
                p.mm(ps, wT_tiles[c][:, fo * 128:(fo + 1) * 128], xs,
                     start=(c == 0), stop=(c == nin - 1), tag=pref + "mm")
            evac(fo, it, ps)


def make_identity(p, name, dtype):
    ident = p.sbuf(name, [128, 128], dtype)
    idc = p.sbuf(name + "_c", [128, 1], F32)
    p.iota(ident[:], [[1, 128]], base=0, channel_multiplier=0, tag=name + "i")
    p.iota(idc[:], [[0, 1]], base=0, channel_multiplier=1, tag=name + "p")
    p.ts(ident[:], ident[:], idc[:, 0:1], None, op0=ALU.is_equal, tag=name)
    return ident


class Arena:
    def __init__(self, p, name, cols):
        self.cols = cols
        self.t = p.sbuf(name, [128, cols], F32)

    def f32(self, c0, cols):
        assert c0 + cols <= self.cols, (c0, cols, self.cols)
        return self.t[:, c0:c0 + cols]

    def bf16(self, c0, cols_bf16):
        assert c0 + (cols_bf16 + 1) // 2 <= self.cols, (c0, cols_bf16)
        return self.t[:, c0:c0 + (cols_bf16 + 1) // 2].bitcast(BF16)


class Common:
    def __init__(self, p, T=TLOC, slim=False):
        self.p = p
        self.T = T
        self.ones_col = p.sbuf("cmn_ones_col", [128, 1], F32)
        p.memset(self.ones_col[:], 1.0, eng="pool")
        self.ones_row = p.sbuf("cmn_ones_row", [1, 128], F32)
        p.memset(self.ones_row[:], 1.0, eng="pool")
        self.psum_stats = p.psum("cmn_pstats", [2, 1024], F32)
        self.bcast_psum = p.psum("cmn_pbcast", [128, 1024], F32)
        if slim:
            self.sq_scratch = p.sbuf("cmn_sq", [128, 512], F32)
            self.scratch = p.sbuf("cmn_scr", [128, 1024], F32)
            self.mean_b1 = p.sbuf("cmn_mb1", [128, 512], F32)[:, 0:512]
            self.rstd_b1 = p.sbuf("cmn_rb1", [128, 512], F32)[:, 0:512]
        else:
            self.sq_scratch = p.sbuf("cmn_sq", [128, 1024], F32)
            self.mean_bc = p.sbuf("cmn_meanbc", [128, T], F32)
            self.rstd_bc = p.sbuf("cmn_rstdbc", [128, T], F32)
            self.scratch = p.sbuf("cmn_scr", [128, T], F32)
            self.mean_b1 = self.mean_bc[:, 0:512]
            self.rstd_b1 = self.rstd_bc[:, 0:512]
        self.srow = p.sbuf("cmn_srow", [1, 2048], F32)
        self.epsc = p.sbuf("cmn_eps", [1, 1], F32)
        p.memset(self.epsc[:], 1e-5, eng="pool")

    def ln(self, x_sb, out_sb, pref, gain_col=None, bias_col=None, out2_sb=None):
        p = self.p
        ln_stats_fm(p, x_sb, self.T, self.ones_col, 1e-5, pref, self.sq_scratch,
                    self.psum_stats, self.mean_bc, self.rstd_bc,
                    self.bcast_psum, self.ones_row)
        ln_apply_fm(p, x_sb, out_sb, self.T, self.mean_bc, self.rstd_bc, pref,
                    self.scratch, gain_col, bias_col, out2_sb)


def load_stream(p, T, name="xs"):
    x_dram = p.dram_in(name, [D, T])
    xsb = p.sbuf("stream", [128, NCH * T], F32)
    for c in range(NCH):
        p.dma(xsb[:, c * T:(c + 1) * T], x_dram[c * 128:(c + 1) * 128, :],
              q="spq", tag="ldstream")
    return xsb


def store_stream(p, xsb, T, name="ys"):
    y_dram = p.dram_out(name, [D, T])
    for c in range(NCH):
        p.dma(y_dram[c * 128:(c + 1) * 128, :], xsb[:, c * T:(c + 1) * T],
              q="actq", tag="ststream")



def build_l3f(T=TLOC, p=None):
    p = p or P("l3f")
    cm = Common(p, T)
    xsb = load_stream(p, T)
    wT = p.dram_in("wT", [D, D])
    wf = p.sbuf("wf", [128, NCH * D], F32)
    wsgn = p.sbuf("wsgn", [128, NCH * D], BF16)
    wtiles = []
    for c in range(NCH):
        sl_f = wf[:, c * D:(c + 1) * D]
        sl_s = wsgn[:, c * D:(c + 1) * D]
        p.dma(sl_f, wT[c * 128:(c + 1) * 128, :], q="poolq", tag="ldw")
        p.ts(sl_s, sl_f, 0.0, None, op0=ALU.is_ge, tag="sign")
        p.ts(sl_s, sl_s, 2.0, -1.0, op0=ALU.mult, op1=ALU.add, tag="sign2")
        wtiles.append(sl_s)
    xn_bf = p.sbuf("xn_bf", [128, NCH * T], BF16)
    cm.ln(xsb, xn_bf, "ln_")
    pp = [p.psum("mmp%d" % i, [128, 512], F32) for i in range(4)]

    def evac(fo, it, ps):
        xs = xsb[:, fo * T + it * 512: fo * T + (it + 1) * 512]
        p.tt(xs, ps, xs, op=ALU.add, tag="resid")

    matmul_fm(p, wtiles, xn_bf, D, D, T, pp, evac, "l3f_")
    store_stream(p, xsb, T)
    return p


def load_row_f32(p, v_dram, n, name, q="spq"):
    h = p.sbuf(name + "_sb", [1, n], F32)
    p.dma(h[0:1, :], v_dram.rearrange("(a b) -> a b", a=1), q=q, tag=name + "_ld")
    return h


def bcast_row(p, row_ap, out_sb, T, bcast_psum, ones_row, pref):
    for it in range(T // 512):
        ps = bcast_psum[:, (it % 2) * 512:(it % 2) * 512 + 512]
        p.mm(ps, ones_row[0:1, :], row_ap[0:1, it * 512:(it + 1) * 512],
             tag=pref + "bc")
        p.copy(out_sb[:, it * 512:(it + 1) * 512], ps, eng="dve",
               tag=pref + "bcc")


def small_vec_mm(p, cm, x_cols, w_dram_bf, k, m, name, bias_row=None,
                 func=None, psum=None):
    out = p.sbuf(name + "_o", [1, m], F32)
    nk = k // 128
    for t0 in range(0, m, 512):
        t1 = min(t0 + 512, m)
        ps = psum[0:1, 0:t1 - t0]
        for c in range(nk):
            p.mm(ps, x_cols[c], w_dram_bf[c][:, t0:t1],
                 start=(c == 0), stop=(c == nk - 1), tag=name + "_mm")
        p.copy(out[0:1, t0:t1], ps, eng="dve", tag=name + "_ev")
    if bias_row is not None:
        p.tt(out[0:1, :], out[0:1, :], bias_row[0:1, :m], op=ALU.add,
             tag=name + "_b")
    if func is not None:
        p.act(out[0:1, :], out[0:1, :], func, tag=name + "_f")
    return out


def row_to_cols(p, row_ap, n, identf, name, psum):
    h = p.sbuf(name + "_cols", [128, n // 128], F32)
    cols = []
    for c in range(n // 128):
        ps = psum[:, c % 4:(c % 4) + 1]
        p.transpose(ps, row_ap[0:1, c * 128:(c + 1) * 128], identf[0:1, 0:1],
                    tag=name + "_tr")
        p.copy(h[:, c:c + 1], ps, eng="act", tag=name + "_cp")
        cols.append(h[:, c:c + 1])
    return cols


def build_l0f(T=TLOC, p=None, fused=False):
    p = p or P("l0f")
    cm = Common(p, T)
    identf = make_identity(p, "identf", F32)
    xs_d = p.dram_in("xs", [D, T])
    ys_d = p.dram_out("ys", [D, T])

    cn_w1 = load_wT_bf16(p, p.dram_in("cn_w1", [1024, 1024]), 1024, 1024, "cnw1")
    cn_w2 = load_wT_bf16(p, p.dram_in("cn_w2", [1024, 512]), 1024, 512, "cnw2")
    cn_w3 = load_wT_bf16(p, p.dram_in("cn_w3", [512, 256]), 512, 256, "cnw3")
    cn_w4 = load_wT_bf16(p, p.dram_in("cn_w4", [256, 1]), 256, 1, "cnw4")
    gc_w1 = load_wT_f32(p, p.dram_in("gc_w1", [1024, 512]), 1024, 512, "gcw1")
    gc_w2 = load_wT_f32(p, p.dram_in("gc_w2", [512, 256]), 512, 256, "gcw2")
    gc_w3 = load_wT_f32(p, p.dram_in("gc_w3", [256, 1]), 256, 1, "gcw3")
    cn_b1 = load_col_f32(p, p.dram_in("cn_b1", [1024]), 1024, "cnb1")
    cn_b2 = load_col_f32(p, p.dram_in("cn_b2", [512]), 512, "cnb2")
    cn_b3 = load_col_f32(p, p.dram_in("cn_b3", [256]), 256, "cnb3")
    cn_b4 = load_row_f32(p, p.dram_in("cn_b4", [1]), 1, "cnb4")
    gc_b1 = load_row_f32(p, p.dram_in("gc_b1", [512]), 512, "gcb1")
    gc_b2 = load_row_f32(p, p.dram_in("gc_b2", [256]), 256, "gcb2")
    gc_b3 = load_row_f32(p, p.dram_in("gc_b3", [1]), 1, "gcb3")
    ar_r = load_row_f32(p, p.dram_in("ar", [1]), 1, "ar")
    if fused:
        gsums4 = p.dram_in("gsums4", [128, 32])
    else:
        gsums = p.dram_in("gsums", [D, 4])

    out = p.sbuf("outbuf", [128, NCH * T], F32)
    xbuf = p.sbuf("xbuf", [128, 1024], F32)
    xslot = [0]

    def x_tile(c, it):
        s = xslot[0] % 2
        xslot[0] += 1
        sl = xbuf[:, s * 512:(s + 1) * 512]
        p.dma(sl, xs_d[c * 128:(c + 1) * 128, it * 512:(it + 1) * 512],
              q="spq", tag="xt")
        return sl

    def emit_out(c, it, sc):
        p.copy(out[:, c * T + it * 512:c * T + (it + 1) * 512], sc,
               eng="pool", tag="lnout")

    ln_streaming(p, cm, T, x_tile, emit_out, "ln_")

    pp = [p.psum("mmp%d" % i, [128, 512], F32) for i in range(3)]
    spsum = p.psum("spsum", [128, 4], F32)

    gs = p.sbuf("gs", [128, NCH], F32)
    gctx_cols = []
    if fused:
        gall = p.sbuf("gall", [128, 32], F32)
        p.dma(gall[:], gsums4, q="spq", tag="gsum_ld")
    for c in range(NCH):
        if fused:
            p.red(gs[:, c:c + 1], bass.AP(gall, c, [[32, 128], [8, 4]]),
                  op=ALU.add, tag="gsum_red")
        else:
            gsl = p.sbuf("gsl%d" % c, [128, 4], F32)
            p.dma(gsl[:], gsums[c * 128:(c + 1) * 128, :], q="spq",
                  tag="gsum_ld")
            p.red(gs[:, c:c + 1], gsl[:], op=ALU.add, tag="gsum_red")
        p.ts(gs[:, c:c + 1], gs[:, c:c + 1], 1.0 / SEQ, None, op0=ALU.mult,
             tag="gsum_sc")
        gctx_cols.append(gs[:, c:c + 1])
    h1 = small_vec_mm(p, cm, gctx_cols, gc_w1, 1024, 512, "gf1",
                      bias_row=gc_b1, func=AF.Gelu, psum=cm.psum_stats)
    h1c = row_to_cols(p, h1[0:1, :], 512, identf, "gf1c", spsum)
    h2 = small_vec_mm(p, cm, h1c, gc_w2, 512, 256, "gf2",
                      bias_row=gc_b2, func=AF.Gelu, psum=cm.psum_stats)
    h2c = row_to_cols(p, h2[0:1, :], 256, identf, "gf2c", spsum)
    gf = small_vec_mm(p, cm, h2c, gc_w3, 256, 1, "gf3",
                      bias_row=gc_b3, func=AF.Sigmoid, psum=cm.psum_stats)

    s0 = p.sbuf("s0", [1, 1], F32)
    s1 = p.sbuf("s1", [1, 1], F32)
    p.tt(s0[0:1, :], gf[0:1, :], ar_r[0:1, :], op=ALU.mult, tag="c0")
    p.ts(s0[0:1, :], s0[0:1, :], 1.4, None, op0=ALU.mult, tag="c1")
    p.stt(s0[0:1, :], ar_r[0:1, :], -1.0, s0[0:1, :], op0=ALU.mult,
          op1=ALU.add, tag="c2")
    p.ts(s0[0:1, :], s0[0:1, :], 0.1, None, op0=ALU.add, tag="c3")
    p.ts(s1[0:1, :], ar_r[0:1, :], 0.6, None, op0=ALU.mult, tag="c4")

    h1t = p.sbuf("cyc_h1", [128, NCH * 512], BF16)
    h2t = p.sbuf("cyc_h2", [128, 4 * 512], BF16)
    h3t = p.sbuf("cyc_h3", [128, 2 * 512], BF16)
    xn_bf_t = p.sbuf("xnbf_t", [128, NCH * 512], BF16)
    tf_row = p.sbuf("tf_row", [1, 512], F32)
    coup_t = cm.mean_bc[:, 512:1024]
    src_map = [4, 5, 6, 7, 0, 1, 2, 3]
    sgn = [1, 1, -1, -1, -1, -1, 1, 1]

    nt = T // 512
    for cyc in range(3):
        for it in range(nt):
            t0, t1 = it * 512, (it + 1) * 512
            for c in range(NCH):
                p.copy(xn_bf_t[:, c * 512:(c + 1) * 512],
                       out[:, c * T + t0:c * T + t1], eng="pool",
                       tag="cybf%d" % cyc)
            def ev(bias_cols, dst):
                def f(fo, it2, ps):
                    p.act(dst[:, fo * 512:(fo + 1) * 512], ps, AF.Gelu,
                          bias=bias_cols[fo], tag="cyev")
                return f
            matmul_fm(p, cn_w1, xn_bf_t, 1024, 1024, 512, pp,
                      ev(cn_b1, h1t), "cy1_%d_%d_" % (cyc, it))
            matmul_fm(p, cn_w2, h1t, 1024, 512, 512, pp,
                      ev(cn_b2, h2t), "cy2_%d_%d_" % (cyc, it))
            matmul_fm(p, cn_w3, h2t, 512, 256, 512, pp,
                      ev(cn_b3, h3t), "cy3_%d_%d_" % (cyc, it))
            ps = cm.psum_stats[0:1, 0:512]
            for c in range(2):
                p.mm(ps, cn_w4[c][:, 0:1], h3t[:, c * 512:(c + 1) * 512],
                     start=(c == 0), stop=(c == 1), tag="cy4")
            p.act(tf_row[0:1, 0:512], ps, AF.Sigmoid, bias=cn_b4[0:1, 0:1],
                  tag="cy4s")
            p.ts(tf_row[0:1, 0:512], tf_row[0:1, 0:512], s1[0:1, 0:1],
                 s0[0:1, 0:1], op0=ALU.mult, op1=ALU.add, tag="coup")
            p.mm(cm.bcast_psum[:, 0:512], cm.ones_row[0:1, :],
                 tf_row[0:1, 0:512], tag="coupb")
            p.copy(coup_t, cm.bcast_psum[:, 0:512], eng="dve", tag="coupbc")
            for a in range(4):
                b = src_map[a]
                tmp = cm.scratch[:, 0:512]
                twv = cm.scratch[:, 512:1024]
                p.copy(tmp, out[:, a * T + t0:a * T + t1], eng="pool",
                       tag="twsv")
                p.tt(twv, coup_t, out[:, b * T + t0:b * T + t1],
                     op=ALU.mult, tag="twm")
                p.tt(out[:, a * T + t0:a * T + t1],
                     out[:, a * T + t0:a * T + t1], twv,
                     op=ALU.add if sgn[a] > 0 else ALU.subtract, tag="twu")
                p.tt(twv, coup_t, tmp, op=ALU.mult, tag="twm2")
                p.tt(out[:, b * T + t0:b * T + t1],
                     out[:, b * T + t0:b * T + t1], twv,
                     op=ALU.add if sgn[b] > 0 else ALU.subtract, tag="twu2")

    rbuf = p.sbuf("rbuf", [128, 1024], F32)
    for c in range(NCH):
        for it in range(nt):
            t0, t1 = it * 512, (it + 1) * 512
            xt = x_tile(c, it)
            rb = rbuf[:, (it % 2) * 512:(it % 2) * 512 + 512]
            p.tt(rb, xt, out[:, c * T + t0:c * T + t1], op=ALU.add,
                 tag="resid")
            p.dma(ys_d[c * 128:(c + 1) * 128, t0:t1], rb, q="actq",
                  tag="stres")
    return p



def build_l1f(T=TLOC, p=None):
    p = p or P("l1f")
    cm = Common(p, T, slim=True)
    xsb = load_stream(p, T)
    ident = make_identity(p, "identb", F32)

    gate_wTf = load_wT_f32(p, p.dram_in("gate_wT", [1024, 8]), 1024, 8, "gwt")
    wg_d = p.dram_in("wg", [8, 1024, 1408])
    wu_d = p.dram_in("wu", [8, 1024, 1408])
    wd_d = p.dram_in("wd", [8, 1408, 1024])
    swg_d = p.dram_in("swg", [1024, 2816])
    swu_d = p.dram_in("swu", [1024, 2816])
    swd_d = p.dram_in("swd", [2816, 1024])

    A = Arena(p, "A", 27904)
    xn_bf = A.bf16(0, NCH * T)
    EB = 8192
    wg_r = A.bf16(EB, 8 * 1408)
    wu_r = A.bf16(13824, 8 * 1408)
    wd_r = A.bf16(19456, 11 * 1024)
    gt_r = A.bf16(25088, 11 * 512)
    su_h = p.sbuf("su_t", [128, 512], BF16)
    su_r = su_h[:, 0:512]
    swd_r = A.bf16(EB, 22 * 1024)
    gsw_r = A.bf16(19456, 22 * 512)
    wjit_r = A.bf16(25088, 2 * 2816)

    pp = [p.psum("mmp%d" % i, [128, 512], F32) for i in range(3)]
    rlog = p.psum("rlog", [128, 32], F32)

    nsub = T // 128
    scores = p.sbuf("scores", [128, 8], F32)
    srtmp = p.sbuf("srtmp", [128, 8], F32)
    m8 = p.sbuf("m8", [128, 8], F32)
    mcol = p.sbuf("mcol", [128, 4], F32)
    comb_tm = p.sbuf("comb_tm", [128, 8], F32)
    comb8 = p.sbuf("comb8", [8, T], F32)
    cps = pp[1]
    rlog_sb = p.sbuf("rlog_sb", [128, 32], F32)

    def emit_xn(c, it, sc):
        p.copy(xn_bf[:, c * T + it * 512: c * T + (it + 1) * 512], sc,
               eng="pool", tag="xnbf")
        if c == 0:
            p.memset(rlog_sb[:], 0.0, eng="dve", tag="rt_z")
        for sub in range(4):
            p.mm(rlog[:, sub * 8:(sub + 1) * 8],
                 sc[:, sub * 128:(sub + 1) * 128], gate_wTf[c][:, 0:8],
                 start=True, stop=True, tag="rt_mm")
        p.tt(rlog_sb[:], rlog_sb[:], rlog[:, 0:32], op=ALU.add, tag="rt_acc")

    def router_it(it):
        pass

    def x_tile(c, it):
        return xsb[:, c * T + it * 512: c * T + (it + 1) * 512]

    def do_router(s):
        rsl = rlog_sb[:, (s % 4) * 8:(s % 4) * 8 + 8]
        p.red(mcol[:, 0:1], rsl, op=ALU.max, eng="dve", negate=True,
              tag="rt_max")
        p.act(scores[:, 0:8], rsl, AF.Exp, bias=mcol[:, 0:1],
              tag="rt_exp")
        p.red(mcol[:, 1:2], scores[:, 0:8], op=ALU.add, tag="rt_sum")
        p.reciprocal(mcol[:, 2:3], mcol[:, 1:2], tag="rt_rcp")
        p.ts(scores[:, 0:8], scores[:, 0:8], mcol[:, 2:3], None,
             op0=ALU.mult, tag="rt_nrm")
        p.top8(m8[:, 0:8], scores[:, 0:8], tag="rt_top")
        p.tt(mcol[:, 0:1], m8[:, 0:1], m8[:, 1:2], op=ALU.add, tag="rt_ws")
        p.ts(mcol[:, 0:1], mcol[:, 0:1], 1e-20, None, op0=ALU.add, tag="rt_eps")
        p.reciprocal(mcol[:, 1:2], mcol[:, 0:1], tag="rt_rcp2")
        p.ts(mcol[:, 1:2], mcol[:, 1:2], 2.5, None, op0=ALU.mult, tag="rt_sc")
        p.ts(srtmp[:, 0:8], scores[:, 0:8], m8[:, 1:2], None, op0=ALU.is_ge,
             tag="rt_msk")
        p.tt(comb_tm[:, 0:8], scores[:, 0:8], srtmp[:, 0:8], op=ALU.mult,
             tag="rt_cw")
        p.ts(comb_tm[:, 0:8], comb_tm[:, 0:8], mcol[:, 1:2], None,
             op0=ALU.mult, tag="rt_cw2")
        p.transpose(cps[0:8, 0:128], comb_tm[:, 0:8], ident[:], tag="rt_tr")
        p.copy(comb8[:, s * 128:(s + 1) * 128], cps[0:8, 0:128], eng="dve",
               tag="rt_cp")

    def router_it2(it):
        router_it(it)
        for sub in range(4):
            do_router(it * 4 + sub)

    ln_streaming(p, cm, T, x_tile, emit_xn, "ln_", it_done=router_it2)

    sel = p.sbuf("sel", [8, 8 * 128], F32)
    selp = p.sbuf("selp", [8, 1], F32)
    p.iota(selp[0:8, :], [[0, 1]], base=0, channel_multiplier=1, tag="selp")
    for e in range(8):
        p.ts(sel[0:8, e * 128:(e + 1) * 128],
             bass.AP(selp, 0, [[1, 8], [0, 128]]), float(e), None,
             op0=ALU.is_equal, tag="sel%d" % e)

    comb_sb = p.sbuf("comb_sb", [128, 512], F32)
    nt = T // 512

    for e in range(8):
        wg = load_wT_bf16(p, wg_d[e], 1024, 1408, "wg%d" % e, region=wg_r)
        wu = load_wT_bf16(p, wu_d[e], 1024, 1408, "wu%d" % e, region=wu_r)
        wd = load_wT_bf16(p, wd_d[e], 1408, 1024, "wd%d" % e, region=wd_r)
        for it in range(nt):
            t0 = it * 512
            for fo in range(11):
                psA = pp[0][:, 0:512]
                psB = pp[1][:, 0:512]
                for c in range(NCH):
                    p.mm(psA, wg[c][:, fo * 128:(fo + 1) * 128],
                         xn_bf[:, c * T + t0: c * T + t0 + 512],
                         start=(c == 0), stop=(c == NCH - 1), tag="eg_mm")
                for c in range(NCH):
                    p.mm(psB, wu[c][:, fo * 128:(fo + 1) * 128],
                         xn_bf[:, c * T + t0: c * T + t0 + 512],
                         start=(c == 0), stop=(c == NCH - 1), tag="eu_mm")
                p.act(su_r[:, 0:512], psA, AF.Silu, tag="eg_silu")
                p.tt(gt_r[:, fo * 512:(fo + 1) * 512], su_r[:, 0:512], psB,
                     op=ALU.mult, tag="eg_mul")
            p.mm(cm.bcast_psum[:, 0:512], sel[:, e * 128:(e + 1) * 128],
                 comb8[0:8, t0:t0 + 512], tag="cb_mm")
            p.act(comb_sb[:, 0:512], cm.bcast_psum[:, 0:512], AF.Copy,
                  tag="cb_cp")
            for fo in range(NCH):
                ps = pp[2][:, 0:512]
                for c in range(11):
                    p.mm(ps, wd[c][:, fo * 128:(fo + 1) * 128],
                         gt_r[:, c * 512:(c + 1) * 512],
                         start=(c == 0), stop=(c == 10), tag="ed_mm")
                p.tt(cm.scratch[:, 0:512], ps, comb_sb[:, 0:512],
                     op=ALU.mult, tag="ed_msk")
                p.tt(xsb[:, fo * T + t0: fo * T + t0 + 512],
                     xsb[:, fo * T + t0: fo * T + t0 + 512],
                     cm.scratch[:, 0:512], op=ALU.add, tag="ed_acc")

    swd = load_wT_bf16(p, swd_d, 2816, 1024, "swd", region=swd_r)
    for it in range(nt):
        t0 = it * 512
        for f in range(22):
            wg_t = wjit_r[:, 0:1024]
            wu_t = wjit_r[:, 1024:2048]
            p.dma(wg_t.rearrange("p (c n) -> p c n", n=128),
                  swg_d[:, f * 128:(f + 1) * 128].rearrange(
                      "(c p) n -> p c n", p=128), q="poolq", tag="swg_ld")
            p.dma(wu_t.rearrange("p (c n) -> p c n", n=128),
                  swu_d[:, f * 128:(f + 1) * 128].rearrange(
                      "(c p) n -> p c n", p=128), q="poolq", tag="swu_ld")
            psA = pp[0][:, 0:512]
            psB = pp[1][:, 0:512]
            for c in range(NCH):
                p.mm(psA, wg_t[:, c * 128:(c + 1) * 128],
                     xn_bf[:, c * T + t0: c * T + t0 + 512],
                     start=(c == 0), stop=(c == NCH - 1), tag="sg_mm")
            for c in range(NCH):
                p.mm(psB, wu_t[:, c * 128:(c + 1) * 128],
                     xn_bf[:, c * T + t0: c * T + t0 + 512],
                     start=(c == 0), stop=(c == NCH - 1), tag="su_mm")
            p.act(su_r[:, 0:512], psA, AF.Silu, tag="sg_silu")
            p.tt(gsw_r[:, f * 512:(f + 1) * 512], su_r[:, 0:512], psB,
                 op=ALU.mult, tag="sg_mul")
        for fo in range(NCH):
            ps = pp[2][:, 0:512]
            for c in range(22):
                p.mm(ps, swd[c][:, fo * 128:(fo + 1) * 128],
                     gsw_r[:, c * 512:(c + 1) * 512],
                     start=(c == 0), stop=(c == 21), tag="sd_mm")
            p.tt(xsb[:, fo * T + t0: fo * T + t0 + 512],
                 xsb[:, fo * T + t0: fo * T + t0 + 512], ps,
                 op=ALU.add, tag="sd_acc")

    store_stream(p, xsb, T)
    return p



def build_l2f(T=TLOC, p=None):
    p = p or P("l2f")
    cm = Common(p, T, slim=True)
    xsb = load_stream(p, T)
    ident = make_identity(p, "identc", F32)

    rw_f = load_wT_f32(p, p.dram_in("router_w", [1024, 8]), 1024, 8, "rw")
    rb_d = p.dram_in("router_b", [8])
    w1_d = p.dram_in("w1", [8, 1024, 8192])
    w2_d = p.dram_in("w2", [8, 4096, 1024])

    A = Arena(p, "A", 22600)
    xn_bf = A.bf16(0, NCH * T)
    gr_r = A.bf16(8192, 8 * 2048)
    wd_r = A.bf16(16384, 8 * 1024)
    wjit_r = A.bf16(20480, 4 * 1024)

    pp = [p.psum("mmp%d" % i, [128, 512], F32) for i in range(3)]
    rlog = p.psum("rlog", [128, 32], F32)
    rlog_sb = p.sbuf("rlog_sb", [128, 32], F32)
    scores = p.sbuf("scores", [128, 8], F32)
    srtmp = p.sbuf("srtmp", [128, 8], F32)
    m8 = p.sbuf("m8", [128, 8], F32)
    mcol = p.sbuf("mcol", [128, 4], F32)
    comb_tm = p.sbuf("comb_tm", [128, 8], F32)
    comb8 = p.sbuf("comb8", [8, T], F32)
    cps = pp[1]

    rb_row = load_row_f32(p, rb_d, 8, "rb")
    rb_bc = p.sbuf("rb_bc", [128, 32], F32)
    for sub in range(4):
        p.mm(cm.bcast_psum[:, 0:8], cm.ones_row[0:1, :],
             rb_row[0:1, 0:8], tag="rb_bc")
        p.copy(rb_bc[:, sub * 8:(sub + 1) * 8], cm.bcast_psum[:, 0:8],
               eng="dve", tag="rb_cp")

    def emit_xn(c, it, sc):
        p.copy(xn_bf[:, c * T + it * 512: c * T + (it + 1) * 512], sc,
               eng="pool", tag="xnbf")
        if c == 0:
            p.copy(rlog_sb[:], rb_bc[:], eng="dve", tag="rt_z")
        for sub in range(4):
            p.mm(rlog[:, sub * 8:(sub + 1) * 8],
                 sc[:, sub * 128:(sub + 1) * 128], rw_f[c][:, 0:8],
                 start=True, stop=True, tag="rt_mm")
        p.tt(rlog_sb[:], rlog_sb[:], rlog[:, 0:32], op=ALU.add, tag="rt_acc")

    def do_router(s):
        rsl = rlog_sb[:, (s % 4) * 8:(s % 4) * 8 + 8]
        p.red(mcol[:, 0:1], rsl, op=ALU.max, eng="dve", negate=True,
              tag="rt_max")
        p.act(scores[:, 0:8], rsl, AF.Exp, bias=mcol[:, 0:1], tag="rt_exp")
        p.red(mcol[:, 1:2], scores[:, 0:8], op=ALU.add, tag="rt_sum")
        p.reciprocal(mcol[:, 2:3], mcol[:, 1:2], tag="rt_rcp")
        p.ts(scores[:, 0:8], scores[:, 0:8], mcol[:, 2:3], None,
             op0=ALU.mult, tag="rt_nrm")
        p.top8(m8[:, 0:8], scores[:, 0:8], tag="rt_top")
        p.ts(srtmp[:, 0:8], scores[:, 0:8], m8[:, 0:1], None,
             op0=ALU.is_equal, tag="rt_msk")
        p.ts(comb_tm[:, 0:8], srtmp[:, 0:8], m8[:, 0:1], None,
             op0=ALU.mult, tag="rt_cw")
        p.transpose(cps[0:8, 0:128], comb_tm[:, 0:8], ident[:], tag="rt_tr")
        p.copy(comb8[:, s * 128:(s + 1) * 128], cps[0:8, 0:128], eng="dve",
               tag="rt_cp")

    def router_it(it):
        for sub in range(4):
            do_router(it * 4 + sub)

    def x_tile(c, it):
        return xsb[:, c * T + it * 512: c * T + (it + 1) * 512]
    ln_streaming(p, cm, T, x_tile, emit_xn, "ln_", it_done=router_it)

    sel = p.sbuf("sel", [8, 8 * 128], F32)
    selp = p.sbuf("selp", [8, 1], F32)
    p.iota(selp[0:8, :], [[0, 1]], base=0, channel_multiplier=1, tag="selp")
    for e in range(8):
        p.ts(sel[0:8, e * 128:(e + 1) * 128],
             bass.AP(selp, 0, [[1, 8], [0, 128]]), float(e), None,
             op0=ALU.is_equal, tag="sel%d" % e)
    comb_sb = p.sbuf("comb_sb", [128, 512], F32)
    su_h = p.sbuf("su_t", [128, 512], BF16)
    nt = T // 512

    for e in range(8):
        for q in range(4):
            for f in range(8):
                fa = q * 8 + f
                jb = (f % 2) * 2048
                wa_t = wjit_r[:, jb:jb + 1024]
                wg_t = wjit_r[:, jb + 1024:jb + 2048]
                p.dma(wa_t.rearrange("p (c n) -> p c n", n=128),
                      w1_d[e][:, fa * 128:(fa + 1) * 128].rearrange(
                          "(c p) n -> p c n", p=128), q="poolq", tag="wa_ld")
                p.dma(wg_t.rearrange("p (c n) -> p c n", n=128),
                      w1_d[e][:, 4096 + fa * 128: 4096 + (fa + 1) * 128]
                      .rearrange("(c p) n -> p c n", p=128), q="poolq",
                      tag="wg_ld")
                for it in range(nt):
                    t0 = it * 512
                    psA = pp[0][:, 0:512]
                    psB = pp[2][:, 0:512]
                    for c in range(NCH):
                        p.mm(psA, wa_t[:, c * 128:(c + 1) * 128],
                             xn_bf[:, c * T + t0: c * T + t0 + 512],
                             start=(c == 0), stop=(c == NCH - 1), tag="e1_mm")
                    for c in range(NCH):
                        p.mm(psB, wg_t[:, c * 128:(c + 1) * 128],
                             xn_bf[:, c * T + t0: c * T + t0 + 512],
                             start=(c == 0), stop=(c == NCH - 1), tag="e2_mm")
                    p.act(su_h[:, 0:512], psA, AF.Silu, tag="e_silu")
                    p.tt(gr_r[:, (f * nt + it) * 512:(f * nt + it + 1) * 512],
                         su_h[:, 0:512], psB, op=ALU.mult, tag="e_mul")
            wd = load_wT_bf16(p, w2_d[e][q * 1024:(q + 1) * 1024, :],
                              1024, 1024, "wd%d_%d" % (e, q), region=wd_r)
            for it in range(nt):
                t0 = it * 512
                p.mm(cm.bcast_psum[:, 0:512], sel[:, e * 128:(e + 1) * 128],
                     comb8[0:8, t0:t0 + 512], tag="cb_mm")
                p.act(comb_sb[:, 0:512], cm.bcast_psum[:, 0:512], AF.Copy,
                      tag="cb_cp")
                for fo in range(NCH):
                    ps = pp[1][:, 0:512]
                    for c in range(8):
                        p.mm(ps, wd[c][:, fo * 128:(fo + 1) * 128],
                             gr_r[:, (c * nt + it) * 512:
                                  (c * nt + it + 1) * 512],
                             start=(c == 0), stop=(c == 7), tag="ed_mm")
                    p.tt(cm.scratch[:, 0:512], ps,
                         comb_sb[:, 0:512], op=ALU.mult, tag="ed_msk")
                    p.tt(xsb[:, fo * T + t0: fo * T + t0 + 512],
                         xsb[:, fo * T + t0: fo * T + t0 + 512],
                         cm.scratch[:, 0:512], op=ALU.add, tag="ed_acc")

    store_stream(p, xsb, T)
    return p



TWO_PI = 6.283185307179586
MAGIC = 12582912.0


def emit_mod2pi(p, out, ang, tmp, pref):
    p.ts(tmp, ang, 1.0 / TWO_PI, -0.5, op0=ALU.mult, op1=ALU.add,
         tag=pref + "m1")
    p.ts(tmp, tmp, MAGIC, None, op0=ALU.add, tag=pref + "m2")
    p.ts(tmp, tmp, -MAGIC, None, op0=ALU.add, tag=pref + "m3")
    p.stt(out, tmp, -TWO_PI, ang, op0=ALU.mult, op1=ALU.add, tag=pref + "m4")


def emit_trig(p, ang, Ct, St, tmp, negpi, pref):
    emit_mod2pi(p, Ct, ang, tmp, pref + "s")
    p.act(St, Ct, AF.Sin, bias=negpi, scale=1.0, tag=pref + "sin")
    p.ts(Ct, ang, np.pi / 2, None, op0=ALU.add, tag=pref + "shift")
    emit_mod2pi(p, Ct, Ct, tmp, pref + "c")
    p.act(Ct, Ct, AF.Sin, bias=negpi, scale=1.0, tag=pref + "cos")


def build_l1a_a(T=TLOC, p=None):
    p = p or P("l1a_a")
    cm = Common(p, T, slim=True)
    xs_d = p.dram_in("xs", [D, T])
    xn_d = p.dram_out("xn", [D, T])
    ure_d = p.dram_out("ure", [D, T])
    uim_d = p.dram_out("uim", [D, T])
    F_d = p.dram_out("Fc", [D, 2])
    t0c_d = p.dram_in("t0col", [128])
    nu_log = load_col_f32(p, p.dram_in("nu_log", [D]), D, "nul")
    th_log = load_col_f32(p, p.dram_in("theta_log", [D]), D, "thl")
    BreT = load_wT_bf16(p, p.dram_in("BreT", [D, D]), D, D, "bre")
    BimT = load_wT_bf16(p, p.dram_in("BimT", [D, D]), D, D, "bim")

    xn_bf = p.sbuf("xn_bf", [128, NCH * T], BF16)
    xstage = p.sbuf("xstage", [128, 1024], F32)
    xbuf = p.sbuf("xbuf", [128, 1024], F32)
    xslot = [0]

    def x_tile(c, it):
        s = xslot[0] % 2
        xslot[0] += 1
        sl = xbuf[:, s * 512:(s + 1) * 512]
        p.dma(sl, xs_d[c * 128:(c + 1) * 128, it * 512:(it + 1) * 512],
              q="spq", tag="xt")
        return sl

    def emit_xn(c, it, sc):
        p.copy(xn_bf[:, c * T + it * 512: c * T + (it + 1) * 512], sc,
               eng="pool", tag="xnbf")
        st = xstage[:, (it % 2) * 512:(it % 2) * 512 + 512]
        p.copy(st, sc, eng="dve", tag="xnst")
        p.dma(xn_d[c * 128:(c + 1) * 128, it * 512:(it + 1) * 512], st,
              q="actq", tag="xnout")

    ln_streaming(p, cm, T, x_tile, emit_xn, "ln_")

    prm = p.sbuf("prm", [128, 4 * NCH], F32)
    t0col = p.sbuf("t0col_sb", [128, 1], F32)
    p.dma(t0col[:, 0:1], t0c_d.rearrange("(a b) -> a b", b=1), q="spq",
          tag="t0")
    for hc in range(NCH):
        nu = prm[:, 4 * hc:4 * hc + 1]
        r = prm[:, 4 * hc + 1:4 * hc + 2]
        gm = prm[:, 4 * hc + 2:4 * hc + 3]
        th = prm[:, 4 * hc + 3:4 * hc + 4]
        p.act(nu, nu_log[hc], AF.Exp, tag="p_nu")
        p.act(r, nu, AF.Exp, scale=-1.0, tag="p_r")
        p.act(gm, nu, AF.Exp, scale=-2.0, tag="p_e2")
        p.ts(gm, gm, -1.0, 1.0 + 1e-5, op0=ALU.mult, op1=ALU.add, tag="p_t2")
        p.act(gm, gm, AF.Sqrt, tag="p_gm")
        p.act(th, th_log[hc], AF.Exp, tag="p_th")

    iota_bc = p.sbuf("iota_bc", [128, T], F32)
    p.iota(iota_bc[:], [[1, T]], base=0, channel_multiplier=0, tag="iota")
    negpi = p.sbuf("negpi", [128, 1], F32)
    p.memset(negpi[:], -np.pi, eng="pool")
    work = p.sbuf("work", [128, 8 * T], F32)
    ang = work[:, 0:T]
    tmp = work[:, T:2 * T]
    Ct = work[:, 2 * T:3 * T]
    St = work[:, 3 * T:4 * T]
    bur = work[:, 4 * T:5 * T]
    bui = work[:, 5 * T:6 * T]
    ur = work[:, 6 * T:7 * T]
    ui = work[:, 7 * T:8 * T]
    tmp2 = cm.scratch[:, 0:512]
    pp = [p.psum("mmp%d" % i, [128, 512], F32) for i in range(2)]
    fstage = p.sbuf("fstage", [128, 2], F32)

    for hc in range(NCH):
        nu = prm[:, 4 * hc:4 * hc + 1]
        r = prm[:, 4 * hc + 1:4 * hc + 2]
        gm = prm[:, 4 * hc + 2:4 * hc + 3]
        th = prm[:, 4 * hc + 3:4 * hc + 4]
        for it in range(T // 512):
            t0 = it * 512
            for (wt, dst) in ((BreT, bur), (BimT, bui)):
                ps = pp[0][:, 0:512] if wt is BreT else pp[1][:, 0:512]
                for c in range(NCH):
                    p.mm(ps, wt[c][:, hc * 128:(hc + 1) * 128],
                         xn_bf[:, c * T + t0:c * T + t0 + 512],
                         start=(c == 0), stop=(c == NCH - 1), tag="bu_mm")
                p.ts(dst[:, t0:t0 + 512], ps, gm, None, op0=ALU.mult,
                     tag="bu_ev")
        p.ts(ang, iota_bc[:], t0col[:, 0:1], th, op0=ALU.add, op1=ALU.mult,
             tag="ang")
        emit_trig(p, ang, Ct, St, tmp, negpi[:, 0:1], "tr%d" % hc)
        p.tt(ur, bur, Ct, op=ALU.mult, tag="d1")
        p.tt(tmp, bui, St, op=ALU.mult, tag="d2")
        p.tt(ur, ur, tmp, op=ALU.add, tag="d3")
        p.tt(ui, bui, Ct, op=ALU.mult, tag="d4")
        p.tt(tmp, bur, St, op=ALU.mult, tag="d5")
        p.tt(ui, ui, tmp, op=ALU.subtract, tag="d6")
        p.scan(ur, r.broadcast_to([128, T]), ur, 0.0, tag="scan_re")
        p.scan(ui, r.broadcast_to([128, T]), ui, 0.0, tag="scan_im")
        p.copy(fstage[:, 0:1], ur[:, T - 1:T], eng="pool", tag="fst")
        p.copy(fstage[:, 1:2], ui[:, T - 1:T], eng="pool", tag="fst2")
        p.dma(F_d[hc * 128:(hc + 1) * 128, :], fstage[:, 0:2], q="actq",
              tag="fout")
        p.dma(ure_d[hc * 128:(hc + 1) * 128, :], ur, q="spq", tag="uout")
        p.dma(uim_d[hc * 128:(hc + 1) * 128, :], ui, q="actq", tag="uout2")
    return p


def build_l1a_b(T=TLOC, p=None):
    p = p or P("l1a_b")
    xs_d = p.dram_in("xs", [D, T])
    xn_d = p.dram_in("xn", [D, T])
    ure_d = p.dram_in("ure", [D, T])
    uim_d = p.dram_in("uim", [D, T])
    Fall_d = p.dram_in("Fall", [D, 8])
    a4_d = p.dram_in("a4", [4])
    t0c_d = p.dram_in("t0col", [128])
    ys_d = p.dram_out("ys", [D, T])
    nu_log = load_col_f32(p, p.dram_in("nu_log", [D]), D, "nul")
    th_log = load_col_f32(p, p.dram_in("theta_log", [D]), D, "thl")
    D_skip = load_col_f32(p, p.dram_in("D_skip", [D]), D, "dsk")
    CreT = load_wT_bf16(p, p.dram_in("CreT", [D, D]), D, D, "cre")
    CimT = load_wT_bf16(p, p.dram_in("CimT", [D, D]), D, D, "cim")

    ones_row = p.sbuf("ones_row", [1, 128], F32)
    p.memset(ones_row[:], 1.0, eng="pool")
    a4r = p.sbuf("a4r", [1, 4], F32)
    p.dma(a4r[0:1, :], a4_d.rearrange("(a b) -> a b", a=1), q="spq", tag="a4")
    acol = p.sbuf("acol", [128, 4], F32)
    bps = p.psum("bps", [128, 8], F32)
    p.mm(bps[:, 0:4], ones_row[0:1, :], a4r[0:1, 0:4], tag="a4bc")
    p.copy(acol[:, 0:4], bps[:, 0:4], eng="dve", tag="a4cp")
    t0col = p.sbuf("t0col_sb", [128, 1], F32)
    p.dma(t0col[:, 0:1], t0c_d.rearrange("(a b) -> a b", b=1), q="spq",
          tag="t0")

    iota_bc = p.sbuf("iota_bc", [128, T], F32)
    p.iota(iota_bc[:], [[1, T]], base=0, channel_multiplier=0, tag="iota")
    negpi = p.sbuf("negpi", [128, 1], F32)
    p.memset(negpi[:], -np.pi, eng="pool")

    hre_bf = p.sbuf("hre_bf", [128, NCH * T], BF16)
    himn_bf = p.sbuf("himn_bf", [128, NCH * T], BF16)
    work = p.sbuf("work", [128, 7 * T], F32)
    ang = work[:, 0:T]
    tmp = work[:, T:2 * T]
    Ct = work[:, 2 * T:3 * T]
    St = work[:, 3 * T:4 * T]
    ur = work[:, 4 * T:5 * T]
    ui = work[:, 5 * T:6 * T]
    tmp3 = work[:, 6 * T:7 * T]
    prm = p.sbuf("prm", [128, 2 * NCH], F32)
    Fh = p.sbuf("Fh", [128, 8], F32)
    Ucol = p.sbuf("Ucol", [128, 4], F32)

    for hc in range(NCH):
        nu = prm[:, 2 * hc:2 * hc + 1]
        th = prm[:, 2 * hc + 1:2 * hc + 2]
        p.act(nu, nu_log[hc], AF.Exp, tag="p_nu")
        p.act(th, th_log[hc], AF.Exp, tag="p_th")
        p.dma(ur, ure_d[hc * 128:(hc + 1) * 128, :], q="spq", tag="uld")
        p.dma(ui, uim_d[hc * 128:(hc + 1) * 128, :], q="spq", tag="uld2")
        p.dma(Fh[:, 0:8], Fall_d[hc * 128:(hc + 1) * 128, :], q="spq",
              tag="fld")
        p.memset(Ucol[:, 0:2], 0.0, eng="dve", tag="uz")
        for q in range(4):
            p.tt(Ucol[:, 2:3], nu, acol[:, q:q + 1], op=ALU.mult, tag="ualp")
            p.act(Ucol[:, 2:3], Ucol[:, 2:3], AF.Exp, scale=-1.0, tag="uexp")
            p.stt(Ucol[:, 0:1], Fh[:, 2 * q:2 * q + 1], Ucol[:, 2:3],
                  Ucol[:, 0:1], op0=ALU.mult, op1=ALU.add, tag="uacc")
            p.stt(Ucol[:, 1:2], Fh[:, 2 * q + 1:2 * q + 2], Ucol[:, 2:3],
                  Ucol[:, 1:2], op0=ALU.mult, op1=ALU.add, tag="uacc2")
        p.ts(tmp, iota_bc[:], 1.0, nu, op0=ALU.add, op1=ALU.mult, tag="fx1")
        p.act(tmp, tmp, AF.Exp, scale=-1.0, tag="fx2")
        p.stt(tmp3, tmp, Ucol[:, 0:1], ur, op0=ALU.mult, op1=ALU.add,
              tag="fx3")
        p.copy(ur, tmp3, eng="pool", tag="fx3c")
        p.stt(tmp3, tmp, Ucol[:, 1:2], ui, op0=ALU.mult, op1=ALU.add,
              tag="fx4")
        p.copy(ui, tmp3, eng="pool", tag="fx4c")
        p.ts(ang, iota_bc[:], t0col[:, 0:1], th, op0=ALU.add, op1=ALU.mult,
             tag="ang")
        emit_trig(p, ang, Ct, St, tmp, negpi[:, 0:1], "tr%d" % hc)
        p.tt(tmp3, ur, Ct, op=ALU.mult, tag="r1")
        p.tt(tmp, ui, St, op=ALU.mult, tag="r2")
        p.tt(hre_bf[:, hc * T:(hc + 1) * T], tmp3, tmp, op=ALU.subtract,
             tag="r3")
        p.tt(tmp3, ui, Ct, op=ALU.mult, tag="r4")
        p.tt(tmp, ur, St, op=ALU.mult, tag="r5")
        p.tt(tmp3, tmp3, tmp, op=ALU.add, tag="r6")
        p.ts(himn_bf[:, hc * T:(hc + 1) * T], tmp3, -1.0, None, op0=ALU.mult,
             tag="r7")

    pp = [p.psum("pm%d" % i, [128, 512], F32) for i in range(3)]
    xnb = p.sbuf("xnb", [128, 1024], F32)
    xb = p.sbuf("xb", [128, 1024], F32)
    rb = p.sbuf("rb", [128, 1024], F32)
    for fo in range(NCH):
        for it in range(T // 512):
            t0 = it * 512
            ps = pp[it % 2][:, 0:512]
            for c in range(NCH):
                p.mm(ps, CreT[c][:, fo * 128:(fo + 1) * 128],
                     hre_bf[:, c * T + t0:c * T + t0 + 512],
                     start=(c == 0), stop=False, tag="pr_mm")
            for c in range(NCH):
                p.mm(ps, CimT[c][:, fo * 128:(fo + 1) * 128],
                     himn_bf[:, c * T + t0:c * T + t0 + 512],
                     start=False, stop=(c == NCH - 1), tag="pi_mm")
            xnt = xnb[:, (it % 2) * 512:(it % 2) * 512 + 512]
            xt = xb[:, (it % 2) * 512:(it % 2) * 512 + 512]
            rt = rb[:, (it % 2) * 512:(it % 2) * 512 + 512]
            p.dma(xnt, xn_d[fo * 128:(fo + 1) * 128, t0:t0 + 512], q="spq",
                  tag="xnld")
            p.dma(xt, xs_d[fo * 128:(fo + 1) * 128, t0:t0 + 512], q="spq",
                  tag="xld")
            p.stt(rt, xnt, D_skip[fo], ps, op0=ALU.mult, op1=ALU.add,
                  tag="ev1")
            p.tt(rt, rt, xt, op=ALU.add, tag="ev2")
            p.dma(ys_d[fo * 128:(fo + 1) * 128, t0:t0 + 512], rt, q="actq",
                  tag="yout")
    return p



def build_l2a_pre(T=TLOC, p=None, fused=False):
    p = p or P("l2a_pre")
    cm = Common(p, T, slim=True)
    xs_d = p.dram_in("xs", [D, T])
    t0c_d = p.dram_in("t0col", [128])
    kidx_d = p.dram_in("kidx", [128])
    alt_d = p.dram_in("alt", [128])
    pswap_d = p.dram_in("pswap", [128, 128])
    wqkvT = load_wT_bf16(p, p.dram_in("wqkvT", [D, 3 * D]), D, 3 * D, "wqkv")
    bqkv = load_col_f32(p, p.dram_in("bqkv", [3 * D]), 3 * D, "bqkv")
    qf_d = p.dram_out("qf", [D, T], BF16)
    kf_d = p.dram_out("kf", [D, T], BF16)
    vf_d = p.dram_out("vf", [D, T], BF16)
    vaug_d = p.dram_out("vaug", [T, 16 * 65], BF16)
    qn_d = p.dram_out("qn", [16, T])
    kn_d = p.dram_out("kn", [16, T])
    if fused:
        vedge_d = p.dram_out("vedge", [2, D], BF16)

    xn_bf = p.sbuf("xn_bf", [128, NCH * T], BF16)
    xbuf = p.sbuf("xbuf", [128, 1024], F32)
    xslot = [0]

    def x_tile(c, it):
        s = xslot[0] % 2
        xslot[0] += 1
        sl = xbuf[:, s * 512:(s + 1) * 512]
        p.dma(sl, xs_d[c * 128:(c + 1) * 128, it * 512:(it + 1) * 512],
              q="spq", tag="xt")
        return sl

    def emit_xn(c, it, sc):
        p.copy(xn_bf[:, c * T + it * 512: c * T + (it + 1) * 512], sc,
               eng="pool", tag="xnbf")
    ln_streaming(p, cm, T, x_tile, emit_xn, "ln_")

    t0col = p.sbuf("t0col_sb", [128, 1], F32)
    p.dma(t0col[:, 0:1], t0c_d.rearrange("(a b) -> a b", b=1), q="spq", tag="t0")
    kidx = p.sbuf("kidx_sb", [128, 1], F32)
    p.dma(kidx[:, 0:1], kidx_d.rearrange("(a b) -> a b", b=1), q="spq", tag="ki")
    alt = p.sbuf("alt_sb", [128, 1], F32)
    p.dma(alt[:, 0:1], alt_d.rearrange("(a b) -> a b", b=1), q="spq", tag="al")
    pswap = p.sbuf("pswap_sb", [128, 128], BF16)
    p.dma(pswap[:], pswap_d, q="poolq", tag="ps")
    negpi = p.sbuf("negpi", [128, 1], F32)
    p.memset(negpi[:], -np.pi, eng="pool")
    iota_bc = p.sbuf("iota_bc", [128, T], F32)
    p.iota(iota_bc[:], [[1, T]], base=0, channel_multiplier=0, tag="iota")

    tb = p.sbuf("tb", [128, 4 * T], F32)
    ang = tb[:, 0:T]
    tmp = tb[:, T:2 * T]
    Ct = tb[:, 2 * T:3 * T]
    St = tb[:, 3 * T:4 * T]
    cosr = tb[:, 0:T]
    sspk = tb[:, T:2 * T]
    divc = p.sbuf("divc", [128, 1], F32)
    p.act(divc[:, 0:1], kidx[:, 0:1], AF.Exp,
          scale=-2.0 * math.log(10000.0) / 64.0, tag="div")
    p.ts(ang, iota_bc[:], t0col[:, 0:1], divc[:, 0:1], op0=ALU.add,
         op1=ALU.mult, tag="ang")
    emit_trig(p, ang, Ct, St, tmp, negpi[:, 0:1], "rt")
    p.ts(cosr, Ct, -1.0, None, op0=ALU.mult, tag="cosr")
    p.ts(sspk, St, -1.0, None, op0=ALU.mult, tag="s1")
    p.tt(sspk, sspk, alt[:, 0:1].broadcast_to([128, T]), op=ALU.mult, tag="s2")

    pp = [p.psum("mmp%d" % i, [128, 512], F32) for i in range(2)]
    ptr_bf = [p.psum("ptr%d" % i, [128, 64], BF16) for i in range(2)]
    identb2 = make_identity(p, "identb2", BF16)
    ew = p.sbuf("ew", [128, 4 * 512], F32)
    nrmst = p.sbuf("nrmst", [2, 1024], F32)
    bd2 = p.sbuf("bd2", [128, 2], F32)
    p.memset(bd2[:, 0:2], 0.0, eng="pool")
    p.memset(bd2[0:64, 0:1], 1.0, eng="pool")
    p.memset(bd2[64:128, 1:2], 1.0, eng="pool")
    vaug_sb = p.sbuf("vaug_sb", [128, 16 * 1040], BF16)
    p.memset(vaug_sb[:], 0.0, eng="pool")
    onecol = p.sbuf("onecol", [128, 1], BF16)
    p.memset(onecol[:], 1.0, eng="pool")
    for tc in range(16):
        for h in range(16):
            p.copy(vaug_sb[:, tc * 1040 + 65 * h + 64: tc * 1040 + 65 * h + 65],
                   onecol[:, 0:1], eng="pool", tag="vone")
    qkv_bf = p.sbuf("qkv_bf", [128, 512], BF16)
    nps = cm.psum_stats[0:2, 0:512]

    for part in range(3):
        for c in range(NCH):
            fo = part * NCH + c
            for it in range(T // 512):
                t0 = it * 512
                ps = pp[it % 2][:, 0:512]
                for ci in range(NCH):
                    p.mm(ps, wqkvT[ci][:, fo * 128:(fo + 1) * 128],
                         xn_bf[:, ci * T + t0: ci * T + t0 + 512],
                         start=(ci == 0), stop=(ci == NCH - 1), tag="qkv_mm")
                if part == 2:
                    p.ts(qkv_bf[:, 0:512], ps, bqkv[fo], None, op0=ALU.add,
                         tag="v_ev")
                    p.dma(vf_d[c * 128:(c + 1) * 128, t0:t0 + 512],
                          qkv_bf[:, 0:512], q="actq", tag="v_out")
                    if fused:
                        if it == 0:
                            p.dma(vedge_d[0:1, c * 128:(c + 1) * 128],
                                  qkv_bf[:, 0:1], q="actq", tag="vedge0")
                        if it == T // 512 - 1:
                            p.dma(vedge_d[1:2, c * 128:(c + 1) * 128],
                                  qkv_bf[:, 511:512], q="actq", tag="vedge1")
                    for tsb in range(4):
                        for hh in range(2):
                            h = 2 * c + hh
                            tc = it * 4 + tsb
                            tp = ptr_bf[(tsb * 2 + hh) % 2]
                            p.transpose(tp[0:128, 0:64],
                                        qkv_bf[hh * 64:(hh + 1) * 64,
                                               tsb * 128:(tsb + 1) * 128],
                                        identb2[hh * 64:(hh + 1) * 64,
                                                hh * 64:(hh + 1) * 64],
                                        tag="v_tr")
                            p.copy(vaug_sb[:, tc * 1040 + 65 * h:
                                           tc * 1040 + 65 * h + 64],
                                   tp[0:128, 0:64], eng="dve", tag="v_cp")
                else:
                    el = ew[:, 0:512]
                    t1 = ew[:, 512:1024]
                    p.ts(el, ps, bqkv[fo], 0.0, op0=ALU.add, op1=ALU.min,
                         tag="e_min")
                    p.act(el, el, AF.Exp, tag="e_exp")
                    p.ts(t1, ps, bqkv[fo], 0.0, op0=ALU.add, op1=ALU.max,
                         tag="e_max")
                    p.tt(el, el, t1, op=ALU.add, tag="e_add")
                    p.tt(t1, el, el, op=ALU.mult, eng="pool", tag="n_sq")
                    p.mm(nps[0:2, 0:512], bd2[:, 0:2], t1, tag="n_mm")
                    scl = 1.0 / 64.0 if part == 0 else 1.0
                    nst = nrmst[0:2, (it % 2) * 512:(it % 2) * 512 + 512]
                    p.act(nst, nps[0:2, 0:512], AF.Sqrt, scale=scl, tag="n_rt")
                    nd = qn_d if part == 0 else kn_d
                    p.dma(nd[2 * c:2 * c + 2, t0:t0 + 512], nst, q="spq",
                          tag="n_out")
                    elb = qkv_bf[:, 0:512]
                    p.copy(elb, el, eng="pool", tag="r_bf")
                    psh = pp[(it + 1) % 2][:, 0:512]
                    p.mm(psh, pswap[:], elb, tag="r_sw")
                    t2 = ew[:, 1024:1536]
                    t3 = ew[:, 1536:2048]
                    p.tt(t2, el, cosr[:, t0:t0 + 512], op=ALU.mult, tag="r_c")
                    p.tt(t3, psh, sspk[:, t0:t0 + 512], op=ALU.mult, tag="r_s")
                    sc2 = 0.125 if part == 0 else 1.0
                    p.tt(t2, t2, t3, op=ALU.add, tag="r_a")
                    dst = qf_d if part == 0 else kf_d
                    p.ts(qkv_bf[:, 0:512], t2, sc2, None, op0=ALU.mult,
                         tag="r_o")
                    p.dma(dst[c * 128:(c + 1) * 128, t0:t0 + 512],
                          qkv_bf[:, 0:512], q="actq", tag="r_out")

    for tc in range(16):
        p.dma(vaug_d[tc * 128:(tc + 1) * 128, :],
              vaug_sb[:, tc * 1040:(tc + 1) * 1040], q="actq", tag="va_out")
    return p


def build_l2a_flash(T=TLOC, p=None):
    p = p or P("l2a_flash")
    xs_d = p.dram_in("xs", [D, T])
    qf_d = p.dram_in("qf", [D, T], BF16)
    Kg_d = p.dram_in("Kg", [4 * D, T], BF16)
    VA_d = p.dram_in("VA", [4 * T, 16 * 65], BF16)
    qn_d = p.dram_in("qn", [16, T])
    KN_d = p.dram_in("KN", [16, 4 * T])
    vf_d = p.dram_in("vf", [D, T], BF16)
    vedl_d = p.dram_in("vedl", [D], BF16)
    vedr_d = p.dram_in("vedr", [D], BF16)
    wlep_d = p.dram_in("wlep", [D, 3])
    blep_d = p.dram_in("blep", [D])
    ys_d = p.dram_out("ys", [D, T])

    SEQT = 4 * T
    kaug = p.sbuf("kaug", [65, SEQT], BF16)
    p.memset(kaug[64:65, :], 1.0, eng="pool")
    qaug = p.sbuf("qaug", [65, T], BF16)
    p.memset(qaug[64:65, :], 0.0, eng="pool")
    qf_sb = p.sbuf("qf_sb", [128, NCH * T], BF16)
    for c in range(NCH):
        p.dma(qf_sb[:, c * T:(c + 1) * T],
              qf_d[c * 128:(c + 1) * 128, :], q="spq", tag="qld")
    qn_sb = p.sbuf("qn_sb", [16, T], F32)
    p.dma(qn_sb[0:16, :], qn_d, q="spq", tag="qnld")
    knt = p.sbuf("knt", [16, T], F32)
    kmax = p.sbuf("kmax", [16, 8], F32)
    for s in range(4):
        p.dma(knt[0:16, :], KN_d[:, s * T:(s + 1) * T], q="spq", tag="knld")
        p.red(kmax[0:16, s:s + 1], knt[0:16, :], op=ALU.max, tag="knmax")
    p.red(kmax[0:16, 4:5], kmax[0:16, 0:4], op=ALU.max, tag="knmax2")
    mneg = p.sbuf("mneg", [16, T], F32)
    p.ts(mneg[0:16, :], qn_sb[0:16, :], kmax[0:16, 4:5], -1.0,
         op0=ALU.mult, op1=ALU.mult, tag="mneg")

    vaug_h = p.sbuf("vaug_h", [128, 64 * 65], BF16)
    PT = p.sbuf("PT", [128, 2 * 512], BF16)
    lrow = p.sbuf("lrow", [1, 512], F32)
    lbc = p.sbuf("lbc", [64, 512], F32)
    ones_bf = p.sbuf("ones_bf", [1, 64], F32)
    p.memset(ones_bf[:], 1.0, eng="pool")
    attn = p.sbuf("attn", [128, NCH * T], BF16)
    psS = [p.psum("psS%d" % i, [128, 512], F32) for i in range(2)]
    psO = [p.psum("psO%d" % i, [65, 512], F32) for i in range(2)]
    psB = p.psum("psB", [64, 512], F32)

    for h in range(16):
        for s in range(4):
            p.dma(kaug[0:64, s * T:(s + 1) * T],
                  Kg_d[s * D + h * 64: s * D + (h + 1) * 64, :], q="spq",
                  tag="kld")
        p.dma(vaug_h.rearrange("p (j n) -> p j n", n=65),
              VA_d[:, 65 * h:65 * h + 65].rearrange("(j p) n -> p j n", p=128),
              q="poolq", tag="vld")
        p.copy(qaug[0:64, :],
               qf_sb[(h % 2) * 64:(h % 2) * 64 + 64,
                     (h // 2) * T:(h // 2 + 1) * T], eng="dve", tag="qcp")
        p.dma(qaug[64:65, :], mneg[h:h + 1, :], q="poolq", tag="qmn")
        for ib in range(4):
            i0 = ib * 512
            po = psO[ib % 2]
            for jt in range(SEQT // 128):
                ps = psS[jt % 2][:, 0:512]
                p.mm(ps, kaug[:, jt * 128:(jt + 1) * 128],
                     qaug[:, i0:i0 + 512], tag="s_mm")
                pt = PT[:, (jt % 2) * 512:(jt % 2) * 512 + 512]
                p.act(pt, ps, AF.Exp, tag="s_exp")
                p.mm(po[0:65, 0:512], vaug_h[:, jt * 65:(jt + 1) * 65], pt,
                     start=(jt == 0), stop=(jt == SEQT // 128 - 1),
                     tag="av_mm")
            p.reciprocal(lrow[0:1, 0:512], po[64:65, 0:512], tag="linv")
            p.mm(psB[0:64, 0:512], ones_bf[0:1, 0:64], lrow[0:1, 0:512],
                 tag="lbc_mm")
            p.copy(lbc[0:64, 0:512], psB[0:64, 0:512], eng="act", tag="lbc_cp")
            p.tt(attn[(h % 2) * 64:(h % 2) * 64 + 64,
                      (h // 2) * T + i0:(h // 2) * T + i0 + 512],
                 po[0:64, 0:512], lbc[0:64, 0:512], op=ALU.mult, tag="o_ev")

    wl = p.sbuf("wl", [128, 3 * NCH], F32)
    for c in range(NCH):
        p.dma(wl[:, 3 * c:3 * c + 3], wlep_d[c * 128:(c + 1) * 128, :],
              q="spq", tag="wlld")
    bl = load_col_f32(p, blep_d, D, "blep")
    vload = p.sbuf("vload", [128, T + 2], BF16)
    lw = p.sbuf("lw", [128, 2 * 512], F32)
    xb = p.sbuf("xb", [128, 2 * 512], F32)
    for c in range(NCH):
        p.dma(vload[:, 1:T + 1], vf_d[c * 128:(c + 1) * 128, :], q="spq",
              tag="vfld")
        p.dma(vload[:, 0:1],
              vedl_d[c * 128:(c + 1) * 128].rearrange("(a b) -> a b", b=1),
              q="spq", tag="vel")
        p.dma(vload[:, T + 1:T + 2],
              vedr_d[c * 128:(c + 1) * 128].rearrange("(a b) -> a b", b=1),
              q="spq", tag="ver")
        for it in range(T // 512):
            t0 = it * 512
            l1 = lw[:, (it % 2) * 512:(it % 2) * 512 + 512]
            p.ts(l1, vload[:, t0:t0 + 512], wl[:, 3 * c:3 * c + 1], None,
                 op0=ALU.mult, tag="lp0")
            p.stt(l1, vload[:, t0 + 1:t0 + 513], wl[:, 3 * c + 1:3 * c + 2],
                  l1, op0=ALU.mult, op1=ALU.add, tag="lp1")
            p.stt(l1, vload[:, t0 + 2:t0 + 514], wl[:, 3 * c + 2:3 * c + 3],
                  l1, op0=ALU.mult, op1=ALU.add, tag="lp2")
            p.tt(l1, l1, attn[:, c * T + t0:c * T + t0 + 512], op=ALU.add,
                 tag="lp3")
            p.ts(l1, l1, bl[c], None, op0=ALU.add, tag="lp4")
            xt = xb[:, (it % 2) * 512:(it % 2) * 512 + 512]
            p.dma(xt, xs_d[c * 128:(c + 1) * 128, t0:t0 + 512], q="spq",
                  tag="xld")
            p.tt(l1, l1, xt, op=ALU.add, tag="lp5")
            p.dma(ys_d[c * 128:(c + 1) * 128, t0:t0 + 512], l1, q="actq",
                  tag="yout")
    return p



def build_l0a_pre(T=TLOC, p=None):
    p = p or P("l0a_pre")
    cm = Common(p, T, slim=True)
    xs_d = p.dram_in("xs", [D, T])
    u_d = p.dram_out("u", [D, T])
    qf_d = p.dram_out("qf", [D, T], BF16)
    kf_d = p.dram_out("kf", [D, T], BF16)
    vaug_d = p.dram_out("vaug", [T, 528], BF16)
    sqv_d = p.dram_out("sqv", [16, T])
    qn_d = p.dram_out("qn", [16, T])
    kn_d = p.dram_out("kn", [16, T])

    wqT = load_wT_bf16(p, p.dram_in("wqT", [D, D]), D, D, "wq")
    wkT = load_wT_bf16(p, p.dram_in("wkT", [D, D]), D, D, "wk")
    wvT = load_wT_bf16(p, p.dram_in("wvT", [D, 512]), D, 512, "wv")
    wqvT = load_wT_bf16(p, p.dram_in("wqvT", [D, 512]), D, 512, "wqv")
    bq = load_col_f32(p, p.dram_in("bq", [D]), D, "bq")
    bk = load_col_f32(p, p.dram_in("bk", [D]), D, "bk")
    bv = load_col_f32(p, p.dram_in("bv", [512]), 512, "bv")
    bqv = load_col_f32(p, p.dram_in("bqv", [512]), 512, "bqv")
    ws_d = p.dram_in("ws", [1, 96])
    bs_d = p.dram_in("bs", [1])
    g1 = load_col_f32(p, p.dram_in("g1", [D]), D, "g1")
    b1 = load_col_f32(p, p.dram_in("b1", [D]), D, "b1")

    identb = make_identity(p, "identb", BF16)
    identf = make_identity(p, "identf", F32)

    ustage = p.sbuf("ustage", [128, 1024], F32)
    xbuf = p.sbuf("xbuf", [128, 1024], F32)
    xslot = [0]

    def x_tile(c, it):
        s = xslot[0] % 2
        xslot[0] += 1
        sl = xbuf[:, s * 512:(s + 1) * 512]
        p.dma(sl, xs_d[c * 128:(c + 1) * 128, it * 512:(it + 1) * 512],
              q="spq", tag="xt")
        return sl

    def emit_u(c, it, sc):
        st = ustage[:, (it % 2) * 512:(it % 2) * 512 + 512]
        p.copy(st, sc, eng="pool", tag="ucp")
        p.dma(u_d[c * 128:(c + 1) * 128, it * 512:(it + 1) * 512], st,
              q="actq", tag="uout")
    ln_streaming(p, cm, T, x_tile, emit_u, "ln1_")

    xn_bf = p.sbuf("xn_bf", [128, NCH * T], BF16)

    def u_tile(c, it):
        s = xslot[0] % 2
        xslot[0] += 1
        sl = xbuf[:, s * 512:(s + 1) * 512]
        p.dma(sl, u_d[c * 128:(c + 1) * 128, it * 512:(it + 1) * 512],
              q="spq", tag="ut")
        return sl

    def emit_xn(c, it, sc):
        p.stt(xn_bf[:, c * T + it * 512: c * T + (it + 1) * 512], sc, g1[c],
              b1[c].broadcast_to([128, 512]), op0=ALU.mult, op1=ALU.add,
              tag="xnaff")
    ln_streaming(p, cm, T, u_tile, emit_xn, "ln2_")

    pp = [p.psum("mmp%d" % i, [128, 512], F32) for i in range(2)]
    nps = cm.psum_stats[0:2, 0:512]
    nrmst = p.sbuf("nrmst", [2, 1024], F32)
    bd2 = p.sbuf("bd2", [128, 2], F32)
    p.memset(bd2[:, 0:2], 0.0, eng="pool")
    p.memset(bd2[0:64, 0:1], 1.0, eng="pool")
    p.memset(bd2[64:128, 1:2], 1.0, eng="pool")
    sq_t = p.sbuf("sq_t", [128, 512], F32)
    qkv_bf = p.sbuf("qkv_bf", [128, 512], BF16)

    for part in range(2):
        wt, bias, dst, nd, scl = ((wqT, bq, qf_d, qn_d, 0.125) if part == 0
                                  else (wkT, bk, kf_d, kn_d, 1.0))
        for c in range(NCH):
            for it in range(T // 512):
                t0 = it * 512
                ps = pp[it % 2][:, 0:512]
                for ci in range(NCH):
                    p.mm(ps, wt[ci][:, c * 128:(c + 1) * 128],
                         xn_bf[:, ci * T + t0: ci * T + t0 + 512],
                         start=(ci == 0), stop=(ci == NCH - 1), tag="p_mm")
                p.ts(qkv_bf[:, 0:512], ps, bias[c], scl, op0=ALU.add,
                     op1=ALU.mult, tag="p_ev")
                p.dma(dst[c * 128:(c + 1) * 128, t0:t0 + 512],
                      qkv_bf[:, 0:512], q="actq", tag="p_out")
                p.tt(sq_t[:, 0:512], qkv_bf[:, 0:512], qkv_bf[:, 0:512],
                 op=ALU.mult, eng="pool", tag="p_sq")
                p.mm(nps[0:2, 0:512], bd2[:, 0:2], sq_t[:, 0:512], tag="p_nm")
                nst = nrmst[0:2, (it % 2) * 512:(it % 2) * 512 + 512]
                p.act(nst, nps[0:2, 0:512], AF.Sqrt, tag="p_rt")
                p.dma(nd[2 * c:2 * c + 2, t0:t0 + 512], nst, q="spq",
                      tag="p_nout")

    ws_row = p.sbuf("ws_row", [1, 96], F32)
    p.dma(ws_row[0:1, :], ws_d, q="spq", tag="wsld")
    bs_row = load_row_f32(p, bs_d, 1, "bs")
    ws_col = p.sbuf("ws_col", [128, 3], F32)
    wpsA = cm.bcast_psum[0:64, 0:1]
    p.transpose(wpsA, ws_row[0:1, 32:96], identf[0:1, 0:1], tag="wstrA")
    p.copy(ws_col[0:64, 0:1], wpsA, eng="dve", tag="wscpA")
    wpsB = cm.bcast_psum[0:32, 1:2]
    p.transpose(wpsB, ws_row[0:1, 0:32], identf[0:1, 0:1], tag="wstrB")
    p.ts(ws_col[0:32, 1:2], wpsB, 1.0 / math.sqrt(32.0), None,
         op0=ALU.mult, tag="wssc")
    wsq_bd = p.sbuf("wsq_bd", [128, 4 * 16], BF16)
    wso_bd = p.sbuf("wso_bd", [128, 4 * 8], BF16)
    p.memset(wsq_bd[:], 0.0, eng="pool")
    p.memset(wso_bd[:], 0.0, eng="pool")
    for ch in range(4):
        for hh in range(4):
            h = 4 * ch + hh
            p.copy(wsq_bd[hh * 32:(hh + 1) * 32, ch * 16 + h - 4 * ch + 0:
                          ch * 16 + hh + 1] if False else
                   wsq_bd[hh * 32:(hh + 1) * 32, ch * 16 + hh:ch * 16 + hh + 1],
                   ws_col[0:32, 1:2], eng="pool", tag="bdq")
    for ch in range(4):
        for rr in range(2):
            p.copy(wso_bd[rr * 64:(rr + 1) * 64,
                          ch * 8 + 2 * ch + rr:ch * 8 + 2 * ch + rr + 1]
                   if False else
                   wso_bd[rr * 64:(rr + 1) * 64, ch * 8 + rr:ch * 8 + rr + 1],
                   ws_col[0:64, 0:1], eng="pool", tag="bdo")
    p.memset(wsq_bd[:], 0.0, eng="pool")
    p.memset(wso_bd[:], 0.0, eng="pool")
    for ch in range(4):
        for hh in range(4):
            h = 4 * ch + hh
            p.copy(wsq_bd[hh * 32:(hh + 1) * 32, ch * 16 + h:ch * 16 + h + 1],
                   ws_col[0:32, 1:2], eng="pool", tag="bdq2")
    for ch in range(4):
        for rr in range(2):
            r = 2 * ch + rr
            p.copy(wso_bd[rr * 64:(rr + 1) * 64, ch * 8 + r:ch * 8 + r + 1],
                   ws_col[0:64, 0:1], eng="pool", tag="bdo2")
    ones_r16 = p.sbuf("ones_r16", [1, 16], F32)
    p.memset(ones_r16[:], 1.0, eng="pool")
    bsps = cm.bcast_psum[0:16, 512:513]
    p.mm(bsps, ones_r16[0:1, 0:16], bs_row[0:1, 0:1], tag="bsbc")
    bs_col = p.sbuf("bs_col", [16, 1], F32)
    p.copy(bs_col[0:16, 0:1], bsps, eng="dve", tag="bscp")

    vaug_sb = p.sbuf("vaug_sb", [128, 16 * 528], BF16)
    p.memset(vaug_sb[:], 0.0, eng="pool")
    onecol = p.sbuf("onecol", [128, 1], BF16)
    p.memset(onecol[:], 1.0, eng="pool")
    for tc in range(16):
        p.copy(vaug_sb[:, tc * 528 + 520:tc * 528 + 521], onecol[:, 0:1],
               eng="pool", tag="vone")
    vf_sb = p.sbuf("vf_sb", [128, 4 * T], BF16)
    qvf = p.sbuf("qvf", [128, 512], BF16)
    sqv_st = p.sbuf("sqv_st", [16, 1024], F32)
    aux_ps = cm.bcast_psum[0:16, 0:512]
    vws_sb = p.sbuf("vws_sb", [8, 512], BF16)
    ptr_bf = [p.psum("ptr%d" % i, [128, 128], BF16) for i in range(2)]

    for c in range(4):
        for it in range(T // 512):
            t0 = it * 512
            ps = pp[it % 2][:, 0:512]
            for ci in range(NCH):
                p.mm(ps, wvT[ci][:, c * 128:(c + 1) * 128],
                     xn_bf[:, ci * T + t0: ci * T + t0 + 512],
                     start=(ci == 0), stop=(ci == NCH - 1), tag="v_mm")
            p.ts(vf_sb[:, c * T + t0:c * T + t0 + 512], ps, bv[c], None,
                 op0=ALU.add, tag="v_ev")
    for it in range(T // 512):
        t0 = it * 512
        for ci in range(4):
            p.mm(cm.bcast_psum[0:8, 0:512], wso_bd[:, ci * 8:(ci + 1) * 8],
                 vf_sb[:, ci * T + t0:ci * T + t0 + 512],
                 start=(ci == 0), stop=(ci == 3), tag="vws_mm")
        p.copy(vws_sb[0:8, 0:512], cm.bcast_psum[0:8, 0:512], eng="act", tag="vws_cp")
        for tsb in range(4):
            tc = it * 4 + tsb
            tp = ptr_bf[tsb % 2]
            p.transpose(tp[0:128, 0:8],
                        vws_sb[0:8, tsb * 128:(tsb + 1) * 128],
                        identb[0:8, 0:8], tag="vws_tr")
            p.copy(vaug_sb[:, tc * 528 + 512:tc * 528 + 520],
                   tp[0:128, 0:8], eng="dve", tag="vws_vcp")
            for ci in range(4):
                tpv = ptr_bf[(tsb + ci) % 2]
                p.transpose(tpv[0:128, 0:128],
                            vf_sb[:, ci * T + t0 + tsb * 128:
                                  ci * T + t0 + (tsb + 1) * 128],
                            identb[:], tag="v_tr")
                p.copy(vaug_sb[:, tc * 528 + ci * 128:tc * 528 + (ci + 1) * 128],
                       tpv[0:128, 0:128], eng="dve", tag="v_vcp")
    for tc in range(16):
        p.dma(vaug_d[tc * 128:(tc + 1) * 128, :],
              vaug_sb[:, tc * 528:(tc + 1) * 528], q="actq", tag="va_out")
    for it in range(T // 512):
        t0 = it * 512
        nst = sqv_st[0:16, (it % 2) * 512:(it % 2) * 512 + 512]
        for c in range(4):
            ps = pp[c % 2][:, 0:512]
            for ci in range(NCH):
                p.mm(ps, wqvT[ci][:, c * 128:(c + 1) * 128],
                     xn_bf[:, ci * T + t0: ci * T + t0 + 512],
                     start=(ci == 0), stop=(ci == NCH - 1), tag="qv_mm")
            p.ts(qvf[:, 0:512], ps, bqv[c], None, op0=ALU.add, tag="qv_ev")
            p.mm(aux_ps, wsq_bd[:, c * 16:(c + 1) * 16],
                 qvf[:, 0:512], tag="sqv_mm")
            if c == 0:
                p.act(nst, aux_ps, AF.Copy, tag="sqv_c0")
            else:
                p.tt(nst, nst, aux_ps, op=ALU.add, tag="sqv_acc")
        p.ts(nst, nst, bs_col[0:16, 0:1], None, op0=ALU.add, tag="sqv_bs")
        p.dma(sqv_d[:, t0:t0 + 512], nst, q="spq", tag="sqv_out")
    return p


def build_l0a_flash(T=TLOC, p=None):
    p = p or P("l0a_flash")
    qf_d = p.dram_in("qf", [D, T], BF16)
    Kg_d = p.dram_in("Kg", [4 * D, T], BF16)
    VA_d = p.dram_in("VA", [4 * T, 528], BF16)
    qn_d = p.dram_in("qn", [16, T])
    KN_d = p.dram_in("KN", [16, 4 * T])
    sqv_d = p.dram_in("sqv", [16, T])
    af_d = p.dram_out("attnf", [D, T], BF16)

    SEQT = 4 * T
    vaug = p.sbuf("vaug", [128, 64 * 528], BF16)
    p.dma(vaug.rearrange("p (j n) -> p j n", n=528),
          VA_d.rearrange("(j p) n -> p j n", p=128), q="poolq", tag="vld")
    kaug = p.sbuf("kaug", [65, SEQT], BF16)
    p.memset(kaug[64:65, :], 1.0, eng="pool")
    qaug = p.sbuf("qaug", [65, T], BF16)
    p.memset(qaug[64:65, :], 0.0, eng="pool")
    qn_sb = p.sbuf("qn_sb", [16, T], F32)
    p.dma(qn_sb[0:16, :], qn_d, q="spq", tag="qnld")
    sqv_sb = p.sbuf("sqv_sb", [16, T], F32)
    p.dma(sqv_sb[0:16, :], sqv_d, q="spq", tag="sqld")
    sqv_row = p.sbuf("sqv_row", [1, T], F32)
    sqv_cols = p.sbuf("sqv_cols", [128, 16], F32)
    knt = p.sbuf("knt", [16, T], F32)
    kmax = p.sbuf("kmax", [16, 8], F32)
    for s in range(4):
        p.dma(knt[0:16, :], KN_d[:, s * T:(s + 1) * T], q="spq", tag="knld")
        p.red(kmax[0:16, s:s + 1], knt[0:16, :], op=ALU.max, tag="knmax")
    p.red(kmax[0:16, 4:5], kmax[0:16, 0:4], op=ALU.max, tag="knmax2")
    mneg = p.sbuf("mneg", [16, T], F32)
    p.ts(mneg[0:16, :], qn_sb[0:16, :], kmax[0:16, 4:5], -1.0,
         op0=ALU.mult, op1=ALU.mult, tag="mneg")

    identf = make_identity(p, "identf", F32)
    PT = p.sbuf("PT", [128, 2 * 512], BF16)
    psS = [p.psum("psS%d" % i, [128, 512], F32) for i in range(2)]
    psO = [p.psum("psO%d" % i, [128, 512], F32) for i in range(4)]
    psaux = p.psum("psaux", [128, 128], F32)
    oti = p.psum("oti", [64, 128], F32)
    comp = p.sbuf("comp", [128, 8], F32)
    ecp = p.sbuf("ecp", [128, 8], F32)
    ccol = p.sbuf("ccol", [128, 4], F32)
    otm = p.sbuf("otm", [128, 64], F32)
    attnst = p.sbuf("attnst", [64, 2 * 128], BF16)

    for h in range(16):
        p.dma(sqv_row[0:1, :], sqv_sb[h:h + 1, :], q="poolq", tag="sqrow")
        for sb_ in range(16):
            p.transpose(psaux[0:128, 0:1],
                        sqv_row[0:1, sb_ * 128:(sb_ + 1) * 128],
                        identf[0:1, 0:1], tag="sqtr")
            p.copy(sqv_cols[:, sb_:sb_ + 1], psaux[0:128, 0:1], eng="dve",
                   tag="sqcp")
        for s in range(4):
            p.dma(kaug[0:64, s * T:(s + 1) * T],
                  Kg_d[s * D + h * 64: s * D + (h + 1) * 64, :], q="spq",
                  tag="kld")
        p.dma(qaug[0:64, :],
              qf_d[(h // 2) * 128 + (h % 2) * 64:
                   (h // 2) * 128 + (h % 2) * 64 + 64, :], q="spq", tag="qld2")
        p.dma(qaug[64:65, :], mneg[h:h + 1, :], q="poolq", tag="qmn")
        for ib in range(4):
            i0 = ib * 512
            njt = SEQT // 128
            for gj in range(njt):
                ps = psS[gj % 2][:, 0:512]
                p.mm(ps, kaug[:, gj * 128:(gj + 1) * 128],
                     qaug[:, i0:i0 + 512], tag="s_mm")
                pt = PT[:, (gj % 2) * 512:(gj % 2) * 512 + 512]
                p.act(pt, ps, AF.Exp, tag="s_exp")
                first = (gj == 0)
                last = (gj == njt - 1)
                for ic in range(4):
                    p.mm(psO[ic][:, 0:512], pt[:, ic * 128:(ic + 1) * 128],
                         vaug[:, gj * 528:gj * 528 + 512],
                         start=first, stop=last, tag="av_mm")
                    p.mm(psaux[:, ic * 32:ic * 32 + 9],
                         pt[:, ic * 128:(ic + 1) * 128],
                         vaug[:, gj * 528 + 512:gj * 528 + 521],
                         start=(first and ic == 0), stop=last, tag="aux_mm")
            for ic in range(4):
                isub = ib * 4 + ic
                p.reciprocal(ccol[:, 0:1], psaux[:, ic * 32 + 8:ic * 32 + 9],
                             tag="linv")
                p.ts(comp[:, 0:8], psaux[:, ic * 32:ic * 32 + 8],
                     ccol[:, 0:1], sqv_cols[:, isub:isub + 1],
                     op0=ALU.mult, op1=ALU.add, tag="comp")
                p.red(ccol[:, 1:2], comp[:, 0:8], op=ALU.max, negate=True,
                      tag="rmax")
                p.act(ecp[:, 0:8], comp[:, 0:8], AF.Exp, bias=ccol[:, 1:2],
                      tag="rexp")
                p.red(ccol[:, 2:3], ecp[:, 0:8], op=ALU.add, tag="rsum")
                p.reciprocal(ccol[:, 3:4], ccol[:, 2:3], tag="rrcp")
                p.ts(ecp[:, 0:8], ecp[:, 0:8], ccol[:, 3:4], ccol[:, 0:1],
                     op0=ALU.mult, op1=ALU.mult, tag="aw")
                for r in range(8):
                    if r == 0:
                        p.ts(otm[:, 0:64], psO[ic][:, 0:64], ecp[:, 0:1],
                             None, op0=ALU.mult, tag="osum0")
                    else:
                        p.stt(otm[:, 0:64], psO[ic][:, r * 64:(r + 1) * 64],
                              ecp[:, r:r + 1], otm[:, 0:64],
                              op0=ALU.mult, op1=ALU.add, tag="osum")
                p.transpose(oti[0:64, 0:128], otm[:, 0:64], identf[:],
                            tag="otr")
                ast_ = attnst[0:64, (ic % 2) * 128:(ic % 2) * 128 + 128]
                p.copy(ast_, oti[0:64, 0:128], eng="act", tag="ocp")
                p.dma(af_d[h * 64:(h + 1) * 64, isub * 128:(isub + 1) * 128],
                      ast_, q="actq", tag="aout")
    return p


def build_l0a_mid(T=TLOC, p=None):
    p = p or P("l0a_mid")
    cm = Common(p, T, slim=True)
    xs_d = p.dram_in("xs", [D, T])
    u_d = p.dram_in("u", [D, T])
    af_d = p.dram_in("attnf", [D, T], BF16)
    ys_d = p.dram_out("ys", [D, T])
    A = Arena(p, "A", 24576)
    wfT_d = p.dram_in("wfT", [D, D])
    wr1T_d = p.dram_in("wr1T", [D, 2048])
    wr2T_d = p.dram_in("wr2T", [2048, D])
    bf_c = load_col_f32(p, p.dram_in("bf", [D]), D, "bfc")
    g2 = load_col_f32(p, p.dram_in("g2", [D]), D, "g2")
    b2 = load_col_f32(p, p.dram_in("b2", [D]), D, "b2")
    br1 = load_col_f32(p, p.dram_in("br1", [2048]), 2048, "br1")
    br2 = load_col_f32(p, p.dram_in("br2", [D]), D, "br2")
    wfT = load_wT_bf16(p, wfT_d, D, D, "wf", region=A.bf16(8192, NCH * D))
    af_sb = A.bf16(0, NCH * T)
    for c in range(NCH):
        p.dma(af_sb[:, c * T:(c + 1) * T], af_d[c * 128:(c + 1) * 128, :],
              q="spq", tag="afld")
    out1 = p.sbuf("out1", [128, NCH * T], F32)
    ub = p.sbuf("ub", [128, 1024], F32)
    uslot = [0]
    pp = [p.psum("mmp%d" % i, [128, 512], F32) for i in range(3)]

    for c in range(NCH):
        for it in range(T // 512):
            t0 = it * 512
            ps = pp[it % 2][:, 0:512]
            for ci in range(NCH):
                p.mm(ps, wfT[ci][:, c * 128:(c + 1) * 128],
                     af_sb[:, ci * T + t0: ci * T + t0 + 512],
                     start=(ci == 0), stop=(ci == NCH - 1), tag="wf_mm")
            ut = ub[:, (it % 2) * 512:(it % 2) * 512 + 512]
            p.dma(ut, u_d[c * 128:(c + 1) * 128, t0:t0 + 512], q="spq",
                  tag="uld")
            p.ts(out1[:, c * T + t0:c * T + t0 + 512], ps, bf_c[c], None,
                 op0=ALU.add, tag="wf_ev")
            p.tt(out1[:, c * T + t0:c * T + t0 + 512],
                 out1[:, c * T + t0:c * T + t0 + 512], ut, op=ALU.add,
                 tag="wf_u")

    on_bf = A.bf16(0, NCH * T)
    wr1T = load_wT_bf16(p, wr1T_d, D, 2048, "wr1",
                        region=A.bf16(8192, NCH * 2048))
    wr2T = load_wT_bf16(p, wr2T_d, 2048, D, "wr2",
                        region=A.bf16(16384, 16 * D))

    def o_tile(c, it):
        return out1[:, c * T + it * 512: c * T + (it + 1) * 512]

    def emit_on(c, it, sc):
        p.stt(on_bf[:, c * T + it * 512: c * T + (it + 1) * 512], sc, g2[c],
              b2[c].broadcast_to([128, 512]), op0=ALU.mult, op1=ALU.add,
              tag="onaff")
    ln_streaming(p, cm, T, o_tile, emit_on, "ln3_")

    h1 = p.sbuf("h1", [128, 16 * 512], BF16)
    xb = p.sbuf("xb", [128, 1024], F32)
    rb = p.sbuf("rb", [128, 1024], F32)
    for it in range(T // 512):
        t0 = it * 512
        for fo in range(16):
            ps = pp[fo % 2][:, 0:512]
            for ci in range(NCH):
                p.mm(ps, wr1T[ci][:, fo * 128:(fo + 1) * 128],
                     on_bf[:, ci * T + t0: ci * T + t0 + 512],
                     start=(ci == 0), stop=(ci == NCH - 1), tag="r1_mm")
            p.ts(h1[:, fo * 512:(fo + 1) * 512], ps, br1[fo], 0.0,
                 op0=ALU.add, op1=ALU.max, tag="r1_ev")
        for fo in range(NCH):
            ps = pp[2][:, 0:512]
            for ci in range(16):
                p.mm(ps, wr2T[ci][:, fo * 128:(fo + 1) * 128],
                     h1[:, ci * 512:(ci + 1) * 512],
                     start=(ci == 0), stop=(ci == 15), tag="r2_mm")
            xt = xb[:, (it % 2) * 512:(it % 2) * 512 + 512]
            rt = rb[:, (it % 2) * 512:(it % 2) * 512 + 512]
            p.dma(xt, xs_d[fo * 128:(fo + 1) * 128, t0:t0 + 512], q="spq",
                  tag="xld")
            p.ts(rt, ps, br2[fo], None, op0=ALU.add, tag="r2_ev")
            p.tt(rt, rt, out1[:, fo * T + t0:fo * T + t0 + 512], op=ALU.add,
                 tag="r2_o1")
            p.tt(rt, rt, xt, op=ALU.add, tag="r2_x")
            p.dma(ys_d[fo * 128:(fo + 1) * 128, t0:t0 + 512], rt, q="actq",
                  tag="yout")
    return p



def build_l3a_pre(T=TLOC, p=None, fused=False):
    p = p or P("l3a_pre")
    cm = Common(p, T, slim=True)
    xs_d = p.dram_in("xs", [D, T])
    winT = load_wT_bf16(p, p.dram_in("winT", [D, D]), D, D, "win")
    wkvT = load_wT_bf16(p, p.dram_in("wkvT", [D, 256]), D, 256, "wkv")
    wbT = load_wT_bf16(p, p.dram_in("wbT", [D, 128]), D, 128, "wb")
    wqrT = load_wT_bf16(p, p.dram_in("wqrT", [D, 128]), D, 128, "wqr")
    wrtT = load_wT_f32(p, p.dram_in("wrtT", [D, 4]), D, 4, "wrt")
    bbeta = load_col_f32(p, p.dram_in("bbeta", [128]), 128, "bb")
    kn_d = p.dram_out("kn", [128, T])
    v_d = p.dram_out("v", [128, T])
    be_d = p.dram_out("be", [128, T])
    qr_d = p.dram_out("qr", [128, T])
    uw_d = p.dram_out("uw", [128, T])
    sm_d = p.dram_out("sm", [128, T])

    xn_bf = p.sbuf("xn_bf", [128, NCH * T], BF16)
    xbuf = p.sbuf("xbuf", [128, 1024], F32)
    xslot = [0]

    def x_tile(c, it):
        s = xslot[0] % 2
        xslot[0] += 1
        sl = xbuf[:, s * 512:(s + 1) * 512]
        p.dma(sl, xs_d[c * 128:(c + 1) * 128, it * 512:(it + 1) * 512],
              q="spq", tag="xt")
        return sl

    def emit_xn(c, it, sc):
        p.copy(xn_bf[:, c * T + it * 512: c * T + (it + 1) * 512], sc,
               eng="pool", tag="xnbf")
    ln_streaming(p, cm, T, x_tile, emit_xn, "ln_")

    pp = [p.psum("mmp%d" % i, [128, 512], F32) for i in range(3)]
    identf = make_identity(p, "identf", F32)
    xp_bf = p.sbuf("xp_bf", [128, NCH * T], BF16)
    xpf = p.sbuf("xpf", [128, 512], F32)
    rlog = pp[2][:, 0:16]
    rlog_sb = p.sbuf("rlog_sb", [128, 16 * (T // 512)], F32)
    sc8 = p.sbuf("sc8", [128, 8], F32)
    m8 = p.sbuf("m8", [128, 8], F32)
    mcol = p.sbuf("mcol", [128, 4], F32)
    uwtm = p.sbuf("uwtm", [128, 4], F32)
    smtm = p.sbuf("smtm", [128, 4], F32)
    sm4 = p.sbuf("sm4", [4, T], F32)
    uw4 = p.sbuf("uw4", [4, T], F32)
    p.memset(sc8[:, 4:8], -2.0, eng="pool")
    tps = cm.bcast_psum[0:4, 0:256]

    for c in range(NCH):
        for it in range(T // 512):
            t0 = it * 512
            ps = pp[it % 2][:, 0:512]
            for ci in range(NCH):
                p.mm(ps, winT[ci][:, c * 128:(c + 1) * 128],
                     xn_bf[:, ci * T + t0: ci * T + t0 + 512],
                     start=(ci == 0), stop=(ci == NCH - 1), tag="xp_mm")
            p.act(xpf[:, 0:512], ps, AF.Silu, tag="xp_silu")
            p.copy(xp_bf[:, c * T + t0:c * T + t0 + 512], xpf[:, 0:512],
                   eng="pool", tag="xp_bf")
            rsb = rlog_sb[:, it * 16:(it + 1) * 16]
            if c == 0:
                p.memset(rsb, 0.0, eng="dve", tag="rt_z")
            for sub in range(4):
                p.mm(rlog[:, sub * 4:(sub + 1) * 4],
                     xpf[:, sub * 128:(sub + 1) * 128], wrtT[c][:, 0:4],
                     start=True, stop=True, tag="rt_mm")
            p.tt(rsb, rsb, rlog[:, 0:16], op=ALU.add,
                 tag="rt_acc")
            if c == NCH - 1:
                for sub in range(4):
                    s_ = it * 4 + sub
                    rsl = rlog_sb[:, it * 16 + sub * 4:it * 16 + sub * 4 + 4]
                    p.red(mcol[:, 0:1], rsl, op=ALU.max, negate=True,
                          tag="r_max")
                    p.act(smtm[:, 0:4], rsl, AF.Exp, bias=mcol[:, 0:1],
                          tag="r_exp")
                    p.red(mcol[:, 1:2], smtm[:, 0:4], op=ALU.add, tag="r_sum")
                    p.reciprocal(mcol[:, 2:3], mcol[:, 1:2], tag="r_rcp")
                    p.ts(smtm[:, 0:4], smtm[:, 0:4], mcol[:, 2:3], None,
                         op0=ALU.mult, tag="r_nrm")
                    p.copy(sc8[:, 0:4], smtm[:, 0:4], eng="dve", tag="r_cp8")
                    p.top8(m8[:, 0:8], sc8[:, 0:8], tag="r_top")
                    p.tt(mcol[:, 0:1], m8[:, 0:1], m8[:, 1:2], op=ALU.add,
                         tag="r_ws")
                    p.ts(mcol[:, 0:1], mcol[:, 0:1], 1e-8, None, op0=ALU.add,
                         tag="r_eps")
                    p.reciprocal(mcol[:, 1:2], mcol[:, 0:1], tag="r_rcp2")
                    p.ts(uwtm[:, 0:4], smtm[:, 0:4], m8[:, 1:2], None,
                         op0=ALU.is_ge, tag="r_msk")
                    p.tt(uwtm[:, 0:4], uwtm[:, 0:4], smtm[:, 0:4],
                         op=ALU.mult, tag="r_cw")
                    p.ts(uwtm[:, 0:4], uwtm[:, 0:4], mcol[:, 1:2], None,
                         op0=ALU.mult, tag="r_cw2")
                    p.transpose(tps[0:4, 0:128], smtm[:, 0:4], identf[:],
                                tag="r_trs")
                    p.copy(sm4[0:4, s_ * 128:(s_ + 1) * 128], tps[0:4, 0:128],
                           eng="dve", tag="r_cps")
                    p.transpose(tps[0:4, 128:256], uwtm[:, 0:4], identf[:],
                                tag="r_tru")
                    p.copy(uw4[0:4, s_ * 128:(s_ + 1) * 128],
                           tps[0:4, 128:256], eng="dve", tag="r_cpu")

    kf = p.sbuf("kf", [128, T], F32)
    for (wt, nout, dst, post) in ((wkvT, 256, None, "kv"),
                                  (wbT, 128, be_d, "be"),
                                  (wqrT, 128, qr_d, "qr")):
        nch_o = nout // 128
        for c in range(nch_o):
            for it in range(T // 512):
                t0 = it * 512
                ps = pp[it % 2][:, 0:512]
                for ci in range(NCH):
                    p.mm(ps, wt[ci][:, c * 128:(c + 1) * 128],
                         xp_bf[:, ci * T + t0: ci * T + t0 + 512],
                         start=(ci == 0), stop=(ci == NCH - 1),
                         tag=post + "_mm")
                if post == "kv":
                    if c == 0:
                        p.copy(kf[:, t0:t0 + 512], ps, eng="dve", tag="k_cp")
                    else:
                        st = xpf[:, 0:512]
                        p.copy(st, ps, eng="dve", tag="v_cp")
                        p.dma(v_d[:, t0:t0 + 512], st, q="actq", tag="v_out")
                elif post == "be":
                    st = xpf[:, 0:512]
                    p.act(st, ps, AF.Sigmoid, bias=bbeta[0], tag="be_sig")
                    p.dma(be_d[:, t0:t0 + 512], st, q="actq", tag="be_out")
                else:
                    st = xpf[:, 0:512]
                    p.copy(st, ps, eng="dve", tag="qr_cp")
                    p.dma(qr_d[:, t0:t0 + 512], st, q="actq", tag="qr_out")

    E128 = p.sbuf("E128", [128, 128], F32)
    p.memset(E128[:], 0.0, eng="pool")
    onec = p.sbuf("onec", [128, 1], BF16)
    p.memset(onec[:], 1.0, eng="pool")
    for blk in range(4):
        for jj in range(32):
            pass
    it4 = p.sbuf("it4", [128, 128], F32)
    p.iota(it4[:], [[1, 128]], base=0, channel_multiplier=0, tag="e_i")
    itp = p.sbuf("itp", [128, 1], F32)
    p.iota(itp[:], [[0, 1]], base=0, channel_multiplier=1, tag="e_p")
    p.ts(it4[:], it4[:], 1.0 / 32.0, -0.499, op0=ALU.mult, op1=ALU.add,
         tag="e_s")
    p.ts(it4[:], it4[:], MAGIC, None, op0=ALU.add, tag="e_m1")
    p.ts(it4[:], it4[:], -MAGIC, None, op0=ALU.add, tag="e_m2")
    p.ts(itp[:], itp[:], 1.0 / 32.0, -0.499, op0=ALU.mult, op1=ALU.add,
         tag="e_sp")
    p.ts(itp[:], itp[:], MAGIC, None, op0=ALU.add, tag="e_m1p")
    p.ts(itp[:], itp[:], -MAGIC, None, op0=ALU.add, tag="e_m2p")
    p.ts(E128[:], it4[:], itp[:, 0:1], None, op0=ALU.is_equal, tag="e_eq")
    ksq = p.sbuf("ksq", [128, 512], F32)
    if fused:
        kq_d = p.dram_out("kq", [128, 2 * T])
        qrt = p.sbuf("qrt", [128, 1024], F32)
        kqt = p.sbuf("kqt", [128, 2048], F32)
    for it in range(T // 512):
        t0 = it * 512
        p.tt(ksq[:, 0:512], kf[:, t0:t0 + 512], kf[:, t0:t0 + 512],
             op=ALU.mult, eng="pool", tag="kn_sq")
        ps = pp[2][:, 0:512]
        p.mm(ps, E128[:], ksq[:, 0:512], tag="kn_mm")
        p.act(ksq[:, 0:512], ps, AF.Sqrt, tag="kn_rt")
        p.ts(ksq[:, 0:512], ksq[:, 0:512], 1e-6, None, op0=ALU.add,
             tag="kn_eps")
        p.reciprocal(ksq[:, 0:512], ksq[:, 0:512], tag="kn_rcp")
        p.tt(ksq[:, 0:512], ksq[:, 0:512], kf[:, t0:t0 + 512], op=ALU.mult,
             tag="kn_mul")
        if fused:
            s_ = it % 2
            qsl = qrt[:, s_ * 512:(s_ + 1) * 512]
            p.dma(qsl, qr_d[:, t0:t0 + 512], q="spq", tag="qr_ld")
            p.copy(bass.AP(kqt, s_ * 1024, [[2048, 128], [2, 512]]),
                   ksq[:, 0:512], eng="dve", tag="kq_i0")
            p.copy(bass.AP(kqt, s_ * 1024 + 1, [[2048, 128], [2, 512]]),
                   qsl, eng="dve", tag="kq_i1")
            p.dma(kq_d[:, 2 * t0:2 * t0 + 1024],
                  kqt[:, s_ * 1024:s_ * 1024 + 1024], q="actq", tag="kq_out")
        else:
            p.dma(kn_d[:, t0:t0 + 512], ksq[:, 0:512], q="actq", tag="kn_out")

    E4 = p.sbuf("E4", [4, 128], F32)
    it4b = p.sbuf("it4b", [4, 128], F32)
    p.iota(it4b[0:4, :], [[1, 128]], base=0, channel_multiplier=0, tag="e4i")
    itpb = p.sbuf("itpb", [4, 1], F32)
    p.iota(itpb[0:4, :], [[0, 1]], base=0, channel_multiplier=32, tag="e4p")
    p.ts(it4b[0:4, :], it4b[0:4, :], 1.0 / 32.0, -0.499, op0=ALU.mult,
         op1=ALU.add, tag="e4s")
    p.ts(it4b[0:4, :], it4b[0:4, :], MAGIC, None, op0=ALU.add, tag="e4m1")
    p.ts(it4b[0:4, :], it4b[0:4, :], -MAGIC, None, op0=ALU.add, tag="e4m2")
    p.ts(itpb[0:4, :], itpb[0:4, :], 1.0 / 32.0, None, op0=ALU.mult,
         tag="e4sp")
    p.ts(E4[0:4, :], it4b[0:4, :], itpb[0:4, 0:1], None, op0=ALU.is_equal,
         tag="e4eq")
    for (src, dst) in ((sm4, sm_d), (uw4, uw_d)):
        for it in range(T // 512):
            t0 = it * 512
            ps = pp[2][:, 0:512]
            p.mm(ps, E4[0:4, :], src[0:4, t0:t0 + 512], tag="b4_mm")
            st = ksq[:, 0:512]
            p.copy(st, ps, eng="dve", tag="b4_cp")
            p.dma(dst[:, t0:t0 + 512], st, q="actq", tag="b4_out")
    return p


def build_l3a_scan(TT_=SEQ, debug=False, p=None):
    p = p or P("l3a_scan")
    NC_ = min(4096, TT_)
    NCHK = TT_ // NC_
    kq_d = p.dram_in("kqA", [128, 2 * TT_])
    v_d = p.dram_in("vA", [128, TT_])
    be_d = p.dram_in("beA", [128, TT_])
    uw_d = p.dram_in("uwA", [128, TT_])
    sm_d = p.dram_in("smA", [128, TT_])
    o32_d = p.dram_out("o32", [32, TT_])

    identf = make_identity(p, "identf", F32)
    ones32 = p.sbuf("ones32", [1, 32], F32)
    p.memset(ones32[:], 1.0, eng="pool")
    ones128 = p.sbuf("ones128", [1, 128], F32)
    p.memset(ones128[:], 1.0, eng="pool")
    S = p.sbuf("S", [128, 32], F32)
    p.memset(S[:], 0.0, eng="dve")
    bulk = p.sbuf("bulk", [128, 6 * NC_], F32)
    outbuf = p.sbuf("outbuf", [128, TT_], F32)
    row_sb = p.sbuf("row_sb", [1, 256], F32)
    kq_st = p.sbuf("kq_st", [128, 2], F32)
    scr32 = p.sbuf("scr32", [128, 32], F32)
    scr32b = p.sbuf("scr32b", [128, 32], F32)
    pre32 = p.sbuf("pre32", [128, 32], F32)
    upd32 = p.sbuf("upd32", [128, 32], F32)
    d32 = p.sbuf("d32", [128, 32], F32)
    cols = p.sbuf("cols", [128, 8], F32)
    rcol = cols[:, 0:1]
    delta = cols[:, 1:2]
    Sq = cols[:, 2:3]
    sg = cols[:, 3:4]
    bo = cols[:, 4:5]
    psR = p.psum("psR", [1, 256], F32)
    psC = p.psum("psC", [128, 32], F32)
    psD = p.psum("psD", [128, 32], F32)

    lpe = [p.es.enter_context(p.nc.semaphore("lpe%d" % c))
           for c in range(NCHK)]
    ldve = [p.es.enter_context(p.nc.semaphore("ldve%d" % c))
            for c in range(NCHK)]
    lact = [p.es.enter_context(p.nc.semaphore("lact%d" % c))
            for c in range(NCHK)]
    lpool = [p.es.enter_context(p.nc.semaphore("lpool%d" % c))
             for c in range(NCHK)]
    ldma = [p.es.enter_context(p.nc.semaphore("ldma%d" % c))
            for c in range(NCHK)]
    K0 = 16
    P_, V_, A_, O_ = 7, 5, 3, 6

    def colv(ofs_nc, i):
        return bass.AP(bulk, ofs_nc * NC_ + i, [[6 * NC_, 128], [1, 1]])

    def kqcols(i):
        return bass.AP(bulk, 2 * i, [[6 * NC_, 128], [1, 2]])

    def obcol(c, i):
        return bass.AP(outbuf, c * NC_ + i, [[TT_, 128], [1, 1]])

    pre_counts = dict(p.counts)

    def base_waits(eng, sems, skip=None):
        for s in COMPUTE_STREAMS + DMA_STREAMS:
            if s == skip:
                continue
            n = pre_counts[s] * (16 if s in DMA_STREAMS else 1)
            if n > 0:
                eng.wait_ge(sems[s], n)

    def fn_act(eng, sems):
        base_waits(eng, sems, skip="act")
        for c in range(NCHK):
            eng.sem_inc(lact[c], K0)
        for c in range(NCHK):
            if c > 0:
                eng.wait_ge(lpe[c - 1], K0 + P_ * NC_)
                eng.wait_ge(ldve[c - 1], K0 + V_ * NC_)
                eng.wait_ge(lpool[c - 1], K0 + O_ * NC_)
            eng.dma_start(out=bulk[:, 0:2 * NC_],
                          in_=kq_d[:, c * 2 * NC_:(c + 1) * 2 * NC_]
                          ).then_inc(ldma[c], 16)
            for bi, dr in enumerate((v_d, be_d, uw_d, sm_d)):
                eng.dma_start(out=bulk[:, (2 + bi) * NC_:(3 + bi) * NC_],
                              in_=dr[:, c * NC_:(c + 1) * NC_]
                              ).then_inc(ldma[c], 16)
            eng.wait_ge(ldma[c], 80)
            with eng.Fori(0, NC_) as i:
                eng.wait_ge(lpe[c], K0 + i * P_ + 2)
                eng.copy(row_sb[0:1, 0:160], psR[0:1, 0:160]
                         ).then_inc(lact[c], 1)
                eng.wait_ge(ldve[c], K0 + i * V_ + 3)
                eng.activation(upd32[:], pre32[:], AF.Tanh
                               ).then_inc(lact[c], 1)
                eng.wait_ge(ldve[c], K0 + i * V_ + 5)
                eng.wait_ge(lpool[c], K0 + i * O_ - 2)
                eng.activation(sg, Sq, AF.Sigmoid).then_inc(lact[c], 1)
        eng.copy(row_sb[0:1, 160:161], row_sb[0:1, 160:161]
                 ).then_inc(sems["act"], 1)

    def fn_pe(eng, sems):
        base_waits(eng, sems, skip="pe")
        for c in range(NCHK):
            eng.sem_inc(lpe[c], K0)
        for c in range(NCHK):
            eng.wait_ge(ldma[c], 80)
            with eng.Fori(0, NC_) as i:
                eng.wait_ge(lact[c], K0 + i * A_ - 2)
                eng.wait_ge(lpool[c], K0 + i * O_ + 1)
                eng.transpose(psR[0:1, 0:128], kq_st[:, 0:1], identf[:]
                              ).then_inc(lpe[c], 1)
                eng.transpose(psR[0:1, 128:256], kq_st[:, 1:2], identf[:]
                              ).then_inc(lpe[c], 1)
                eng.wait_ge(lact[c], K0 + i * A_ + 1)
                eng.wait_ge(ldve[c], K0 + i * V_ - 3)
                for b in range(4):
                    eng.matmul(psC[b * 32:(b + 1) * 32, 0:32],
                               ones32[0:1, 0:32],
                               row_sb[0:1, b * 32:(b + 1) * 32],
                               start=True, stop=True,
                               tile_position=(0, b * 32)).then_inc(lpe[c], 1)
                eng.wait_ge(ldve[c], K0 + i * V_)
                eng.matmul(psD[:, 0:32], ones128[0:1, 0:128],
                           row_sb[0:1, 128:160], start=True, stop=True
                           ).then_inc(lpe[c], 1)
        eng.nop().then_inc(sems["pe"], 1)

    def fn_dve(eng, sems):
        base_waits(eng, sems, skip="dve")
        for c in range(NCHK):
            eng.sem_inc(ldve[c], K0)
        for c in range(NCHK):
            eng.wait_ge(ldma[c], 80)
            with eng.Fori(0, NC_) as i:
                eng.wait_ge(lpe[c], K0 + i * P_ + 6)
                eng.wait_ge(lpool[c], K0 + i * O_ - 4)
                eng.wait_ge(ldve[c], K0 + i * V_ - 1)
                eng.scalar_tensor_tensor(scr32[:], S[:], 1.0, psC[:, 0:32],
                                         ALU.mult, ALU.mult, accum_out=rcol
                                         ).then_inc(ldve[c], 1)
                eng.wait_ge(lpool[c], K0 + i * O_ + 2)
                eng.tensor_scalar(scr32b[:], psC[:, 0:32], delta, None,
                                  ALU.mult).then_inc(ldve[c], 1)
                eng.wait_ge(ldve[c], K0 + i * V_ + 2)
                eng.wait_ge(lact[c], K0 + i * A_ - 1)
                eng.scalar_tensor_tensor(pre32[:], S[:], colv(3, i),
                                         scr32b[:], ALU.mult, ALU.add
                                         ).then_inc(ldve[c], 1)
                eng.wait_ge(lpool[c], K0 + i * O_ + 3)
                eng.scalar_tensor_tensor(S[:], d32[:], colv(4, i), S[:],
                                         ALU.mult, ALU.add
                                         ).then_inc(ldve[c], 1)
                eng.wait_ge(ldve[c], K0 + i * V_ + 4)
                eng.wait_ge(lpe[c], K0 + i * P_ + 7)
                eng.wait_ge(lact[c], K0 + i * A_)
                eng.wait_ge(lpool[c], K0 + i * O_ - 1)
                eng.scalar_tensor_tensor(scr32[:], S[:], 1.0, psD[:, 0:32],
                                         ALU.mult, ALU.mult, accum_out=Sq
                                         ).then_inc(ldve[c], 1)
        eng.engine_nop().then_inc(sems["dve"], 1)

    def fn_pool(eng, sems):
        base_waits(eng, sems, skip="pool")
        for c in range(NCHK):
            eng.sem_inc(lpool[c], K0)
        t1 = cols[:, 5:6]
        t2 = cols[:, 6:7]
        for c in range(NCHK):
            eng.wait_ge(ldma[c], 80)
            with eng.Fori(0, NC_) as i:
                eng.wait_ge(lpe[c], K0 + i * P_ - 5)
                eng.tensor_copy(kq_st[:, 0:2], kqcols(i)
                                ).then_inc(lpool[c], 1)
                eng.wait_ge(ldve[c], K0 + i * V_ + 1)
                eng.tensor_tensor(delta, colv(2, i), rcol, ALU.subtract
                                  ).then_inc(lpool[c], 1)
                eng.wait_ge(lact[c], K0 + i * A_ + 2)
                eng.tensor_tensor(d32[:], upd32[:], S[:], ALU.subtract
                                  ).then_inc(lpool[c], 1)
                eng.wait_ge(lact[c], K0 + i * A_ + 3)
                eng.tensor_tensor(t1, sg, Sq, ALU.mult).then_inc(lpool[c], 1)
                eng.tensor_tensor(t2, t1, Sq, ALU.mult).then_inc(lpool[c], 1)
                eng.tensor_tensor(obcol(c, i), t2, colv(5, i), ALU.mult
                                  ).then_inc(lpool[c], 1)
        eng.memset(cols[:, 7:8], 0.0).then_inc(sems["pool"], 1)

    p.rec("act", fn_act, reads=[kq_d, v_d, be_d, uw_d, sm_d],
          writes=[bulk[:, :], row_sb[0:1, :]], tag="megaact")
    p.rec("pe", fn_pe, reads=[identf[:], ones32[:], ones128[:]], writes=[],
          tag="megape")
    p.rec("dve", fn_dve, reads=[S[:]], writes=[S[:]], tag="megadve")
    p.rec("pool", fn_pool, reads=[], writes=[outbuf[:, :]], tag="megapool")

    I4 = p.sbuf("I4", [128, 32], F32)
    i4f = p.sbuf("i4f", [128, 32], F32)
    i4p = p.sbuf("i4p", [128, 2], F32)
    p.iota(i4f[:], [[1, 32]], base=0, channel_multiplier=0, tag="i4i")
    p.iota(i4p[:, 0:1], [[0, 1]], base=0, channel_multiplier=1, tag="i4p")
    p.ts(i4p[:, 1:2], i4p[:, 0:1], 1.0 / 32.0, -0.499, op0=ALU.mult,
         op1=ALU.add, tag="i4s")
    p.ts(i4p[:, 1:2], i4p[:, 1:2], MAGIC, None, op0=ALU.add, tag="i4m1")
    p.ts(i4p[:, 1:2], i4p[:, 1:2], -MAGIC, None, op0=ALU.add, tag="i4m2")
    p.stt(i4p[:, 0:1], i4p[:, 1:2], -32.0, i4p[:, 0:1], op0=ALU.mult,
          op1=ALU.add, tag="i4mod")
    p.ts(I4[:], i4f[:], i4p[:, 0:1], None, op0=ALU.is_equal, tag="i4eq")
    if debug:
        dbg = p.dram_out("dbg", [128, 160])
        dsb = p.sbuf("dsb", [128, 160], F32)
        p.copy(dsb[:, 0:32], S[:], eng="dve", tag="dbg")
        p.copy(dsb[:, 32:64], scr32b[:], eng="dve", tag="dbg")
        p.copy(dsb[:, 64:96], pre32[:], eng="dve", tag="dbg")
        p.copy(dsb[:, 96:128], upd32[:], eng="dve", tag="dbg")
        p.copy(dsb[:, 128:136], cols[:, 0:8], eng="dve", tag="dbg")
        p.copy(dsb[:, 136:138], kq_st[:, 0:2], eng="dve", tag="dbg")
        p.copy(dsb[:, 138:160], psC[:, 0:22], eng="dve", tag="dbg")
        p.dma(dbg, dsb[:, 0:160], q="spq", tag="dbgout")
        i4g = p.dram_out("i4g", [128, 32])
        p.dma(i4g, I4[:], q="spq", tag="i4out")
        i4h = p.dram_out("i4h", [128, 34])
        i4hs = p.sbuf("i4hs", [128, 34], F32)
        p.copy(i4hs[:, 0:32], i4f[:], eng="dve", tag="dbg2")
        p.copy(i4hs[:, 32:34], i4p[:, 0:2], eng="dve", tag="dbg2")
        p.dma(i4h, i4hs[:, 0:34], q="spq", tag="i4out2")
        ob_d = p.dram_out("obd", [128, TT_])
        p.dma(ob_d, outbuf[:, 0:TT_], q="spq", tag="dbgout2")
    pso = p.psum("pso", [32, 512], F32)
    ost = p.sbuf("ost", [32, 1024], F32)
    tok = p.sbuf("tok", [1, 4], F32)
    p.memset(tok[0:1, 0:1], 0.0, eng="dve", tag="bar")
    p.memset(tok[0:1, 1:2], 0.0, eng="pool", tag="bar")
    p.copy(tok[0:1, 2:3], tok[0:1, 0:1], eng="act", tag="bar")

    for it, t0 in enumerate(range(0, TT_, 512)):
        w = min(512, TT_ - t0)
        st = ost[0:32, (it % 2) * 512:(it % 2) * 512 + w]
        p.mm(pso[0:32, 0:w], I4[:], outbuf[:, t0:t0 + w], tag="o_red")
        p.copy(st, pso[0:32, 0:w], eng="act", tag="o_cp")
        p.dma(o32_d[:, t0:t0 + w], st, q="spq", tag="o_out")
    return p


def build_l3a_post(T=TLOC, p=None):
    p = p or P("l3a_post")
    cm = Common(p, T, slim=True)
    xs_d = p.dram_in("xs", [D, T])
    o32_d = p.dram_in("o32", [32, T])
    woT = load_wT_f32(p, p.dram_in("woT", [32, D]), 128, 8 * D, "wo",
                      q="spq") if False else None
    woT_d = p.dram_in("woT", [32, D])
    wo_sb = p.sbuf("wo_sb", [32, D], BF16)
    p.dma(wo_sb[0:32, :], woT_d, q="poolq", tag="wold")
    wT = p.dram_in("wT", [D, D])
    ys_d = p.dram_out("ys", [D, T])

    o32_sb = p.sbuf("o32_sb", [32, T], BF16)
    p.dma(o32_sb[0:32, :], o32_d, q="poolq", tag="o32ld")

    x2 = p.sbuf("x2", [128, NCH * T], F32)
    pp = [p.psum("mmp%d" % i, [128, 512], F32) for i in range(3)]
    xbuf = p.sbuf("xbuf", [128, 1024], F32)
    for c in range(NCH):
        for it in range(T // 512):
            t0 = it * 512
            ps = pp[it % 2][:, 0:512]
            p.mm(ps, wo_sb[0:32, c * 128:(c + 1) * 128],
                 o32_sb[0:32, t0:t0 + 512], tag="wo_mm")
            xt = xbuf[:, (it % 2) * 512:(it % 2) * 512 + 512]
            p.dma(xt, xs_d[c * 128:(c + 1) * 128, t0:t0 + 512], q="spq",
                  tag="xld")
            p.tt(x2[:, c * T + t0:c * T + t0 + 512], ps, xt, op=ALU.add,
                 tag="res1")

    wf = p.sbuf("wf", [128, NCH * D], F32)
    wsgn = p.sbuf("wsgn", [128, NCH * D], BF16)
    wtiles = []
    p.dma(wf.rearrange("p (c n) -> p c n", n=D),
          wT.rearrange("(c p) n -> p c n", p=128), q="spq", tag="ldw")
    for c in range(NCH):
        sl_f = wf[:, c * D:(c + 1) * D]
        sl_s = wsgn[:, c * D:(c + 1) * D]
        p.ts(sl_s, sl_f, 0.0, None, op0=ALU.is_ge, tag="sign")
        p.ts(sl_s, sl_s, 2.0, -1.0, op0=ALU.mult, op1=ALU.add, tag="sign2")
        wtiles.append(sl_s)
    xn_bf = p.sbuf("xn_bf", [128, NCH * T], BF16)

    def x2_tile(c, it):
        return x2[:, c * T + it * 512: c * T + (it + 1) * 512]

    def emit_xn(c, it, sc):
        p.copy(xn_bf[:, c * T + it * 512: c * T + (it + 1) * 512], sc,
               eng="pool", tag="xnbf")
    ln_streaming(p, cm, T, x2_tile, emit_xn, "ln_")

    def evac(fo, it, ps):
        xsl = x2[:, fo * T + it * 512: fo * T + (it + 1) * 512]
        p.tt(xsl, ps, xsl, op=ALU.add, tag="resid")

    matmul_fm(p, wtiles, xn_bf, D, D, T, pp, evac, "l3f_")
    for c in range(NCH):
        p.dma(ys_d[c * 128:(c + 1) * 128, :], x2[:, c * T:(c + 1) * T],
              q="actq", tag="ststream")
    return p


def build_lnsum(T=TLOC, p=None):
    p = p or P("lnsum")
    cm = Common(p, T, slim=True)
    xs_d = p.dram_in("xs", [D, T])
    sln_d = p.dram_out("sln", [128, NCH])
    acc = p.sbuf("acc", [128, NCH], F32)
    rtmp = p.sbuf("rtmp", [128, 1], F32)
    p.memset(acc[:], 0.0, eng="dve")
    xbuf = p.sbuf("xbuf", [128, 1024], F32)
    xslot = [0]

    def x_tile(c, it):
        s = xslot[0] % 2
        xslot[0] += 1
        sl = xbuf[:, s * 512:(s + 1) * 512]
        p.dma(sl, xs_d[c * 128:(c + 1) * 128, it * 512:(it + 1) * 512],
              q="spq", tag="xt")
        return sl

    def emit_out(c, it, sc):
        p.red(rtmp[:], sc, op=ALU.add, tag="rsum")
        p.tt(acc[:, c:c + 1], acc[:, c:c + 1], rtmp[:], op=ALU.add, tag="racc")

    ln_streaming(p, cm, T, x_tile, emit_out, "ln_")
    p.dma(sln_d, acc[:], q="actq", tag="sout")
    return p



_PROGS = {}


def _prog(name, builder, *args):
    if name not in _PROGS:
        _PROGS[name] = builder(*args) if args else builder(TLOC)
    return _PROGS[name]


def kernel_unfused(**ins):
    asc = np.ascontiguousarray
    gi = lambda k: np.asarray(ins[k])
    x = gi("L0_attn_x")
    xs = [asc(x[c // 4, (c % 4) * TLOC:((c % 4) + 1) * TLOC, :].T)
          for c in range(8)]
    t0col = [np.full(128, (c % 4) * TLOC, np.float32) for c in range(8)]

    in_a = [{"xs": xs[c],
             "wqT": asc(gi("L0_attn_Wq").T), "wkT": asc(gi("L0_attn_Wk").T),
             "wvT": asc(gi("L0_attn_Wv").T), "wqvT": asc(gi("L0_attn_Wqv").T),
             "bq": gi("L0_attn_bq"), "bk": gi("L0_attn_bk"),
             "bv": gi("L0_attn_bv"), "bqv": gi("L0_attn_bqv"),
             "ws": gi("L0_attn_Ws"), "bs": gi("L0_attn_bs"),
             "g1": gi("L0_attn_g1"), "b1": gi("L0_attn_b1")}
            for c in range(8)]
    ra = _prog("l0a_pre", build_l0a_pre).run(in_a)

    in_b = []
    for c in range(8):
        b = c // 4
        g = lambda name, cc: np.asarray(ra.results[cc][name])
        in_b.append({
            "qf": g("qf", c),
            "Kg": np.concatenate([g("kf", b * 4 + p_) for p_ in range(4)], 0),
            "VA": np.concatenate([g("vaug", b * 4 + p_) for p_ in range(4)], 0),
            "qn": g("qn", c),
            "KN": np.concatenate([g("kn", b * 4 + p_) for p_ in range(4)], 1),
            "sqv": g("sqv", c)})
    rb = _prog("l0a_flash", build_l0a_flash).run(in_b)

    in_c = [{"xs": xs[c], "u": np.asarray(ra.results[c]["u"]),
             "attnf": np.asarray(rb.results[c]["attnf"]),
             "wfT": asc(gi("L0_attn_Wf").T), "bf": gi("L0_attn_bf"),
             "g2": gi("L0_attn_g2"), "b2": gi("L0_attn_b2"),
             "wr1T": asc(gi("L0_attn_Wr1").T), "br1": gi("L0_attn_br1"),
             "wr2T": asc(gi("L0_attn_Wr2").T), "br2": gi("L0_attn_br2")}
            for c in range(8)]
    rc = _prog("l0a_mid", build_l0a_mid).run(in_c)
    ys = [asc(np.asarray(rc.results[c]["ys"])) for c in range(8)]

    rg = _prog("lnsum", build_lnsum).run([{"xs": ys[c]} for c in range(8)])
    slns = [np.asarray(rg.results[c]["sln"]).T.reshape(D) for c in range(8)]
    gsums = {b: asc(np.stack([slns[b * 4 + q] for q in range(4)], 1))
             for b in (0, 1)}
    in_f = [{"xs": ys[c], "gsums": gsums[c // 4],
             "cn_w1": gi("L0_ffn_cn_w1"), "cn_w2": gi("L0_ffn_cn_w2"),
             "cn_w3": gi("L0_ffn_cn_w3"), "cn_w4": gi("L0_ffn_cn_w4"),
             "cn_b1": gi("L0_ffn_cn_b1"), "cn_b2": gi("L0_ffn_cn_b2"),
             "cn_b3": gi("L0_ffn_cn_b3"), "cn_b4": gi("L0_ffn_cn_b4"),
             "gc_w1": gi("L0_ffn_gc_w1"), "gc_w2": gi("L0_ffn_gc_w2"),
             "gc_w3": gi("L0_ffn_gc_w3"),
             "gc_b1": gi("L0_ffn_gc_b1"), "gc_b2": gi("L0_ffn_gc_b2"),
             "gc_b3": gi("L0_ffn_gc_b3"),
             "ar": gi("L0_ffn_adaptive_range")} for c in range(8)]
    rf = _prog("l0f", build_l0f).run(in_f)
    ys = [asc(np.asarray(rf.results[c]["ys"])) for c in range(8)]

    in_1a = [{"xs": ys[c], "t0col": t0col[c],
              "nu_log": gi("L1_attn_nu_log"),
              "theta_log": gi("L1_attn_theta_log"),
              "BreT": asc(gi("L1_attn_B_re").T),
              "BimT": asc(gi("L1_attn_B_im").T)} for c in range(8)]
    r1a = _prog("l1a_a", build_l1a_a).run(in_1a)
    in_1b = []
    for c in range(8):
        b, q = c // 4, c % 4
        Fall = np.zeros((D, 8), np.float32)
        for p_ in range(4):
            Fp = np.asarray(r1a.results[b * 4 + p_]["Fc"])
            Fall[:, 2 * p_] = Fp[:, 0]
            Fall[:, 2 * p_ + 1] = Fp[:, 1]
        a4 = np.full(4, 1e7, np.float32)
        for p_ in range(q):
            a4[p_] = 2048.0 * (q - 1 - p_)
        in_1b.append({
            "xs": ys[c], "xn": np.asarray(r1a.results[c]["xn"]),
            "ure": np.asarray(r1a.results[c]["ure"]),
            "uim": np.asarray(r1a.results[c]["uim"]),
            "Fall": Fall, "a4": a4, "t0col": t0col[c],
            "nu_log": gi("L1_attn_nu_log"),
            "theta_log": gi("L1_attn_theta_log"),
            "D_skip": gi("L1_attn_D_skip"),
            "CreT": asc(gi("L1_attn_C_re").T),
            "CimT": asc(gi("L1_attn_C_im").T)})
    r1b = _prog("l1a_b", build_l1a_b).run(in_1b)
    ys = [asc(np.asarray(r1b.results[c]["ys"])) for c in range(8)]

    in_1f = [{"xs": ys[c], "gate_wT": asc(gi("L1_ffn_gate_w").T),
              "wg": gi("L1_ffn_w_gate"), "wu": gi("L1_ffn_w_up"),
              "wd": gi("L1_ffn_w_down"), "swg": gi("L1_ffn_sw_gate"),
              "swu": gi("L1_ffn_sw_up"), "swd": gi("L1_ffn_sw_down")}
             for c in range(8)]
    r1f = _prog("l1f", build_l1f).run(in_1f)
    ys = [asc(np.asarray(r1f.results[c]["ys"])) for c in range(8)]

    kidx = ((np.arange(128) % 64) // 2).astype(np.float32)
    alt = np.where(np.arange(128) % 2 == 0, -1.0, 1.0).astype(np.float32)
    pswap = np.zeros((128, 128), np.float32)
    for i in range(128):
        pswap[i ^ 1, i] = 1.0
    in_2a = [{"xs": ys[c], "t0col": t0col[c], "kidx": kidx, "alt": alt,
              "pswap": pswap, "wqkvT": asc(gi("L2_attn_w_qkv").T),
              "bqkv": gi("L2_attn_b_qkv")} for c in range(8)]
    r2a = _prog("l2a_pre", build_l2a_pre).run(in_2a)
    in_2b = []
    for c in range(8):
        b, q = c // 4, c % 4
        g = lambda name, cc: np.asarray(r2a.results[cc][name])
        vedl = np.zeros(D, np.asarray(r2a.results[c]["vf"]).dtype)
        vedr = np.zeros(D, vedl.dtype)
        if q > 0:
            vedl[:] = g("vf", c - 1)[:, -1]
        if q < 3:
            vedr[:] = g("vf", c + 1)[:, 0]
        in_2b.append({
            "xs": ys[c], "qf": g("qf", c),
            "Kg": np.concatenate([g("kf", b * 4 + p_) for p_ in range(4)], 0),
            "VA": np.concatenate([g("vaug", b * 4 + p_) for p_ in range(4)], 0),
            "qn": g("qn", c),
            "KN": np.concatenate([g("kn", b * 4 + p_) for p_ in range(4)], 1),
            "vf": g("vf", c), "vedl": vedl, "vedr": vedr,
            "wlep": gi("L2_attn_w_lepe").reshape(D, 3),
            "blep": gi("L2_attn_b_lepe")})
    r2b = _prog("l2a_flash", build_l2a_flash).run(in_2b)
    ys = [asc(np.asarray(r2b.results[c]["ys"])) for c in range(8)]

    in_2f = [{"xs": ys[c], "router_w": gi("L2_ffn_router_w"),
              "router_b": gi("L2_ffn_router_b"),
              "w1": gi("L2_ffn_w1"), "w2": gi("L2_ffn_w2")}
             for c in range(8)]
    r2f = _prog("l2f", build_l2f).run(in_2f)
    ys = [asc(np.asarray(r2f.results[c]["ys"])) for c in range(8)]

    perm_k = [blk * 64 + n for blk in range(4) for n in range(32)]
    perm_v = [blk * 64 + 32 + n for blk in range(4) for n in range(32)]
    wkv = gi("L3_attn_W_kv")
    in_3a = [{"xs": ys[c],
              "winT": asc(gi("L3_attn_W_in").T),
              "wkvT": asc(wkv[perm_k + perm_v].T),
              "wbT": asc(gi("L3_attn_W_beta").T),
              "wqrT": asc(np.tile(gi("L3_attn_W_q"), (4, 1)).T),
              "wrtT": asc(gi("L3_attn_W_router").T),
              "bbeta": gi("L3_attn_b_beta").reshape(-1)} for c in range(8)]
    r3a = _prog("l3a_pre", build_l3a_pre).run(in_3a)
    in_3s = []
    for c in range(8):
        b = c // 4
        g = lambda name, cc: np.asarray(r3a.results[cc][name])
        m = {}
        for nm, key in (("vA", "v"), ("beA", "be"), ("uwA", "uw"),
                        ("smA", "sm")):
            m[nm] = asc(np.concatenate(
                [g(key, b * 4 + p_) for p_ in range(4)], axis=1))
        kn = np.concatenate([g("kn", b * 4 + p_) for p_ in range(4)], axis=1)
        qr = np.concatenate([g("qr", b * 4 + p_) for p_ in range(4)], axis=1)
        kq = np.empty((128, 2 * kn.shape[1]), np.float32)
        kq[:, 0::2] = kn
        kq[:, 1::2] = qr
        m["kqA"] = kq
        in_3s.append(m)
    r3s = _prog("l3a_scan", build_l3a_scan, SEQ).run(in_3s)
    in_3p = []
    for c in range(8):
        q = c % 4
        o32 = np.asarray(r3s.results[c]["o32"])[:, q * TLOC:(q + 1) * TLOC]
        in_3p.append({"xs": ys[c], "o32": asc(o32),
                      "woT": asc(gi("L3_attn_W_out").T),
                      "wT": asc(gi("L3_ffn_weight").T)})
    r3p = _prog("l3a_post", build_l3a_post).run(in_3p)

    outs = [np.asarray(r3p.results[c]["ys"]).T for c in range(8)]
    return np.ascontiguousarray(np.stack(
        [np.concatenate(outs[0:4], 0), np.concatenate(outs[4:8], 0)],
        0).astype(np.float32))



T2 = TLOC


def _fused_exchange(p, ev, ctx, pre0, pre1, post0, post1, reads, writes):
    nc = p.nc
    fl0 = nc.dram_tensor("exfl%d_0" % ev, [1, 1], I32, addr_space="Shared")
    fl1 = nc.dram_tensor("exfl%d_1" % ev, [1, 1], I32, addr_space="Shared")
    exd = p.es.enter_context(nc.semaphore("exd%d" % ev))
    exp_ = p.es.enter_context(nc.semaphore("exp%d" % ev))
    pt, nt, ft, dum = ctx["pt"], ctx["nt"], ctx["ft"], ctx["dum"]
    assert len(pre0) == len(pre1) and len(post0) == len(post1)

    def fn(eng, sems):
        dc = [0]
        with eng.register("xr%d" % ev) as rpar, \
             eng.register("xn%d" % ev) as rn, \
             eng.register("xf%d" % ev) as rfl, \
             eng.register("xc%d" % ev) as rpc:
            eng.reg_load(rpar, pt[:1, :1])
            eng.reg_load(rn, nt[:1, :1])
            with eng.If_eq(rpar, 0):
                for dst, src in pre0:
                    eng.dma_start(out=dst, in_=src).then_inc(exd, 16)
            with eng.Else():
                for dst, src in pre1:
                    eng.dma_start(out=dst, in_=src).then_inc(exd, 16)
            dc[0] += len(pre0)
            eng.wait_ge(exd, 16 * dc[0])
            with eng.If_eq(rpar, 0):
                eng.dma_start(out=fl0.ap(), in_=nt[:]).then_inc(exd, 16)
            with eng.Else():
                eng.dma_start(out=fl1.ap(), in_=nt[:]).then_inc(exd, 16)
            dc[0] += 1
            eng.wait_ge(exd, 16 * dc[0])
            eng.reg_mov(rfl, 1)
            eng.reg_mov(rpc, 0)
            with eng.While(rfl):
                eng.reg_add(rpc, rpc, 16)
                with eng.If_eq(rpar, 0):
                    eng.dma_start(out=ft[:], in_=fl1.ap()).then_inc(exp_, 16)
                with eng.Else():
                    eng.dma_start(out=ft[:], in_=fl0.ap()).then_inc(exp_, 16)
                eng.wait_ge(exp_, rpc)
                eng.reg_load(rfl, ft[:1, :1])
                eng.reg_alu(rfl, rfl, rn, ALU.subtract)
            with eng.If_eq(rpar, 0):
                for dst, src in post0:
                    eng.dma_start(out=dst, in_=src).then_inc(exd, 16)
            with eng.Else():
                for dst, src in post1:
                    eng.dma_start(out=dst, in_=src).then_inc(exd, 16)
            dc[0] += len(post0)
            eng.wait_ge(exd, 16 * dc[0])
        eng.memset(dum[:], 0.0).then_inc(sems["pool"], 1)
    return p.rec("pool", fn, reads=list(reads), writes=list(writes),
                 tag="ex%d" % ev)


def _fused_sidecopy(p, ev, ctx, plan0, plan1, reads, writes):
    exd = p.es.enter_context(p.nc.semaphore("scd%d" % ev))
    pt, dum = ctx["pt"], ctx["dum"]
    assert len(plan0) == len(plan1)

    def fn(eng, sems):
        with eng.register("sr%d" % ev) as rpar:
            eng.reg_load(rpar, pt[:1, :1])
            with eng.If_eq(rpar, 0):
                for dst, src in plan0:
                    eng.dma_start(out=dst, in_=src).then_inc(exd, 16)
            with eng.Else():
                for dst, src in plan1:
                    eng.dma_start(out=dst, in_=src).then_inc(exd, 16)
            eng.wait_ge(exd, 16 * len(plan0))
        eng.memset(dum[:], 0.0).then_inc(sems["pool"], 1)
    return p.rec("pool", fn, reads=list(reads), writes=list(writes),
                 tag="sc%d" % ev)


def build_fused(debug=False):
    p = P("fused")
    nc = p.nc

    xs_in = p.dram_in("xs", [D, 2 * T2])
    cid = p.dram_in("cid", [1, 1], I32)
    non = p.dram_in("nonce", [1, 1], I32)
    out_ys = p.dram_out("ys", [D, 2 * T2])

    W = {}

    def win(name, shape, dtype=F32):
        W[name] = p.dram_in(name, shape, dtype)
        return W[name]

    for nm, sh in (("l0p_wqT", [D, D]), ("l0p_wkT", [D, D]),
                   ("l0p_wvT", [D, 512]), ("l0p_wqvT", [D, 512]),
                   ("l0p_bq", [D]), ("l0p_bk", [D]), ("l0p_bv", [512]),
                   ("l0p_bqv", [512]), ("l0p_ws", [1, 96]), ("l0p_bs", [1]),
                   ("l0p_g1", [D]), ("l0p_b1", [D]),
                   ("l0m_wfT", [D, D]), ("l0m_bf", [D]), ("l0m_g2", [D]),
                   ("l0m_b2", [D]), ("l0m_wr1T", [D, 2048]),
                   ("l0m_br1", [2048]), ("l0m_wr2T", [2048, D]),
                   ("l0m_br2", [D]),
                   ("l0f_cn_w1", [1024, 1024]), ("l0f_cn_w2", [1024, 512]),
                   ("l0f_cn_w3", [512, 256]), ("l0f_cn_w4", [256, 1]),
                   ("l0f_cn_b1", [1024]), ("l0f_cn_b2", [512]),
                   ("l0f_cn_b3", [256]), ("l0f_cn_b4", [1]),
                   ("l0f_gc_w1", [1024, 512]), ("l0f_gc_w2", [512, 256]),
                   ("l0f_gc_w3", [256, 1]), ("l0f_gc_b1", [512]),
                   ("l0f_gc_b2", [256]), ("l0f_gc_b3", [1]),
                   ("l0f_ar", [1]),
                   ("l1_nu_log", [D]), ("l1_theta_log", [D]),
                   ("l1_BreT", [D, D]), ("l1_BimT", [D, D]),
                   ("l1_CreT", [D, D]), ("l1_CimT", [D, D]),
                   ("l1_D_skip", [D]),
                   ("l1f_gate_wT", [1024, 8]), ("l1f_wg", [8, 1024, 1408]),
                   ("l1f_wu", [8, 1024, 1408]), ("l1f_wd", [8, 1408, 1024]),
                   ("l1f_swg", [1024, 2816]), ("l1f_swu", [1024, 2816]),
                   ("l1f_swd", [2816, 1024]),
                   ("l2_wqkvT", [D, 3 * D]), ("l2_bqkv", [3 * D]),
                   ("l2_kidx", [128]), ("l2_alt", [128]),
                   ("l2_pswap", [128, 128]),
                   ("l2_wlep", [D, 3]), ("l2_blep", [D]),
                   ("l2f_router_w", [1024, 8]), ("l2f_router_b", [8]),
                   ("l2f_w1", [8, 1024, 8192]), ("l2f_w2", [8, 4096, 1024]),
                   ("l3_winT", [D, D]), ("l3_wkvT", [D, 256]),
                   ("l3_wbT", [D, 128]), ("l3_wqrT", [D, 128]),
                   ("l3_wrtT", [D, 4]), ("l3_bbeta", [128]),
                   ("l3_woT", [32, D]), ("l3_wT", [D, D]),
                   ("t0A", [128]), ("t0B", [128]), ("a4A", [4]), ("a4B", [4])):
        win(nm, sh)

    def tmp(name, shape, dtype=F32):
        return p.dram_tmp(name, shape, dtype)

    ys = {k: tmp("ys%d" % k, [D, 2 * T2]) for k in range(6)}
    loc = {}
    for i, I in enumerate("AB"):
        loc["u" + I] = tmp("u" + I, [D, T2])
        loc["qf" + I] = tmp("qf" + I, [D, T2], BF16)
        loc["kf" + I] = tmp("kf" + I, [D, T2], BF16)
        loc["vaug" + I] = tmp("vaug" + I, [T2, 528], BF16)
        loc["sqv" + I] = tmp("sqv" + I, [16, T2])
        loc["qn" + I] = tmp("qn" + I, [16, T2])
        loc["kn" + I] = tmp("kn" + I, [16, T2])
        loc["attnf" + I] = tmp("attnf" + I, [D, T2], BF16)
        loc["sln" + I] = tmp("sln" + I, [128, NCH])
        loc["xn1" + I] = tmp("xn1" + I, [D, T2])
        loc["ure" + I] = tmp("ure" + I, [D, T2])
        loc["uim" + I] = tmp("uim" + I, [D, T2])
        loc["Fc" + I] = tmp("Fc" + I, [D, 2])
        loc["qf2" + I] = tmp("qf2" + I, [D, T2], BF16)
        loc["kf2" + I] = tmp("kf2" + I, [D, T2], BF16)
        loc["vf2" + I] = tmp("vf2" + I, [D, T2], BF16)
        loc["vaug2" + I] = tmp("vaug2" + I, [T2, 1040], BF16)
        loc["qn2" + I] = tmp("qn2" + I, [16, T2])
        loc["kn2" + I] = tmp("kn2" + I, [16, T2])
        loc["vedge" + I] = tmp("vedge" + I, [2, D], BF16)
        loc["vedl" + I] = tmp("vedl" + I, [D], BF16)
        loc["vedr" + I] = tmp("vedr" + I, [D], BF16)
        loc["kq3" + I] = tmp("kq3" + I, [128, 2 * T2])
        loc["qr3" + I] = tmp("qr3" + I, [128, T2])
        loc["kn3" + I] = tmp("kn3" + I, [128, T2])
        loc["v3" + I] = tmp("v3" + I, [128, T2])
        loc["be3" + I] = tmp("be3" + I, [128, T2])
        loc["uw3" + I] = tmp("uw3" + I, [128, T2])
        loc["sm3" + I] = tmp("sm3" + I, [128, T2])
        loc["o32" + I] = tmp("o32" + I, [32, T2])
    Kg0 = tmp("Kg0", [4 * D, T2], BF16)
    VA0 = tmp("VA0", [4 * T2, 528], BF16)
    KN0 = tmp("KN0", [16, 4 * T2])
    gq = tmp("gq", [128, 32])
    Fq = [tmp("Fq%d" % i, [D, 2]) for i in range(4)]
    Fall = tmp("Fall", [D, 8])
    Kg2 = tmp("Kg2", [4 * D, T2], BF16)
    VA2 = tmp("VA2", [4 * T2, 1040], BF16)
    KN2 = tmp("KN2", [16, 4 * T2])
    zed = tmp("zed", [D], BF16)
    kqF = tmp("kqF", [128, 2 * SEQ])
    vF = tmp("vF", [128, SEQ])
    beF = tmp("beF", [128, SEQ])
    uwF = tmp("uwF", [128, SEQ])
    smF = tmp("smF", [128, SEQ])
    o32F = tmp("o32F", [32, SEQ])

    def shpair(name, shape, dtype=F32):
        return (nc.dram_tensor("sh_%s_0" % name, shape, dtype,
                               addr_space="Shared").ap(),
                nc.dram_tensor("sh_%s_1" % name, shape, dtype,
                               addr_space="Shared").ap())

    sh = {}
    for I in "AB":
        sh["kf" + I] = shpair("kf" + I, [D, T2], BF16)
        sh["vaug" + I] = shpair("vaug" + I, [T2, 528], BF16)
        sh["kn" + I] = shpair("kn" + I, [16, T2])
        sh["sln" + I] = shpair("sln" + I, [128, NCH])
        sh["Fc" + I] = shpair("Fc" + I, [D, 2])
        sh["kf2" + I] = shpair("kf2" + I, [D, T2], BF16)
        sh["vaug2" + I] = shpair("vaug2" + I, [T2, 1040], BF16)
        sh["kn2" + I] = shpair("kn2" + I, [16, T2])
        sh["vedge" + I] = shpair("vedge" + I, [2, D], BF16)
        sh["kq3" + I] = shpair("kq3" + I, [128, 2 * T2])
        sh["v3" + I] = shpair("v3" + I, [128, T2])
        sh["be3" + I] = shpair("be3" + I, [128, T2])
        sh["uw3" + I] = shpair("uw3" + I, [128, T2])
        sh["sm3" + I] = shpair("sm3" + I, [128, T2])

    pt = p.sbuf("ex_pt", [1, 1], I32)
    nt = p.sbuf("ex_nt", [1, 1], I32)
    ft = p.sbuf("ex_ft", [1, 1], I32)
    dum = p.sbuf("ex_dum", [1, 1], F32)
    ctx = {"pt": pt, "nt": nt, "ft": ft, "dum": dum}
    p.dma(pt[:], cid, q="poolq", tag="ld_cid")
    p.dma(nt[:], non, q="poolq", tag="ld_nonce")
    with p.phase("init0"):
        zb = p.sbuf("zb", [1, 1024], BF16)
        p.memset(zb[:], 0.0, eng="pool", tag="zb")
        p.dma(zed, zb[0:1, :], q="poolq", tag="zed")

    def xslice(t, i):
        return t[:, i * T2:(i + 1) * T2]

    for i, I in enumerate("AB"):
        with p.phase("p0p" + I, wire={
                "xs": xslice(xs_in, i), "u": loc["u" + I],
                "qf": loc["qf" + I], "kf": loc["kf" + I],
                "vaug": loc["vaug" + I], "sqv": loc["sqv" + I],
                "qn": loc["qn" + I], "kn": loc["kn" + I],
                "wqT": W["l0p_wqT"], "wkT": W["l0p_wkT"],
                "wvT": W["l0p_wvT"], "wqvT": W["l0p_wqvT"],
                "bq": W["l0p_bq"], "bk": W["l0p_bk"], "bv": W["l0p_bv"],
                "bqv": W["l0p_bqv"], "ws": W["l0p_ws"], "bs": W["l0p_bs"],
                "g1": W["l0p_g1"], "b1": W["l0p_b1"]}):
            build_l0a_pre(T2, p=p)

    def gather_plans(kf_n, vaug_n, kn_n, Kg, VA, KN, vaw):
        pre = {0: [], 1: []}
        post = {0: [], 1: []}
        for s in (0, 1):
            qA, qB = 2 * s, 2 * s + 1
            pA, pB = 2 * (1 - s), 2 * (1 - s) + 1
            for i, (qt, pt_) in enumerate(((qA, pA), (qB, pB))):
                I = "AB"[i]
                pre[s] += [
                    (sh[kf_n + I][s], loc[kf_n + I]),
                    (sh[vaug_n + I][s], loc[vaug_n + I]),
                    (sh[kn_n + I][s], loc[kn_n + I]),
                    (Kg[qt * D:(qt + 1) * D, :], loc[kf_n + I]),
                    (VA[qt * T2:(qt + 1) * T2, :], loc[vaug_n + I]),
                    (KN[:, qt * T2:(qt + 1) * T2], loc[kn_n + I]),
                ]
                post[s] += [
                    (Kg[pt_ * D:(pt_ + 1) * D, :], sh[kf_n + I][1 - s]),
                    (VA[pt_ * T2:(pt_ + 1) * T2, :], sh[vaug_n + I][1 - s]),
                    (KN[:, pt_ * T2:(pt_ + 1) * T2], sh[kn_n + I][1 - s]),
                ]
        reads = [loc[kf_n + I] for I in "AB"] + \
                [loc[vaug_n + I] for I in "AB"] + [loc[kn_n + I] for I in "AB"]
        writes = [Kg[:, :], VA[:, :], KN[:, :]]
        return pre[0], pre[1], post[0], post[1], reads, writes

    _fused_exchange(p, 1, ctx, *gather_plans("kf", "vaug", "kn",
                                             Kg0, VA0, KN0, 528))

    for i, I in enumerate("AB"):
        with p.phase("p0f" + I, wire={
                "qf": loc["qf" + I], "Kg": Kg0, "VA": VA0,
                "qn": loc["qn" + I], "KN": KN0, "sqv": loc["sqv" + I],
                "attnf": loc["attnf" + I]}):
            build_l0a_flash(T2, p=p)

    for i, I in enumerate("AB"):
        with p.phase("p0m" + I, wire={
                "xs": xslice(xs_in, i), "u": loc["u" + I],
                "attnf": loc["attnf" + I], "ys": xslice(ys[0], i),
                "wfT": W["l0m_wfT"], "bf": W["l0m_bf"], "g2": W["l0m_g2"],
                "b2": W["l0m_b2"], "wr1T": W["l0m_wr1T"],
                "br1": W["l0m_br1"], "wr2T": W["l0m_wr2T"],
                "br2": W["l0m_br2"]}):
            build_l0a_mid(T2, p=p)

    for i, I in enumerate("AB"):
        with p.phase("p0s" + I, wire={"xs": xslice(ys[0], i),
                                      "sln": loc["sln" + I]}):
            build_lnsum(T2, p=p)

    pre = {0: [], 1: []}
    post = {0: [], 1: []}
    for s in (0, 1):
        for i, I in enumerate("AB"):
            qt, pt_ = 2 * s + i, 2 * (1 - s) + i
            pre[s] += [(sh["sln" + I][s], loc["sln" + I]),
                       (gq[:, qt * 8:(qt + 1) * 8], loc["sln" + I])]
            post[s] += [(gq[:, pt_ * 8:(pt_ + 1) * 8], sh["sln" + I][1 - s])]
    _fused_exchange(p, 2, ctx, pre[0], pre[1], post[0], post[1],
                    [loc["slnA"], loc["slnB"]], [gq[:, :]])

    for i, I in enumerate("AB"):
        with p.phase("p0c" + I, wire={
                "xs": xslice(ys[0], i), "ys": xslice(ys[1], i), "gsums4": gq,
                **{k: W["l0f_" + k] for k in
                   ("cn_w1", "cn_w2", "cn_w3", "cn_w4", "cn_b1", "cn_b2",
                    "cn_b3", "cn_b4", "gc_w1", "gc_w2", "gc_w3", "gc_b1",
                    "gc_b2", "gc_b3", "ar")}}):
            build_l0f(T2, p=p, fused=True)

    for i, I in enumerate("AB"):
        with p.phase("p1a" + I, wire={
                "xs": xslice(ys[1], i), "xn": loc["xn1" + I],
                "ure": loc["ure" + I], "uim": loc["uim" + I],
                "Fc": loc["Fc" + I], "t0col": W["t0A" if i == 0 else "t0B"],
                "nu_log": W["l1_nu_log"], "theta_log": W["l1_theta_log"],
                "BreT": W["l1_BreT"], "BimT": W["l1_BimT"]}):
            build_l1a_a(T2, p=p)

    pre = {0: [], 1: []}
    post = {0: [], 1: []}
    for s in (0, 1):
        for i, I in enumerate("AB"):
            qt, pt_ = 2 * s + i, 2 * (1 - s) + i
            pre[s] += [(sh["Fc" + I][s], loc["Fc" + I]),
                       (Fq[qt], loc["Fc" + I])]
            post[s] += [(Fq[pt_], sh["Fc" + I][1 - s])]
    _fused_exchange(p, 3, ctx, pre[0], pre[1], post[0], post[1],
                    [loc["FcA"], loc["FcB"]], [f for f in Fq])

    with p.phase("fall"):
        fstage = p.sbuf("fstage", [128, 64], F32)
        fout = p.sbuf("fout", [128, 64], F32)
        for q in range(4):
            p.dma(fstage.rearrange("p (q c n) -> p q c n", q=1, c=8, n=2
                                   )[:, 0, :, :] if False else
                  bass.AP(fstage, q * 16, [[64, 128], [1, 16]]),
                  Fq[q].rearrange("(c p) n -> p c n", p=128), q="spq",
                  tag="f_ld")
        for q in range(4):
            for n in range(2):
                p.copy(bass.AP(fout, 2 * q + n, [[64, 128], [8, 8]]),
                       bass.AP(fstage, q * 16 + n, [[64, 128], [2, 8]]),
                       eng="dve", tag="f_as")
        p.dma(Fall.rearrange("(c p) n -> p c n", p=128),
              bass.AP(fout, 0, [[64, 128], [8, 8], [1, 8]]), q="actq",
              tag="f_st")

    for i, I in enumerate("AB"):
        with p.phase("p1b" + I, wire={
                "xs": xslice(ys[1], i), "xn": loc["xn1" + I],
                "ure": loc["ure" + I], "uim": loc["uim" + I],
                "Fall": Fall, "a4": W["a4A" if i == 0 else "a4B"],
                "t0col": W["t0A" if i == 0 else "t0B"],
                "ys": xslice(ys[2], i),
                "nu_log": W["l1_nu_log"], "theta_log": W["l1_theta_log"],
                "D_skip": W["l1_D_skip"], "CreT": W["l1_CreT"],
                "CimT": W["l1_CimT"]}):
            build_l1a_b(T2, p=p)

    for i, I in enumerate("AB"):
        with p.phase("p1f" + I, wire={
                "xs": xslice(ys[2], i), "ys": xslice(ys[3], i),
                "gate_wT": W["l1f_gate_wT"], "wg": W["l1f_wg"],
                "wu": W["l1f_wu"], "wd": W["l1f_wd"], "swg": W["l1f_swg"],
                "swu": W["l1f_swu"], "swd": W["l1f_swd"]}):
            build_l1f(T2, p=p)

    for i, I in enumerate("AB"):
        with p.phase("p2p" + I, wire={
                "xs": xslice(ys[3], i), "t0col": W["t0A" if i == 0 else "t0B"],
                "kidx": W["l2_kidx"], "alt": W["l2_alt"],
                "pswap": W["l2_pswap"], "wqkvT": W["l2_wqkvT"],
                "bqkv": W["l2_bqkv"], "qf": loc["qf2" + I],
                "kf": loc["kf2" + I], "vf": loc["vf2" + I],
                "vaug": loc["vaug2" + I], "qn": loc["qn2" + I],
                "kn": loc["kn2" + I], "vedge": loc["vedge" + I]}):
            build_l2a_pre(T2, p=p, fused=True)

    pre0, pre1, post0, post1, reads, writes = gather_plans(
        "kf2", "vaug2", "kn2", Kg2, VA2, KN2, 1040)
    for s, (pre_l, post_l) in ((0, (pre0, post0)), (1, (pre1, post1))):
        pre_l += [(sh["vedge" + I][s], loc["vedge" + I]) for I in "AB"]
        pre_l += [(loc["vedrA"], loc["vedgeB"][0:1, :]),
                  (loc["vedlB"], loc["vedgeA"][1:2, :])]
    post0 += [(loc["vedlA"], zed), (loc["vedrB"], sh["vedgeA"][1][0:1, :])]
    post1 += [(loc["vedlA"], sh["vedgeB"][0][1:2, :]), (loc["vedrB"], zed)]
    reads += [loc["vedgeA"], loc["vedgeB"]]
    writes += [loc["vedlA"], loc["vedrA"], loc["vedlB"], loc["vedrB"]]
    _fused_exchange(p, 4, ctx, pre0, pre1, post0, post1, reads, writes)

    for i, I in enumerate("AB"):
        with p.phase("p2f" + I, wire={
                "xs": xslice(ys[3], i), "qf": loc["qf2" + I], "Kg": Kg2,
                "VA": VA2, "qn": loc["qn2" + I], "KN": KN2,
                "vf": loc["vf2" + I], "vedl": loc["vedl" + I],
                "vedr": loc["vedr" + I], "wlep": W["l2_wlep"],
                "blep": W["l2_blep"], "ys": xslice(ys[4], i)}):
            build_l2a_flash(T2, p=p)

    for i, I in enumerate("AB"):
        with p.phase("p2m" + I, wire={
                "xs": xslice(ys[4], i), "ys": xslice(ys[5], i),
                "router_w": W["l2f_router_w"], "router_b": W["l2f_router_b"],
                "w1": W["l2f_w1"], "w2": W["l2f_w2"]}):
            build_l2f(T2, p=p)

    for i, I in enumerate("AB"):
        with p.phase("p3p" + I, wire={
                "xs": xslice(ys[5], i), "kq": loc["kq3" + I],
                "qr": loc["qr3" + I], "kn": loc["kn3" + I],
                "v": loc["v3" + I], "be": loc["be3" + I],
                "uw": loc["uw3" + I], "sm": loc["sm3" + I],
                "winT": W["l3_winT"], "wkvT": W["l3_wkvT"],
                "wbT": W["l3_wbT"], "wqrT": W["l3_wqrT"],
                "wrtT": W["l3_wrtT"], "bbeta": W["l3_bbeta"]}):
            build_l3a_pre(T2, p=p, fused=True)

    pre = {0: [], 1: []}
    post = {0: [], 1: []}
    for s in (0, 1):
        for i, I in enumerate("AB"):
            qt, pt_ = 2 * s + i, 2 * (1 - s) + i
            pre[s] += [(sh["kq3" + I][s], loc["kq3" + I]),
                       (kqF[:, 2 * qt * T2:2 * (qt + 1) * T2],
                        loc["kq3" + I])]
            post[s] += [(kqF[:, 2 * pt_ * T2:2 * (pt_ + 1) * T2],
                         sh["kq3" + I][1 - s])]
            for nm, full in (("v3", vF), ("be3", beF), ("uw3", uwF),
                             ("sm3", smF)):
                pre[s] += [(sh[nm + I][s], loc[nm + I]),
                           (full[:, qt * T2:(qt + 1) * T2], loc[nm + I])]
                post[s] += [(full[:, pt_ * T2:(pt_ + 1) * T2],
                             sh[nm + I][1 - s])]
    _fused_exchange(p, 5, ctx, pre[0], pre[1], post[0], post[1],
                    [loc[nm + I] for nm in ("kq3", "v3", "be3", "uw3", "sm3")
                     for I in "AB"],
                    [kqF[:, :], vF[:, :], beF[:, :], uwF[:, :], smF[:, :]])

    with p.phase("scan", wire={"kqA": kqF, "vA": vF, "beA": beF,
                               "uwA": uwF, "smA": smF, "o32": o32F}):
        build_l3a_scan(SEQ, p=p)

    _fused_sidecopy(
        p, 6, ctx,
        [(loc["o32A"], o32F[:, 0:T2]), (loc["o32B"], o32F[:, T2:2 * T2])],
        [(loc["o32A"], o32F[:, 2 * T2:3 * T2]),
         (loc["o32B"], o32F[:, 3 * T2:4 * T2])],
        [o32F[:, :]], [loc["o32A"], loc["o32B"]])

    for i, I in enumerate("AB"):
        with p.phase("p3o" + I, wire={
                "xs": xslice(ys[5], i), "o32": loc["o32" + I],
                "woT": W["l3_woT"], "wT": W["l3_wT"],
                "ys": xslice(out_ys, i)}):
            build_l3a_post(T2, p=p)

    if debug:
        for k in range(6):
            dbg = p.dram_out("dbg%d" % k, [D, 2 * T2])
            p.dma(dbg, ys[k], q="spq", tag="dbg%d" % k)
    return p


def kernel_fused(debug=False, **ins):
    asc = np.ascontiguousarray
    gi = lambda k: np.asarray(ins[k])
    x = gi("L0_attn_x")
    key = "fused_dbg" if debug else "fused"
    if key not in _PROGS:
        _PROGS[key] = build_fused(debug=debug)
    prog = _PROGS[key]

    perm_k = [blk * 64 + n for blk in range(4) for n in range(32)]
    perm_v = [blk * 64 + 32 + n for blk in range(4) for n in range(32)]
    kidx = ((np.arange(128) % 64) // 2).astype(np.float32)
    alt = np.where(np.arange(128) % 2 == 0, -1.0, 1.0).astype(np.float32)
    pswap = np.zeros((128, 128), np.float32)
    for i in range(128):
        pswap[i ^ 1, i] = 1.0
    wk = {
        "l0p_wqT": asc(gi("L0_attn_Wq").T), "l0p_wkT": asc(gi("L0_attn_Wk").T),
        "l0p_wvT": asc(gi("L0_attn_Wv").T),
        "l0p_wqvT": asc(gi("L0_attn_Wqv").T),
        "l0p_bq": gi("L0_attn_bq"), "l0p_bk": gi("L0_attn_bk"),
        "l0p_bv": gi("L0_attn_bv"), "l0p_bqv": gi("L0_attn_bqv"),
        "l0p_ws": gi("L0_attn_Ws"), "l0p_bs": gi("L0_attn_bs"),
        "l0p_g1": gi("L0_attn_g1"), "l0p_b1": gi("L0_attn_b1"),
        "l0m_wfT": asc(gi("L0_attn_Wf").T), "l0m_bf": gi("L0_attn_bf"),
        "l0m_g2": gi("L0_attn_g2"), "l0m_b2": gi("L0_attn_b2"),
        "l0m_wr1T": asc(gi("L0_attn_Wr1").T), "l0m_br1": gi("L0_attn_br1"),
        "l0m_wr2T": asc(gi("L0_attn_Wr2").T), "l0m_br2": gi("L0_attn_br2"),
        "l1_nu_log": gi("L1_attn_nu_log"),
        "l1_theta_log": gi("L1_attn_theta_log"),
        "l1_BreT": asc(gi("L1_attn_B_re").T),
        "l1_BimT": asc(gi("L1_attn_B_im").T),
        "l1_CreT": asc(gi("L1_attn_C_re").T),
        "l1_CimT": asc(gi("L1_attn_C_im").T),
        "l1_D_skip": gi("L1_attn_D_skip"),
        "l1f_gate_wT": asc(gi("L1_ffn_gate_w").T), "l1f_wg": gi("L1_ffn_w_gate"),
        "l1f_wu": gi("L1_ffn_w_up"), "l1f_wd": gi("L1_ffn_w_down"),
        "l1f_swg": gi("L1_ffn_sw_gate"), "l1f_swu": gi("L1_ffn_sw_up"),
        "l1f_swd": gi("L1_ffn_sw_down"),
        "l2_wqkvT": asc(gi("L2_attn_w_qkv").T), "l2_bqkv": gi("L2_attn_b_qkv"),
        "l2_kidx": kidx, "l2_alt": alt, "l2_pswap": pswap,
        "l2_wlep": gi("L2_attn_w_lepe").reshape(D, 3),
        "l2_blep": gi("L2_attn_b_lepe"),
        "l2f_router_w": gi("L2_ffn_router_w"),
        "l2f_router_b": gi("L2_ffn_router_b"),
        "l2f_w1": gi("L2_ffn_w1"), "l2f_w2": gi("L2_ffn_w2"),
        "l3_winT": asc(gi("L3_attn_W_in").T),
        "l3_wkvT": asc(gi("L3_attn_W_kv")[perm_k + perm_v].T),
        "l3_wbT": asc(gi("L3_attn_W_beta").T),
        "l3_wqrT": asc(np.tile(gi("L3_attn_W_q"), (4, 1)).T),
        "l3_wrtT": asc(gi("L3_attn_W_router").T),
        "l3_bbeta": gi("L3_attn_b_beta").reshape(-1),
        "l3_woT": asc(gi("L3_attn_W_out").T),
        "l3_wT": asc(gi("L3_ffn_weight").T),
    }
    for k in ("l0f_cn_w1", "l0f_cn_w2", "l0f_cn_w3", "l0f_cn_w4",
              "l0f_cn_b1", "l0f_cn_b2", "l0f_cn_b3", "l0f_cn_b4",
              "l0f_gc_w1", "l0f_gc_w2", "l0f_gc_w3",
              "l0f_gc_b1", "l0f_gc_b2", "l0f_gc_b3"):
        wk[k] = gi("L0_ffn_" + k[4:])
    wk["l0f_ar"] = gi("L0_ffn_adaptive_range")

    nonce = int(np.random.randint(1, 2**31 - 1))
    in_maps = []
    for c in range(8):
        s = c % 2
        b = (c // 2) % 2
        qA, qB = 2 * s, 2 * s + 1
        a4A = np.full(4, 1e7, np.float32)
        for p_ in range(qA):
            a4A[p_] = 2048.0 * (qA - 1 - p_)
        a4B = np.full(4, 1e7, np.float32)
        for p_ in range(qB):
            a4B[p_] = 2048.0 * (qB - 1 - p_)
        m = dict(wk)
        m["xs"] = asc(x[b, s * 2 * T2:(s + 1) * 2 * T2, :].T)
        m["cid"] = np.array([[s]], np.int32)
        m["nonce"] = np.array([[nonce]], np.int32)
        m["t0A"] = np.full(128, qA * T2, np.float32)
        m["t0B"] = np.full(128, qB * T2, np.float32)
        m["a4A"] = a4A
        m["a4B"] = a4B
        in_maps.append(m)
    res = prog.run(in_maps)
    out = np.empty((2, SEQ, D), np.float32)
    for c in range(4):
        s, b = c % 2, (c // 2) % 2
        out[b, s * 2 * T2:(s + 1) * 2 * T2, :] = \
            np.asarray(res.results[c]["ys"]).T
    return out, res


def kernel(**ins):
    out, _ = kernel_fused(**ins)
    return np.ascontiguousarray(out)


# version 03
# speedup vs baseline: 1.0484x; 1.0484x over previous
import math
import numpy as np
from contextlib import ExitStack

import concourse.bass as bass
import concourse.mybir as mybir
from concourse.bass_utils import run_bass_kernel_spmd

F32 = mybir.dt.float32
BF16 = mybir.dt.bfloat16
I32 = mybir.dt.int32
AF = mybir.ActivationFunctionType
ALU = mybir.AluOpType
AX = mybir.AxisListType

NCORES = 8
D = 1024
TLOC = 2048
SEQ = 8192
BATCH = 2


COMPUTE_STREAMS = ("pe", "dve", "act", "pool")
DMA_STREAMS = ("spq", "actq", "poolq")
ALL_STREAMS = COMPUTE_STREAMS + DMA_STREAMS
STREAM_ENGINE = {
    "pe": "tensor", "dve": "vector", "act": "scalar", "pool": "gpsimd",
    "spq": "sync", "actq": "scalar", "poolq": "gpsimd",
}


def _ap_range(ap):
    name = ap.tensor.name if hasattr(ap.tensor, "name") else str(ap.tensor)
    esz = mybir.dt.size(ap.dtype)
    off = ap.offset
    if not isinstance(off, (int, np.integer)):
        return name, 0, 1 << 60
    span = 1
    for step, count in ap.ap:
        span += abs(int(step)) * (int(count) - 1)
    return name, int(off) * esz, (int(off) + span) * esz


class Op:
    __slots__ = ("idx", "stream", "fn", "reads", "writes", "waits", "tag",
                 "_count", "phase_name")

    def __init__(self, idx, stream, fn, reads, writes, tag):
        self.idx = idx
        self.stream = stream
        self.fn = fn
        self.reads = reads
        self.writes = writes
        self.waits = {}
        self.tag = tag
        self.phase_name = None


class P:
    def __init__(self, name="prog"):
        self.nc = bass.Bass("TRN2", target_bir_lowering=False)
        self.name = name
        self.ops = []
        self.counts = {s: 0 for s in ALL_STREAMS}
        self.access = {}
        self.es = ExitStack()
        self._sb_names = 0
        self.handles = []
        self.prefix = ""
        self.wire = {}
        self.alloc_es = self.es
        self.floor = {s: 0 for s in ALL_STREAMS}

    def dram_in(self, name, shape, dtype=F32):
        if name in self.wire:
            return self.wire[name]
        return self.nc.dram_tensor(self.prefix + name, list(shape), dtype,
                                   kind="ExternalInput").ap()

    def dram_out(self, name, shape, dtype=F32):
        if name in self.wire:
            return self.wire[name]
        return self.nc.dram_tensor(self.prefix + name, list(shape), dtype,
                                   kind="ExternalOutput").ap()

    def dram_tmp(self, name, shape, dtype=F32, shared=False):
        return self.nc.dram_tensor(
            self.prefix + name, list(shape), dtype,
            addr_space="Shared" if shared else "Local").ap()

    def sbuf(self, name, shape, dtype=F32):
        h = self.alloc_es.enter_context(
            self.nc.sbuf_tensor(self.prefix + name, list(shape), dtype))
        return h

    def psum(self, name, shape, dtype=F32):
        h = self.alloc_es.enter_context(
            self.nc.psum_tensor(self.prefix + name, list(shape), dtype))
        return h

    def phase(self, name, wire=None):
        from contextlib import contextmanager

        @contextmanager
        def _cm():
            old = (self.prefix, self.wire, self.alloc_es)
            self.prefix = name + "_"
            self.wire = wire or {}
            self.alloc_es = ExitStack()
            try:
                yield
            finally:
                self.alloc_es.close()
                self.prefix, self.wire, self.alloc_es = old
                self.floor = {s: self.counts[s] * (16 if s in DMA_STREAMS else 1)
                              for s in ALL_STREAMS}
        return _cm()

    def rec(self, stream, fn, reads=(), writes=(), tag=""):
        idx = len(self.ops)
        op = Op(idx, stream, fn, [], [], tag)
        deps = {}

        def note_dep(pidx):
            if pidx is not None:
                deps[pidx] = True

        for ap in reads:
            trip = _ap_range(ap)
            op.reads.append(trip)
        for ap in writes:
            trip = _ap_range(ap)
            op.writes.append(trip)

        for (name, lo, hi) in op.reads:
            acc = self.access.get(name)
            if acc:
                for (pidx, plo, phi) in acc[0]:
                    if plo < hi and lo < phi:
                        note_dep(pidx)
        for (name, lo, hi) in op.writes:
            acc = self.access.get(name)
            if acc:
                for (pidx, plo, phi) in acc[0]:
                    if plo < hi and lo < phi:
                        note_dep(pidx)
                for (pidx, plo, phi) in acc[1]:
                    if plo < hi and lo < phi:
                        note_dep(pidx)

        for s, v in self.floor.items():
            if v > 0 and v > op.waits.get(s, 0):
                if s != stream or stream in ("dve", "act", "pool"):
                    op.waits[s] = v

        for pidx in deps:
            pop = self.ops[pidx]
            ps = pop.stream
            if ps == stream and (stream == "pe" or stream in DMA_STREAMS):
                continue
            need = self._op_count_after(pidx)
            if ps == stream:
                if need > op.waits.get(ps, 0):
                    op.waits[ps] = need
            else:
                if need > op.waits.get(ps, 0):
                    op.waits[ps] = need

        for (name, lo, hi) in op.reads:
            self.access.setdefault(name, ([], []))[1].append((idx, lo, hi))
        for (name, lo, hi) in op.writes:
            acc = self.access.setdefault(name, ([], []))
            acc[0][:] = [e for e in acc[0] if not (lo <= e[1] and e[2] <= hi)]
            acc[1][:] = [e for e in acc[1] if not (lo <= e[1] and e[2] <= hi)]
            acc[0].append((idx, lo, hi))

        op._count = self.counts[stream] + 1
        self.counts[stream] += 1
        op.phase_name = self.prefix.rstrip("_") or "root"
        self.ops.append(op)
        return op

    def _op_count_after(self, idx):
        op = self.ops[idx]
        inc = 16 if op.stream in DMA_STREAMS else 1
        return op._count * inc

    def dma(self, dst, src, q="spq", tag=""):
        def f(eng, sems):
            eng.dma_start(out=dst, in_=src).then_inc(sems[q], 16)
        return self.rec(q, f, reads=[src], writes=[dst], tag=tag or "dma")

    def tt(self, out, a, b, op=ALU.add, eng="dve", tag=""):
        def f(e, sems):
            e.tensor_tensor(out, a, b, op).then_inc(sems[eng], 1)
        return self.rec(eng, f, reads=[a, b], writes=[out], tag=tag or "tt")

    def ts(self, out, in0, s1, s2=None, op0=ALU.mult, op1=ALU.bypass, eng="dve",
           accum_out=None, tag=""):
        reads = [in0] + [x for x in (s1, s2) if isinstance(x, bass.AP)]
        writes = [out] + ([accum_out] if accum_out is not None else [])
        def f(e, sems):
            e.tensor_scalar(out, in0, s1, s2, op0, op1, accum_out=accum_out
                            ).then_inc(sems[eng], 1)
        return self.rec(eng, f, reads=reads, writes=writes, tag=tag or "ts")

    def stt(self, out, in0, scalar, in1, op0=ALU.mult, op1=ALU.add, eng="dve",
            accum_out=None, tag=""):
        reads = [in0, in1] + ([scalar] if isinstance(scalar, bass.AP) else [])
        writes = [out] + ([accum_out] if accum_out is not None else [])
        def f(e, sems):
            e.scalar_tensor_tensor(out, in0, scalar, in1, op0, op1,
                                   accum_out=accum_out).then_inc(sems[eng], 1)
        return self.rec(eng, f, reads=reads, writes=writes, tag=tag or "stt")

    def copy(self, out, in_, eng="dve", tag=""):
        def f(e, sems):
            if eng == "act":
                e.copy(out, in_).then_inc(sems[eng], 1)
            else:
                e.tensor_copy(out, in_).then_inc(sems[eng], 1)
        return self.rec(eng, f, reads=[in_], writes=[out], tag=tag or "copy")

    def memset(self, ap, val, eng="dve", tag=""):
        def f(e, sems):
            e.memset(ap, val).then_inc(sems[eng], 1)
        return self.rec(eng, f, reads=[], writes=[ap], tag=tag or "memset")

    def red(self, out, in_, op=ALU.add, eng="dve", negate=False, tag=""):
        def f(e, sems):
            e.tensor_reduce(out, in_, AX.X, op, negate=negate
                            ).then_inc(sems[eng], 1)
        return self.rec(eng, f, reads=[in_], writes=[out], tag=tag or "red")

    def act(self, out, in_, func, bias=0.0, scale=1.0, accum_out=None, tag=""):
        reads = [in_] + [x for x in (bias, scale) if isinstance(x, bass.AP)]
        writes = [out] + ([accum_out] if accum_out is not None else [])
        def f(e, sems):
            e.activation(out, in_, func, bias=bias, scale=scale,
                         accum_out=accum_out).then_inc(sems["act"], 1)
        return self.rec("act", f, reads=reads, writes=writes, tag=tag or "act")

    def mm(self, out, lhsT, rhs, start=True, stop=True, tag=""):
        def f(e, sems):
            e.matmul(out, lhsT, rhs, start=start, stop=stop).then_inc(sems["pe"], 1)
        return self.rec("pe", f, reads=[lhsT, rhs], writes=[out], tag=tag or "mm")

    def transpose(self, out, in_, ident, tag=""):
        nc = self.nc
        def f(e, sems):
            if out.dtype != F32:
                with nc.allow_low_precision(reason="bf16 transpose"):
                    e.transpose(out, in_, ident).then_inc(sems["pe"], 1)
            else:
                e.transpose(out, in_, ident).then_inc(sems["pe"], 1)
        return self.rec("pe", f, reads=[in_, ident], writes=[out], tag=tag or "tr")

    def scan(self, out, data0, data1, initial, op0=ALU.mult, op1=ALU.add,
             eng="dve", tag=""):
        reads = [data0, data1] + ([initial] if isinstance(initial, bass.AP) else [])
        def f(e, sems):
            e.tensor_tensor_scan(out, data0, data1, initial, op0, op1
                                 ).then_inc(sems[eng], 1)
        return self.rec(eng, f, reads=reads, writes=[out], tag=tag or "scan")

    def iota(self, out, pattern, base=0, channel_multiplier=0, tag=""):
        def f(e, sems):
            e.iota(out, pattern, base=base, channel_multiplier=channel_multiplier,
                   allow_small_or_imprecise_dtypes=True).then_inc(sems["pool"], 1)
        return self.rec("pool", f, reads=[], writes=[out], tag=tag or "iota")

    def affine_select(self, out, in_, compare_op, fill, pattern, base=0,
                      channel_multiplier=0, tag=""):
        def f(e, sems):
            e.affine_select(out=out, in_=in_, compare_op=compare_op, fill=fill,
                            base=base, pattern=pattern,
                            channel_multiplier=channel_multiplier
                            ).then_inc(sems["pool"], 1)
        return self.rec("pool", f, reads=[in_], writes=[out], tag=tag or "asel")

    def reciprocal(self, out, in_, tag=""):
        def f(e, sems):
            e.reciprocal(out, in_).then_inc(sems["dve"], 1)
        return self.rec("dve", f, reads=[in_], writes=[out], tag=tag or "recip")

    def top8(self, out, in_, tag=""):
        def f(e, sems):
            e.max(out, in_).then_inc(sems["dve"], 1)
        return self.rec("dve", f, reads=[in_], writes=[out], tag=tag or "top8")

    def build(self):
        nc = self.nc
        sems = {}
        for s in ALL_STREAMS:
            sems[s] = self.es.enter_context(nc.semaphore("sem_" + s))
        block = self.es.enter_context(nc.Block())

        per_stream = {s: [] for s in ALL_STREAMS}
        for op in self.ops:
            per_stream[op.stream].append(op)

        final = {s: self.counts[s] * (16 if s in DMA_STREAMS else 1)
                 for s in ALL_STREAMS}

        def make_section(streams_here):
            def emit(eng):
                waited = {s: 0 for s in ALL_STREAMS}
                oplist = []
                for s in streams_here:
                    oplist.extend(per_stream[s])
                oplist.sort(key=lambda o: o.idx)
                cur = [None, None]

                def close_scope():
                    if cur[0] is not None:
                        nc.leave_named_scope(cur[0], cur[1], False)
                        cur[0] = None

                for op in oplist:
                    ph = getattr(op, "phase_name", None) or "root"
                    if ph != cur[0]:
                        close_scope()
                        sid, _ = nc.enter_named_scope(ph, False)
                        cur[0], cur[1] = ph, sid
                    for s, v in op.waits.items():
                        if v > waited[s]:
                            eng.wait_ge(sems[s], v)
                            waited[s] = v
                    op.fn(eng, sems)
                close_scope()
                return waited
            return emit

        emitted_sections = {}

        @block.tensor
        def _(eng):
            emitted_sections["pe"] = make_section(["pe"])(eng)

        @block.vector
        def _(eng):
            emitted_sections["dve"] = make_section(["dve"])(eng)

        @block.scalar
        def _(eng):
            w = make_section(["act", "actq"])(eng)
            emitted_sections["act"] = w

        @block.gpsimd
        def _(eng):
            w = make_section(["pool", "poolq"])(eng)
            for s in ALL_STREAMS:
                if final[s] > w.get(s, 0):
                    eng.wait_ge(sems[s], final[s])
            emitted_sections["pool"] = w

        @block.sync
        def _(eng):
            emitted_sections["spq"] = make_section(["spq"])(eng)

        return nc

    def run(self, in_maps, core_ids=None, **kw):
        if getattr(self, "_built_nc", None) is None:
            self._built_nc = self.build()
            with self.es:
                pass
        res = run_bass_kernel_spmd(self._built_nc, in_maps,
                                   core_ids=core_ids or list(range(NCORES)),
                                   **kw)
        return res


NCH = D // 128


def ln_stats_fm(p, x_sb, T, ones_col, eps, pref, sq_scratch, psum_stats,
                mean_bc, rstd_bc, bcast_psum, ones_row):
    nt = T // 512
    srow = p.sbuf(pref + "srow", [1, 2048], F32)
    epsc = p.sbuf(pref + "eps", [1, 1], F32)
    p.memset(epsc[:], eps, eng="pool")
    for it in range(nt):
        t0, t1 = it * 512, (it + 1) * 512
        for c in range(NCH):
            p.mm(psum_stats[0:1, 0:512], ones_col[:, 0:1],
                 x_sb[:, c * T + t0: c * T + t1],
                 start=(c == 0), stop=(c == NCH - 1), tag=pref + "sum")
        for c in range(NCH):
            sl = sq_scratch[:, (c % 2) * 512:(c % 2) * 512 + 512]
            p.tt(sl, x_sb[:, c * T + t0: c * T + t1],
                 x_sb[:, c * T + t0: c * T + t1], op=ALU.mult, eng="pool",
                  tag=pref + "sq")
            p.mm(psum_stats[0:1, 512:1024], ones_col[:, 0:1], sl,
                 start=(c == 0), stop=(c == NCH - 1), tag=pref + "sumsq")
        mean_r = srow[0:1, 0:512]
        ms_r = srow[0:1, 512:1024]
        var_r = srow[0:1, 1024:1536]
        rstd_r = srow[0:1, 1536:2048]
        p.act(mean_r, psum_stats[0:1, 0:512], AF.Copy, scale=1.0 / D,
              tag=pref + "mean")
        p.tt(ms_r, mean_r, mean_r, op=ALU.mult,
              tag=pref + "ms")
        p.stt(var_r, psum_stats[0:1, 512:1024], 1.0 / D, ms_r,
              op0=ALU.mult, op1=ALU.subtract, tag=pref + "var")
        p.act(var_r, var_r, AF.Sqrt, bias=epsc[:], tag=pref + "sqrt")
        p.reciprocal(rstd_r, var_r, tag=pref + "recip")
        p.mm(bcast_psum[:, 0:512], ones_row[0:1, :], mean_r, tag=pref + "bm")
        p.copy(mean_bc[:, t0:t1], bcast_psum[:, 0:512], eng="dve",
               tag=pref + "bmc")
        p.mm(bcast_psum[:, 512:1024], ones_row[0:1, :], rstd_r, tag=pref + "br")
        p.copy(rstd_bc[:, t0:t1], bcast_psum[:, 512:1024], eng="dve",
               tag=pref + "brc")


def ln_apply_fm(p, x_sb, out_sb, T, mean_bc, rstd_bc, pref, scratch,
                gain_col=None, bias_col=None, out2_sb=None):
    for c in range(NCH):
        xs = x_sb[:, c * T:(c + 1) * T]
        sc = scratch[:, 0:T]
        p.tt(sc, xs, mean_bc[:, 0:T], op=ALU.subtract, tag=pref + "sub")
        dst = out_sb[:, c * T:(c + 1) * T]
        fdst = out2_sb[:, c * T:(c + 1) * T] if out2_sb is not None else None
        if gain_col is None:
            p.tt(fdst if fdst is not None else dst, sc, rstd_bc[:, 0:T],
                 op=ALU.mult, tag=pref + "mul")
        else:
            p.tt(sc, sc, rstd_bc[:, 0:T], op=ALU.mult, tag=pref + "mul")
            p.stt(fdst if fdst is not None else dst, sc, gain_col[c],
                  bias_col[c].broadcast_to([128, T]),
                  op0=ALU.mult, op1=ALU.add, tag=pref + "aff")
        if fdst is not None:
            p.copy(dst, fdst, eng="pool", tag=pref + "cp2")


def ln_streaming(p, cm, T, x_tile, emit_out, pref, it_done=None):
    nt = T // 512
    srow = cm.srow
    for it in range(nt):
        for c in range(NCH):
            xs = x_tile(c, it)
            p.mm(cm.psum_stats[0:1, 0:512], cm.ones_col[:, 0:1], xs,
                 start=(c == 0), stop=(c == NCH - 1), tag=pref + "sum")
            sl = cm.sq_scratch[:, 0:512]
            p.tt(sl, xs, xs, op=ALU.mult, eng="pool", tag=pref + "sq")
            p.mm(cm.psum_stats[0:1, 512:1024], cm.ones_col[:, 0:1], sl,
                 start=(c == 0), stop=(c == NCH - 1), tag=pref + "sumsq")
        mean_r = srow[0:1, 0:512]
        ms_r = srow[0:1, 512:1024]
        var_r = srow[0:1, 1024:1536]
        p.act(mean_r, cm.psum_stats[0:1, 0:512], AF.Copy, scale=1.0 / D,
              tag=pref + "mean")
        p.tt(ms_r, mean_r, mean_r, op=ALU.mult,
              tag=pref + "ms")
        p.stt(var_r, cm.psum_stats[0:1, 512:1024], 1.0 / D, ms_r,
              op0=ALU.mult, op1=ALU.subtract, tag=pref + "var")
        p.act(var_r, var_r, AF.Sqrt, bias=cm.epsc[0:1, 0:1], tag=pref + "sqrt")
        p.reciprocal(srow[0:1, 1536:2048], var_r, tag=pref + "recip")
        p.mm(cm.bcast_psum[:, 0:512], cm.ones_row[0:1, :], mean_r,
             tag=pref + "bm")
        p.copy(cm.mean_b1[:, 0:512], cm.bcast_psum[:, 0:512], eng="dve",
               tag=pref + "bmc")
        p.mm(cm.bcast_psum[:, 512:1024], cm.ones_row[0:1, :],
             srow[0:1, 1536:2048], tag=pref + "br")
        p.copy(cm.rstd_b1[:, 0:512], cm.bcast_psum[:, 512:1024], eng="dve",
               tag=pref + "brc")
        for c in range(NCH):
            xs = x_tile(c, it)
            sc = cm.scratch[:, 0:512]
            p.tt(sc, xs, cm.mean_b1[:, 0:512], op=ALU.subtract,
                 tag=pref + "asub")
            p.tt(sc, sc, cm.rstd_b1[:, 0:512], op=ALU.mult, tag=pref + "amul")
            emit_out(c, it, sc)
        if it_done is not None:
            it_done(it)


def load_wT_bf16(p, w_dram, d_in, d_out, name, q="poolq", region=None):
    nin = d_in // 128
    h = region if region is not None else p.sbuf(
        name + "_sb", [128, nin * d_out], BF16)
    hap = h if region is not None else h[:, 0:nin * d_out]
    if d_out == 1:
        for c in range(nin):
            p.dma(hap[:, c:c + 1], w_dram[c * 128:(c + 1) * 128, :], q=q,
                  tag=name + "_ld")
    else:
        p.dma(hap.rearrange("p (c n) -> p c n", n=d_out),
              w_dram.rearrange("(c p) n -> p c n", p=128), q=q,
              tag=name + "_ld")
    return [hap[:, c * d_out:(c + 1) * d_out] for c in range(nin)]


def load_wT_f32(p, w_dram, d_in, d_out, name, q="spq"):
    nin = d_in // 128
    h = p.sbuf(name + "_sb", [128, nin * d_out], F32)
    hap = h[:, 0:nin * d_out]
    if d_out == 1:
        for c in range(nin):
            p.dma(hap[:, c:c + 1], w_dram[c * 128:(c + 1) * 128, :], q=q,
                  tag=name + "_ld")
    else:
        p.dma(hap.rearrange("p (c n) -> p c n", n=d_out),
              w_dram.rearrange("(c p) n -> p c n", p=128), q=q,
              tag=name + "_ld")
    return [hap[:, c * d_out:(c + 1) * d_out] for c in range(nin)]


def load_col_f32(p, v_dram, n, name, q="spq"):
    h = p.sbuf(name + "_sb", [128, n // 128], F32)
    cols = []
    for c in range(n // 128):
        sl = h[:, c:c + 1]
        p.dma(sl, v_dram[c * 128:(c + 1) * 128].rearrange("(a b) -> a b", b=1), q=q,
              tag=name + "_ld")
        cols.append(sl)
    return cols


def matmul_fm(p, wT_tiles, x_bf, d_in, d_out, T, psum_pool, evac, pref,
              x_cols=None):
    nin = d_in // 128
    nt = T // 512
    nfo = d_out // 128
    for fo in range(nfo):
        for it in range(nt):
            ps = psum_pool[(fo * nt + it) % len(psum_pool)][:, 0:512]
            for c in range(nin):
                xs = (x_cols[c][:, it * 512:(it + 1) * 512] if x_cols is not None
                      else x_bf[:, c * T + it * 512: c * T + (it + 1) * 512])
                p.mm(ps, wT_tiles[c][:, fo * 128:(fo + 1) * 128], xs,
                     start=(c == 0), stop=(c == nin - 1), tag=pref + "mm")
            evac(fo, it, ps)


def make_identity(p, name, dtype):
    ident = p.sbuf(name, [128, 128], dtype)
    idc = p.sbuf(name + "_c", [128, 1], F32)
    p.iota(ident[:], [[1, 128]], base=0, channel_multiplier=0, tag=name + "i")
    p.iota(idc[:], [[0, 1]], base=0, channel_multiplier=1, tag=name + "p")
    p.ts(ident[:], ident[:], idc[:, 0:1], None, op0=ALU.is_equal, tag=name)
    return ident


class Arena:
    def __init__(self, p, name, cols):
        self.cols = cols
        self.t = p.sbuf(name, [128, cols], F32)

    def f32(self, c0, cols):
        assert c0 + cols <= self.cols, (c0, cols, self.cols)
        return self.t[:, c0:c0 + cols]

    def bf16(self, c0, cols_bf16):
        assert c0 + (cols_bf16 + 1) // 2 <= self.cols, (c0, cols_bf16)
        return self.t[:, c0:c0 + (cols_bf16 + 1) // 2].bitcast(BF16)


class Common:
    def __init__(self, p, T=TLOC, slim=False):
        self.p = p
        self.T = T
        self.ones_col = p.sbuf("cmn_ones_col", [128, 1], F32)
        p.memset(self.ones_col[:], 1.0, eng="pool")
        self.ones_row = p.sbuf("cmn_ones_row", [1, 128], F32)
        p.memset(self.ones_row[:], 1.0, eng="pool")
        self.psum_stats = p.psum("cmn_pstats", [2, 1024], F32)
        self.bcast_psum = p.psum("cmn_pbcast", [128, 1024], F32)
        if slim:
            self.sq_scratch = p.sbuf("cmn_sq", [128, 512], F32)
            self.scratch = p.sbuf("cmn_scr", [128, 1024], F32)
            self.mean_b1 = p.sbuf("cmn_mb1", [128, 512], F32)[:, 0:512]
            self.rstd_b1 = p.sbuf("cmn_rb1", [128, 512], F32)[:, 0:512]
        else:
            self.sq_scratch = p.sbuf("cmn_sq", [128, 1024], F32)
            self.mean_bc = p.sbuf("cmn_meanbc", [128, T], F32)
            self.rstd_bc = p.sbuf("cmn_rstdbc", [128, T], F32)
            self.scratch = p.sbuf("cmn_scr", [128, T], F32)
            self.mean_b1 = self.mean_bc[:, 0:512]
            self.rstd_b1 = self.rstd_bc[:, 0:512]
        self.srow = p.sbuf("cmn_srow", [1, 2048], F32)
        self.epsc = p.sbuf("cmn_eps", [1, 1], F32)
        p.memset(self.epsc[:], 1e-5, eng="pool")

    def ln(self, x_sb, out_sb, pref, gain_col=None, bias_col=None, out2_sb=None):
        p = self.p
        ln_stats_fm(p, x_sb, self.T, self.ones_col, 1e-5, pref, self.sq_scratch,
                    self.psum_stats, self.mean_bc, self.rstd_bc,
                    self.bcast_psum, self.ones_row)
        ln_apply_fm(p, x_sb, out_sb, self.T, self.mean_bc, self.rstd_bc, pref,
                    self.scratch, gain_col, bias_col, out2_sb)


def load_stream(p, T, name="xs"):
    x_dram = p.dram_in(name, [D, T])
    xsb = p.sbuf("stream", [128, NCH * T], F32)
    for c in range(NCH):
        p.dma(xsb[:, c * T:(c + 1) * T], x_dram[c * 128:(c + 1) * 128, :],
              q="spq", tag="ldstream")
    return xsb


def store_stream(p, xsb, T, name="ys"):
    y_dram = p.dram_out(name, [D, T])
    for c in range(NCH):
        p.dma(y_dram[c * 128:(c + 1) * 128, :], xsb[:, c * T:(c + 1) * T],
              q="actq", tag="ststream")



def build_l3f(T=TLOC, p=None):
    p = p or P("l3f")
    cm = Common(p, T)
    xsb = load_stream(p, T)
    wT = p.dram_in("wT", [D, D])
    wf = p.sbuf("wf", [128, NCH * D], F32)
    wsgn = p.sbuf("wsgn", [128, NCH * D], BF16)
    wtiles = []
    for c in range(NCH):
        sl_f = wf[:, c * D:(c + 1) * D]
        sl_s = wsgn[:, c * D:(c + 1) * D]
        p.dma(sl_f, wT[c * 128:(c + 1) * 128, :], q="poolq", tag="ldw")
        p.ts(sl_s, sl_f, 0.0, None, op0=ALU.is_ge, tag="sign")
        p.ts(sl_s, sl_s, 2.0, -1.0, op0=ALU.mult, op1=ALU.add, tag="sign2")
        wtiles.append(sl_s)
    xn_bf = p.sbuf("xn_bf", [128, NCH * T], BF16)
    cm.ln(xsb, xn_bf, "ln_")
    pp = [p.psum("mmp%d" % i, [128, 512], F32) for i in range(4)]

    def evac(fo, it, ps):
        xs = xsb[:, fo * T + it * 512: fo * T + (it + 1) * 512]
        p.tt(xs, ps, xs, op=ALU.add, tag="resid")

    matmul_fm(p, wtiles, xn_bf, D, D, T, pp, evac, "l3f_")
    store_stream(p, xsb, T)
    return p


def load_row_f32(p, v_dram, n, name, q="spq"):
    h = p.sbuf(name + "_sb", [1, n], F32)
    p.dma(h[0:1, :], v_dram.rearrange("(a b) -> a b", a=1), q=q, tag=name + "_ld")
    return h


def bcast_row(p, row_ap, out_sb, T, bcast_psum, ones_row, pref):
    for it in range(T // 512):
        ps = bcast_psum[:, (it % 2) * 512:(it % 2) * 512 + 512]
        p.mm(ps, ones_row[0:1, :], row_ap[0:1, it * 512:(it + 1) * 512],
             tag=pref + "bc")
        p.copy(out_sb[:, it * 512:(it + 1) * 512], ps, eng="dve",
               tag=pref + "bcc")


def small_vec_mm(p, cm, x_cols, w_dram_bf, k, m, name, bias_row=None,
                 func=None, psum=None):
    out = p.sbuf(name + "_o", [1, m], F32)
    nk = k // 128
    for t0 in range(0, m, 512):
        t1 = min(t0 + 512, m)
        ps = psum[0:1, 0:t1 - t0]
        for c in range(nk):
            p.mm(ps, x_cols[c], w_dram_bf[c][:, t0:t1],
                 start=(c == 0), stop=(c == nk - 1), tag=name + "_mm")
        p.copy(out[0:1, t0:t1], ps, eng="dve", tag=name + "_ev")
    if bias_row is not None:
        p.tt(out[0:1, :], out[0:1, :], bias_row[0:1, :m], op=ALU.add,
             tag=name + "_b")
    if func is not None:
        p.act(out[0:1, :], out[0:1, :], func, tag=name + "_f")
    return out


def row_to_cols(p, row_ap, n, identf, name, psum):
    h = p.sbuf(name + "_cols", [128, n // 128], F32)
    cols = []
    for c in range(n // 128):
        ps = psum[:, c % 4:(c % 4) + 1]
        p.transpose(ps, row_ap[0:1, c * 128:(c + 1) * 128], identf[0:1, 0:1],
                    tag=name + "_tr")
        p.copy(h[:, c:c + 1], ps, eng="act", tag=name + "_cp")
        cols.append(h[:, c:c + 1])
    return cols


def build_l0f(T=TLOC, p=None, fused=False):
    p = p or P("l0f")
    cm = Common(p, T)
    identf = make_identity(p, "identf", F32)
    xs_d = p.dram_in("xs", [D, T])
    ys_d = p.dram_out("ys", [D, T])

    cn_w1 = load_wT_bf16(p, p.dram_in("cn_w1", [1024, 1024]), 1024, 1024, "cnw1")
    cn_w2 = load_wT_bf16(p, p.dram_in("cn_w2", [1024, 512]), 1024, 512, "cnw2")
    cn_w3 = load_wT_bf16(p, p.dram_in("cn_w3", [512, 256]), 512, 256, "cnw3")
    cn_w4 = load_wT_bf16(p, p.dram_in("cn_w4", [256, 1]), 256, 1, "cnw4")
    gc_w1 = load_wT_f32(p, p.dram_in("gc_w1", [1024, 512]), 1024, 512, "gcw1")
    gc_w2 = load_wT_f32(p, p.dram_in("gc_w2", [512, 256]), 512, 256, "gcw2")
    gc_w3 = load_wT_f32(p, p.dram_in("gc_w3", [256, 1]), 256, 1, "gcw3")
    cn_b1 = load_col_f32(p, p.dram_in("cn_b1", [1024]), 1024, "cnb1")
    cn_b2 = load_col_f32(p, p.dram_in("cn_b2", [512]), 512, "cnb2")
    cn_b3 = load_col_f32(p, p.dram_in("cn_b3", [256]), 256, "cnb3")
    cn_b4 = load_row_f32(p, p.dram_in("cn_b4", [1]), 1, "cnb4")
    gc_b1 = load_row_f32(p, p.dram_in("gc_b1", [512]), 512, "gcb1")
    gc_b2 = load_row_f32(p, p.dram_in("gc_b2", [256]), 256, "gcb2")
    gc_b3 = load_row_f32(p, p.dram_in("gc_b3", [1]), 1, "gcb3")
    ar_r = load_row_f32(p, p.dram_in("ar", [1]), 1, "ar")
    if fused:
        gsums4 = p.dram_in("gsums4", [128, 32])
    else:
        gsums = p.dram_in("gsums", [D, 4])

    out = p.sbuf("outbuf", [128, NCH * T], F32)
    xbuf = p.sbuf("xbuf", [128, 1024], F32)
    xslot = [0]

    def x_tile(c, it):
        s = xslot[0] % 2
        xslot[0] += 1
        sl = xbuf[:, s * 512:(s + 1) * 512]
        p.dma(sl, xs_d[c * 128:(c + 1) * 128, it * 512:(it + 1) * 512],
              q="spq", tag="xt")
        return sl

    def emit_out(c, it, sc):
        p.copy(out[:, c * T + it * 512:c * T + (it + 1) * 512], sc,
               eng="pool", tag="lnout")

    ln_streaming(p, cm, T, x_tile, emit_out, "ln_")

    pp = [p.psum("mmp%d" % i, [128, 512], F32) for i in range(3)]
    spsum = p.psum("spsum", [128, 4], F32)

    gs = p.sbuf("gs", [128, NCH], F32)
    gctx_cols = []
    if fused:
        gall = p.sbuf("gall", [128, 32], F32)
        p.dma(gall[:], gsums4, q="spq", tag="gsum_ld")
    for c in range(NCH):
        if fused:
            p.red(gs[:, c:c + 1], bass.AP(gall, c, [[32, 128], [8, 4]]),
                  op=ALU.add, tag="gsum_red")
        else:
            gsl = p.sbuf("gsl%d" % c, [128, 4], F32)
            p.dma(gsl[:], gsums[c * 128:(c + 1) * 128, :], q="spq",
                  tag="gsum_ld")
            p.red(gs[:, c:c + 1], gsl[:], op=ALU.add, tag="gsum_red")
        p.ts(gs[:, c:c + 1], gs[:, c:c + 1], 1.0 / SEQ, None, op0=ALU.mult,
             tag="gsum_sc")
        gctx_cols.append(gs[:, c:c + 1])
    h1 = small_vec_mm(p, cm, gctx_cols, gc_w1, 1024, 512, "gf1",
                      bias_row=gc_b1, func=AF.Gelu, psum=cm.psum_stats)
    h1c = row_to_cols(p, h1[0:1, :], 512, identf, "gf1c", spsum)
    h2 = small_vec_mm(p, cm, h1c, gc_w2, 512, 256, "gf2",
                      bias_row=gc_b2, func=AF.Gelu, psum=cm.psum_stats)
    h2c = row_to_cols(p, h2[0:1, :], 256, identf, "gf2c", spsum)
    gf = small_vec_mm(p, cm, h2c, gc_w3, 256, 1, "gf3",
                      bias_row=gc_b3, func=AF.Sigmoid, psum=cm.psum_stats)

    s0 = p.sbuf("s0", [1, 1], F32)
    s1 = p.sbuf("s1", [1, 1], F32)
    p.tt(s0[0:1, :], gf[0:1, :], ar_r[0:1, :], op=ALU.mult, tag="c0")
    p.ts(s0[0:1, :], s0[0:1, :], 1.4, None, op0=ALU.mult, tag="c1")
    p.stt(s0[0:1, :], ar_r[0:1, :], -1.0, s0[0:1, :], op0=ALU.mult,
          op1=ALU.add, tag="c2")
    p.ts(s0[0:1, :], s0[0:1, :], 0.1, None, op0=ALU.add, tag="c3")
    p.ts(s1[0:1, :], ar_r[0:1, :], 0.6, None, op0=ALU.mult, tag="c4")

    h1t = p.sbuf("cyc_h1", [128, NCH * 512], BF16)
    h2t = p.sbuf("cyc_h2", [128, 4 * 512], BF16)
    h3t = p.sbuf("cyc_h3", [128, 2 * 512], BF16)
    xn_bf_t = p.sbuf("xnbf_t", [128, NCH * 512], BF16)
    tf_row = p.sbuf("tf_row", [1, 512], F32)
    coup_t = cm.mean_bc[:, 512:1024]
    src_map = [4, 5, 6, 7, 0, 1, 2, 3]
    sgn = [1, 1, -1, -1, -1, -1, 1, 1]

    nt = T // 512
    for cyc in range(3):
        for it in range(nt):
            t0, t1 = it * 512, (it + 1) * 512
            for c in range(NCH):
                p.copy(xn_bf_t[:, c * 512:(c + 1) * 512],
                       out[:, c * T + t0:c * T + t1], eng="pool",
                       tag="cybf%d" % cyc)
            def ev(bias_cols, dst):
                def f(fo, it2, ps):
                    p.act(dst[:, fo * 512:(fo + 1) * 512], ps, AF.Gelu,
                          bias=bias_cols[fo], tag="cyev")
                return f
            matmul_fm(p, cn_w1, xn_bf_t, 1024, 1024, 512, pp,
                      ev(cn_b1, h1t), "cy1_%d_%d_" % (cyc, it))
            matmul_fm(p, cn_w2, h1t, 1024, 512, 512, pp,
                      ev(cn_b2, h2t), "cy2_%d_%d_" % (cyc, it))
            matmul_fm(p, cn_w3, h2t, 512, 256, 512, pp,
                      ev(cn_b3, h3t), "cy3_%d_%d_" % (cyc, it))
            ps = cm.psum_stats[0:1, 0:512]
            for c in range(2):
                p.mm(ps, cn_w4[c][:, 0:1], h3t[:, c * 512:(c + 1) * 512],
                     start=(c == 0), stop=(c == 1), tag="cy4")
            p.act(tf_row[0:1, 0:512], ps, AF.Sigmoid, bias=cn_b4[0:1, 0:1],
                  tag="cy4s")
            p.ts(tf_row[0:1, 0:512], tf_row[0:1, 0:512], s1[0:1, 0:1],
                 s0[0:1, 0:1], op0=ALU.mult, op1=ALU.add, tag="coup")
            p.mm(cm.bcast_psum[:, 0:512], cm.ones_row[0:1, :],
                 tf_row[0:1, 0:512], tag="coupb")
            p.copy(coup_t, cm.bcast_psum[:, 0:512], eng="dve", tag="coupbc")
            for a in range(4):
                b = src_map[a]
                tmp = cm.scratch[:, 0:512]
                twv = cm.scratch[:, 512:1024]
                p.copy(tmp, out[:, a * T + t0:a * T + t1], eng="pool",
                       tag="twsv")
                p.tt(twv, coup_t, out[:, b * T + t0:b * T + t1],
                     op=ALU.mult, tag="twm")
                p.tt(out[:, a * T + t0:a * T + t1],
                     out[:, a * T + t0:a * T + t1], twv,
                     op=ALU.add if sgn[a] > 0 else ALU.subtract, tag="twu")
                p.tt(twv, coup_t, tmp, op=ALU.mult, tag="twm2")
                p.tt(out[:, b * T + t0:b * T + t1],
                     out[:, b * T + t0:b * T + t1], twv,
                     op=ALU.add if sgn[b] > 0 else ALU.subtract, tag="twu2")

    rbuf = p.sbuf("rbuf", [128, 1024], F32)
    for c in range(NCH):
        for it in range(nt):
            t0, t1 = it * 512, (it + 1) * 512
            xt = x_tile(c, it)
            rb = rbuf[:, (it % 2) * 512:(it % 2) * 512 + 512]
            p.tt(rb, xt, out[:, c * T + t0:c * T + t1], op=ALU.add,
                 tag="resid")
            p.dma(ys_d[c * 128:(c + 1) * 128, t0:t1], rb, q="actq",
                  tag="stres")
    return p



def build_l1f(T=TLOC, p=None):
    p = p or P("l1f")
    cm = Common(p, T, slim=True)
    xsb = load_stream(p, T)
    ident = make_identity(p, "identb", F32)

    gate_wTf = load_wT_f32(p, p.dram_in("gate_wT", [1024, 8]), 1024, 8, "gwt")
    wg_d = p.dram_in("wg", [8, 1024, 1408])
    wu_d = p.dram_in("wu", [8, 1024, 1408])
    wd_d = p.dram_in("wd", [8, 1408, 1024])
    swg_d = p.dram_in("swg", [1024, 2816])
    swu_d = p.dram_in("swu", [1024, 2816])
    swd_d = p.dram_in("swd", [2816, 1024])

    A = Arena(p, "A", 27904)
    xn_bf = A.bf16(0, NCH * T)
    EB = 8192
    wg_r = A.bf16(EB, 8 * 1408)
    wu_r = A.bf16(13824, 8 * 1408)
    wd_r = A.bf16(19456, 11 * 1024)
    gt_r = A.bf16(25088, 11 * 512)
    su_h = p.sbuf("su_t", [128, 512], BF16)
    su_r = su_h[:, 0:512]
    swd_r = A.bf16(EB, 22 * 1024)
    gsw_r = A.bf16(19456, 22 * 512)
    wjit_r = A.bf16(25088, 2 * 2816)

    pp = [p.psum("mmp%d" % i, [128, 512], F32) for i in range(3)]
    rlog = p.psum("rlog", [128, 32], F32)

    nsub = T // 128
    scores = p.sbuf("scores", [128, 8], F32)
    srtmp = p.sbuf("srtmp", [128, 8], F32)
    m8 = p.sbuf("m8", [128, 8], F32)
    mcol = p.sbuf("mcol", [128, 4], F32)
    comb_tm = p.sbuf("comb_tm", [128, 8], F32)
    comb8 = p.sbuf("comb8", [8, T], F32)
    cps = pp[1]
    rlog_sb = p.sbuf("rlog_sb", [128, 32], F32)

    def emit_xn(c, it, sc):
        p.copy(xn_bf[:, c * T + it * 512: c * T + (it + 1) * 512], sc,
               eng="pool", tag="xnbf")
        if c == 0:
            p.memset(rlog_sb[:], 0.0, eng="dve", tag="rt_z")
        for sub in range(4):
            p.mm(rlog[:, sub * 8:(sub + 1) * 8],
                 sc[:, sub * 128:(sub + 1) * 128], gate_wTf[c][:, 0:8],
                 start=True, stop=True, tag="rt_mm")
        p.tt(rlog_sb[:], rlog_sb[:], rlog[:, 0:32], op=ALU.add, tag="rt_acc")

    def router_it(it):
        pass

    def x_tile(c, it):
        return xsb[:, c * T + it * 512: c * T + (it + 1) * 512]

    def do_router(s):
        rsl = rlog_sb[:, (s % 4) * 8:(s % 4) * 8 + 8]
        p.red(mcol[:, 0:1], rsl, op=ALU.max, eng="dve", negate=True,
              tag="rt_max")
        p.act(scores[:, 0:8], rsl, AF.Exp, bias=mcol[:, 0:1],
              tag="rt_exp")
        p.red(mcol[:, 1:2], scores[:, 0:8], op=ALU.add, tag="rt_sum")
        p.reciprocal(mcol[:, 2:3], mcol[:, 1:2], tag="rt_rcp")
        p.ts(scores[:, 0:8], scores[:, 0:8], mcol[:, 2:3], None,
             op0=ALU.mult, tag="rt_nrm")
        p.top8(m8[:, 0:8], scores[:, 0:8], tag="rt_top")
        p.tt(mcol[:, 0:1], m8[:, 0:1], m8[:, 1:2], op=ALU.add, tag="rt_ws")
        p.ts(mcol[:, 0:1], mcol[:, 0:1], 1e-20, None, op0=ALU.add, tag="rt_eps")
        p.reciprocal(mcol[:, 1:2], mcol[:, 0:1], tag="rt_rcp2")
        p.ts(mcol[:, 1:2], mcol[:, 1:2], 2.5, None, op0=ALU.mult, tag="rt_sc")
        p.ts(srtmp[:, 0:8], scores[:, 0:8], m8[:, 1:2], None, op0=ALU.is_ge,
             tag="rt_msk")
        p.tt(comb_tm[:, 0:8], scores[:, 0:8], srtmp[:, 0:8], op=ALU.mult,
             tag="rt_cw")
        p.ts(comb_tm[:, 0:8], comb_tm[:, 0:8], mcol[:, 1:2], None,
             op0=ALU.mult, tag="rt_cw2")
        p.transpose(cps[0:8, 0:128], comb_tm[:, 0:8], ident[:], tag="rt_tr")
        p.copy(comb8[:, s * 128:(s + 1) * 128], cps[0:8, 0:128], eng="dve",
               tag="rt_cp")

    def router_it2(it):
        router_it(it)
        for sub in range(4):
            do_router(it * 4 + sub)

    ln_streaming(p, cm, T, x_tile, emit_xn, "ln_", it_done=router_it2)

    sel = p.sbuf("sel", [8, 8 * 128], F32)
    selp = p.sbuf("selp", [8, 1], F32)
    p.iota(selp[0:8, :], [[0, 1]], base=0, channel_multiplier=1, tag="selp")
    for e in range(8):
        p.ts(sel[0:8, e * 128:(e + 1) * 128],
             bass.AP(selp, 0, [[1, 8], [0, 128]]), float(e), None,
             op0=ALU.is_equal, tag="sel%d" % e)

    comb_sb = p.sbuf("comb_sb", [128, 512], F32)
    nt = T // 512

    for e in range(8):
        wg = load_wT_bf16(p, wg_d[e], 1024, 1408, "wg%d" % e, region=wg_r)
        wu = load_wT_bf16(p, wu_d[e], 1024, 1408, "wu%d" % e, region=wu_r)
        wd = load_wT_bf16(p, wd_d[e], 1408, 1024, "wd%d" % e, region=wd_r)
        for it in range(nt):
            t0 = it * 512
            for fo in range(11):
                psA = pp[0][:, 0:512]
                psB = pp[1][:, 0:512]
                for c in range(NCH):
                    p.mm(psA, wg[c][:, fo * 128:(fo + 1) * 128],
                         xn_bf[:, c * T + t0: c * T + t0 + 512],
                         start=(c == 0), stop=(c == NCH - 1), tag="eg_mm")
                for c in range(NCH):
                    p.mm(psB, wu[c][:, fo * 128:(fo + 1) * 128],
                         xn_bf[:, c * T + t0: c * T + t0 + 512],
                         start=(c == 0), stop=(c == NCH - 1), tag="eu_mm")
                p.act(su_r[:, 0:512], psA, AF.Silu, tag="eg_silu")
                p.tt(gt_r[:, fo * 512:(fo + 1) * 512], su_r[:, 0:512], psB,
                     op=ALU.mult, tag="eg_mul")
            p.mm(cm.bcast_psum[:, 0:512], sel[:, e * 128:(e + 1) * 128],
                 comb8[0:8, t0:t0 + 512], tag="cb_mm")
            p.act(comb_sb[:, 0:512], cm.bcast_psum[:, 0:512], AF.Copy,
                  tag="cb_cp")
            for fo in range(NCH):
                ps = pp[2][:, 0:512]
                for c in range(11):
                    p.mm(ps, wd[c][:, fo * 128:(fo + 1) * 128],
                         gt_r[:, c * 512:(c + 1) * 512],
                         start=(c == 0), stop=(c == 10), tag="ed_mm")
                p.tt(cm.scratch[:, 0:512], ps, comb_sb[:, 0:512],
                     op=ALU.mult, tag="ed_msk")
                p.tt(xsb[:, fo * T + t0: fo * T + t0 + 512],
                     xsb[:, fo * T + t0: fo * T + t0 + 512],
                     cm.scratch[:, 0:512], op=ALU.add, tag="ed_acc")

    swd = load_wT_bf16(p, swd_d, 2816, 1024, "swd", region=swd_r)
    for it in range(nt):
        t0 = it * 512
        for f in range(22):
            wg_t = wjit_r[:, 0:1024]
            wu_t = wjit_r[:, 1024:2048]
            p.dma(wg_t.rearrange("p (c n) -> p c n", n=128),
                  swg_d[:, f * 128:(f + 1) * 128].rearrange(
                      "(c p) n -> p c n", p=128), q="poolq", tag="swg_ld")
            p.dma(wu_t.rearrange("p (c n) -> p c n", n=128),
                  swu_d[:, f * 128:(f + 1) * 128].rearrange(
                      "(c p) n -> p c n", p=128), q="poolq", tag="swu_ld")
            psA = pp[0][:, 0:512]
            psB = pp[1][:, 0:512]
            for c in range(NCH):
                p.mm(psA, wg_t[:, c * 128:(c + 1) * 128],
                     xn_bf[:, c * T + t0: c * T + t0 + 512],
                     start=(c == 0), stop=(c == NCH - 1), tag="sg_mm")
            for c in range(NCH):
                p.mm(psB, wu_t[:, c * 128:(c + 1) * 128],
                     xn_bf[:, c * T + t0: c * T + t0 + 512],
                     start=(c == 0), stop=(c == NCH - 1), tag="su_mm")
            p.act(su_r[:, 0:512], psA, AF.Silu, tag="sg_silu")
            p.tt(gsw_r[:, f * 512:(f + 1) * 512], su_r[:, 0:512], psB,
                 op=ALU.mult, tag="sg_mul")
        for fo in range(NCH):
            ps = pp[2][:, 0:512]
            for c in range(22):
                p.mm(ps, swd[c][:, fo * 128:(fo + 1) * 128],
                     gsw_r[:, c * 512:(c + 1) * 512],
                     start=(c == 0), stop=(c == 21), tag="sd_mm")
            p.tt(xsb[:, fo * T + t0: fo * T + t0 + 512],
                 xsb[:, fo * T + t0: fo * T + t0 + 512], ps,
                 op=ALU.add, tag="sd_acc")

    store_stream(p, xsb, T)
    return p



def build_l2f(T=TLOC, p=None):
    p = p or P("l2f")
    cm = Common(p, T, slim=True)
    xsb = load_stream(p, T)
    ident = make_identity(p, "identc", F32)

    rw_f = load_wT_f32(p, p.dram_in("router_w", [1024, 8]), 1024, 8, "rw")
    rb_d = p.dram_in("router_b", [8])
    w1_d = p.dram_in("w1", [8, 1024, 8192])
    w2_d = p.dram_in("w2", [8, 4096, 1024])

    A = Arena(p, "A", 22600)
    xn_bf = A.bf16(0, NCH * T)
    gr_r = A.bf16(8192, 8 * 2048)
    wd_r = A.bf16(16384, 8 * 1024)
    wjit_r = A.bf16(20480, 4 * 1024)

    pp = [p.psum("mmp%d" % i, [128, 512], F32) for i in range(3)]
    rlog = p.psum("rlog", [128, 32], F32)
    rlog_sb = p.sbuf("rlog_sb", [128, 32], F32)
    scores = p.sbuf("scores", [128, 8], F32)
    srtmp = p.sbuf("srtmp", [128, 8], F32)
    m8 = p.sbuf("m8", [128, 8], F32)
    mcol = p.sbuf("mcol", [128, 4], F32)
    comb_tm = p.sbuf("comb_tm", [128, 8], F32)
    comb8 = p.sbuf("comb8", [8, T], F32)
    cps = pp[1]

    rb_row = load_row_f32(p, rb_d, 8, "rb")
    rb_bc = p.sbuf("rb_bc", [128, 32], F32)
    for sub in range(4):
        p.mm(cm.bcast_psum[:, 0:8], cm.ones_row[0:1, :],
             rb_row[0:1, 0:8], tag="rb_bc")
        p.copy(rb_bc[:, sub * 8:(sub + 1) * 8], cm.bcast_psum[:, 0:8],
               eng="dve", tag="rb_cp")

    def emit_xn(c, it, sc):
        p.copy(xn_bf[:, c * T + it * 512: c * T + (it + 1) * 512], sc,
               eng="pool", tag="xnbf")
        if c == 0:
            p.copy(rlog_sb[:], rb_bc[:], eng="dve", tag="rt_z")
        for sub in range(4):
            p.mm(rlog[:, sub * 8:(sub + 1) * 8],
                 sc[:, sub * 128:(sub + 1) * 128], rw_f[c][:, 0:8],
                 start=True, stop=True, tag="rt_mm")
        p.tt(rlog_sb[:], rlog_sb[:], rlog[:, 0:32], op=ALU.add, tag="rt_acc")

    def do_router(s):
        rsl = rlog_sb[:, (s % 4) * 8:(s % 4) * 8 + 8]
        p.red(mcol[:, 0:1], rsl, op=ALU.max, eng="dve", negate=True,
              tag="rt_max")
        p.act(scores[:, 0:8], rsl, AF.Exp, bias=mcol[:, 0:1], tag="rt_exp")
        p.red(mcol[:, 1:2], scores[:, 0:8], op=ALU.add, tag="rt_sum")
        p.reciprocal(mcol[:, 2:3], mcol[:, 1:2], tag="rt_rcp")
        p.ts(scores[:, 0:8], scores[:, 0:8], mcol[:, 2:3], None,
             op0=ALU.mult, tag="rt_nrm")
        p.top8(m8[:, 0:8], scores[:, 0:8], tag="rt_top")
        p.ts(srtmp[:, 0:8], scores[:, 0:8], m8[:, 0:1], None,
             op0=ALU.is_equal, tag="rt_msk")
        p.ts(comb_tm[:, 0:8], srtmp[:, 0:8], m8[:, 0:1], None,
             op0=ALU.mult, tag="rt_cw")
        p.transpose(cps[0:8, 0:128], comb_tm[:, 0:8], ident[:], tag="rt_tr")
        p.copy(comb8[:, s * 128:(s + 1) * 128], cps[0:8, 0:128], eng="dve",
               tag="rt_cp")

    def router_it(it):
        for sub in range(4):
            do_router(it * 4 + sub)

    def x_tile(c, it):
        return xsb[:, c * T + it * 512: c * T + (it + 1) * 512]
    ln_streaming(p, cm, T, x_tile, emit_xn, "ln_", it_done=router_it)

    sel = p.sbuf("sel", [8, 8 * 128], F32)
    selp = p.sbuf("selp", [8, 1], F32)
    p.iota(selp[0:8, :], [[0, 1]], base=0, channel_multiplier=1, tag="selp")
    for e in range(8):
        p.ts(sel[0:8, e * 128:(e + 1) * 128],
             bass.AP(selp, 0, [[1, 8], [0, 128]]), float(e), None,
             op0=ALU.is_equal, tag="sel%d" % e)
    comb_sb = p.sbuf("comb_sb", [128, 512], F32)
    su_h = p.sbuf("su_t", [128, 512], BF16)
    nt = T // 512

    for e in range(8):
        for q in range(4):
            for f in range(8):
                fa = q * 8 + f
                jb = (f % 2) * 2048
                wa_t = wjit_r[:, jb:jb + 1024]
                wg_t = wjit_r[:, jb + 1024:jb + 2048]
                p.dma(wa_t.rearrange("p (c n) -> p c n", n=128),
                      w1_d[e][:, fa * 128:(fa + 1) * 128].rearrange(
                          "(c p) n -> p c n", p=128), q="poolq", tag="wa_ld")
                p.dma(wg_t.rearrange("p (c n) -> p c n", n=128),
                      w1_d[e][:, 4096 + fa * 128: 4096 + (fa + 1) * 128]
                      .rearrange("(c p) n -> p c n", p=128), q="poolq",
                      tag="wg_ld")
                for it in range(nt):
                    t0 = it * 512
                    psA = pp[0][:, 0:512]
                    psB = pp[2][:, 0:512]
                    for c in range(NCH):
                        p.mm(psA, wa_t[:, c * 128:(c + 1) * 128],
                             xn_bf[:, c * T + t0: c * T + t0 + 512],
                             start=(c == 0), stop=(c == NCH - 1), tag="e1_mm")
                    for c in range(NCH):
                        p.mm(psB, wg_t[:, c * 128:(c + 1) * 128],
                             xn_bf[:, c * T + t0: c * T + t0 + 512],
                             start=(c == 0), stop=(c == NCH - 1), tag="e2_mm")
                    p.act(su_h[:, 0:512], psA, AF.Silu, tag="e_silu")
                    p.tt(gr_r[:, (f * nt + it) * 512:(f * nt + it + 1) * 512],
                         su_h[:, 0:512], psB, op=ALU.mult, tag="e_mul")
            wd = load_wT_bf16(p, w2_d[e][q * 1024:(q + 1) * 1024, :],
                              1024, 1024, "wd%d_%d" % (e, q), region=wd_r)
            for it in range(nt):
                t0 = it * 512
                p.mm(cm.bcast_psum[:, 0:512], sel[:, e * 128:(e + 1) * 128],
                     comb8[0:8, t0:t0 + 512], tag="cb_mm")
                p.act(comb_sb[:, 0:512], cm.bcast_psum[:, 0:512], AF.Copy,
                      tag="cb_cp")
                for fo in range(NCH):
                    ps = pp[1][:, 0:512]
                    for c in range(8):
                        p.mm(ps, wd[c][:, fo * 128:(fo + 1) * 128],
                             gr_r[:, (c * nt + it) * 512:
                                  (c * nt + it + 1) * 512],
                             start=(c == 0), stop=(c == 7), tag="ed_mm")
                    p.tt(cm.scratch[:, 0:512], ps,
                         comb_sb[:, 0:512], op=ALU.mult, tag="ed_msk")
                    p.tt(xsb[:, fo * T + t0: fo * T + t0 + 512],
                         xsb[:, fo * T + t0: fo * T + t0 + 512],
                         cm.scratch[:, 0:512], op=ALU.add, tag="ed_acc")

    store_stream(p, xsb, T)
    return p



TWO_PI = 6.283185307179586
MAGIC = 12582912.0


def emit_mod2pi(p, out, ang, tmp, pref):
    p.ts(tmp, ang, 1.0 / TWO_PI, -0.5, op0=ALU.mult, op1=ALU.add,
         tag=pref + "m1")
    p.ts(tmp, tmp, MAGIC, None, op0=ALU.add, tag=pref + "m2")
    p.ts(tmp, tmp, -MAGIC, None, op0=ALU.add, tag=pref + "m3")
    p.stt(out, tmp, -TWO_PI, ang, op0=ALU.mult, op1=ALU.add, tag=pref + "m4")


def emit_trig(p, ang, Ct, St, tmp, negpi, pref):
    emit_mod2pi(p, Ct, ang, tmp, pref + "s")
    p.act(St, Ct, AF.Sin, bias=negpi, scale=1.0, tag=pref + "sin")
    p.ts(Ct, ang, np.pi / 2, None, op0=ALU.add, tag=pref + "shift")
    emit_mod2pi(p, Ct, Ct, tmp, pref + "c")
    p.act(Ct, Ct, AF.Sin, bias=negpi, scale=1.0, tag=pref + "cos")


def build_l1a_a(T=TLOC, p=None):
    p = p or P("l1a_a")
    cm = Common(p, T, slim=True)
    xs_d = p.dram_in("xs", [D, T])
    xn_d = p.dram_out("xn", [D, T])
    ure_d = p.dram_out("ure", [D, T])
    uim_d = p.dram_out("uim", [D, T])
    F_d = p.dram_out("Fc", [D, 2])
    t0c_d = p.dram_in("t0col", [128])
    nu_log = load_col_f32(p, p.dram_in("nu_log", [D]), D, "nul")
    th_log = load_col_f32(p, p.dram_in("theta_log", [D]), D, "thl")
    BreT = load_wT_bf16(p, p.dram_in("BreT", [D, D]), D, D, "bre")
    BimT = load_wT_bf16(p, p.dram_in("BimT", [D, D]), D, D, "bim")

    xn_bf = p.sbuf("xn_bf", [128, NCH * T], BF16)
    xstage = p.sbuf("xstage", [128, 1024], F32)
    xbuf = p.sbuf("xbuf", [128, 1024], F32)
    xslot = [0]

    def x_tile(c, it):
        s = xslot[0] % 2
        xslot[0] += 1
        sl = xbuf[:, s * 512:(s + 1) * 512]
        p.dma(sl, xs_d[c * 128:(c + 1) * 128, it * 512:(it + 1) * 512],
              q="spq", tag="xt")
        return sl

    def emit_xn(c, it, sc):
        p.copy(xn_bf[:, c * T + it * 512: c * T + (it + 1) * 512], sc,
               eng="pool", tag="xnbf")
        st = xstage[:, (it % 2) * 512:(it % 2) * 512 + 512]
        p.copy(st, sc, eng="dve", tag="xnst")
        p.dma(xn_d[c * 128:(c + 1) * 128, it * 512:(it + 1) * 512], st,
              q="actq", tag="xnout")

    ln_streaming(p, cm, T, x_tile, emit_xn, "ln_")

    prm = p.sbuf("prm", [128, 4 * NCH], F32)
    t0col = p.sbuf("t0col_sb", [128, 1], F32)
    p.dma(t0col[:, 0:1], t0c_d.rearrange("(a b) -> a b", b=1), q="spq",
          tag="t0")
    for hc in range(NCH):
        nu = prm[:, 4 * hc:4 * hc + 1]
        r = prm[:, 4 * hc + 1:4 * hc + 2]
        gm = prm[:, 4 * hc + 2:4 * hc + 3]
        th = prm[:, 4 * hc + 3:4 * hc + 4]
        p.act(nu, nu_log[hc], AF.Exp, tag="p_nu")
        p.act(r, nu, AF.Exp, scale=-1.0, tag="p_r")
        p.act(gm, nu, AF.Exp, scale=-2.0, tag="p_e2")
        p.ts(gm, gm, -1.0, 1.0 + 1e-5, op0=ALU.mult, op1=ALU.add, tag="p_t2")
        p.act(gm, gm, AF.Sqrt, tag="p_gm")
        p.act(th, th_log[hc], AF.Exp, tag="p_th")

    iota_bc = p.sbuf("iota_bc", [128, T], F32)
    p.iota(iota_bc[:], [[1, T]], base=0, channel_multiplier=0, tag="iota")
    negpi = p.sbuf("negpi", [128, 1], F32)
    p.memset(negpi[:], -np.pi, eng="pool")
    work = p.sbuf("work", [128, 8 * T], F32)
    ang = work[:, 0:T]
    tmp = work[:, T:2 * T]
    Ct = work[:, 2 * T:3 * T]
    St = work[:, 3 * T:4 * T]
    bur = work[:, 4 * T:5 * T]
    bui = work[:, 5 * T:6 * T]
    ur = work[:, 6 * T:7 * T]
    ui = work[:, 7 * T:8 * T]
    tmp2 = cm.scratch[:, 0:512]
    pp = [p.psum("mmp%d" % i, [128, 512], F32) for i in range(2)]
    fstage = p.sbuf("fstage", [128, 2], F32)

    for hc in range(NCH):
        nu = prm[:, 4 * hc:4 * hc + 1]
        r = prm[:, 4 * hc + 1:4 * hc + 2]
        gm = prm[:, 4 * hc + 2:4 * hc + 3]
        th = prm[:, 4 * hc + 3:4 * hc + 4]
        for it in range(T // 512):
            t0 = it * 512
            for (wt, dst) in ((BreT, bur), (BimT, bui)):
                ps = pp[0][:, 0:512] if wt is BreT else pp[1][:, 0:512]
                for c in range(NCH):
                    p.mm(ps, wt[c][:, hc * 128:(hc + 1) * 128],
                         xn_bf[:, c * T + t0:c * T + t0 + 512],
                         start=(c == 0), stop=(c == NCH - 1), tag="bu_mm")
                p.ts(dst[:, t0:t0 + 512], ps, gm, None, op0=ALU.mult,
                     tag="bu_ev")
        p.ts(ang, iota_bc[:], t0col[:, 0:1], th, op0=ALU.add, op1=ALU.mult,
             tag="ang")
        emit_trig(p, ang, Ct, St, tmp, negpi[:, 0:1], "tr%d" % hc)
        p.tt(ur, bur, Ct, op=ALU.mult, tag="d1")
        p.tt(tmp, bui, St, op=ALU.mult, tag="d2")
        p.tt(ur, ur, tmp, op=ALU.add, tag="d3")
        p.tt(ui, bui, Ct, op=ALU.mult, tag="d4")
        p.tt(tmp, bur, St, op=ALU.mult, tag="d5")
        p.tt(ui, ui, tmp, op=ALU.subtract, tag="d6")
        p.scan(ur, r.broadcast_to([128, T]), ur, 0.0, tag="scan_re")
        p.scan(ui, r.broadcast_to([128, T]), ui, 0.0, tag="scan_im")
        p.copy(fstage[:, 0:1], ur[:, T - 1:T], eng="pool", tag="fst")
        p.copy(fstage[:, 1:2], ui[:, T - 1:T], eng="pool", tag="fst2")
        p.dma(F_d[hc * 128:(hc + 1) * 128, :], fstage[:, 0:2], q="actq",
              tag="fout")
        p.dma(ure_d[hc * 128:(hc + 1) * 128, :], ur, q="spq", tag="uout")
        p.dma(uim_d[hc * 128:(hc + 1) * 128, :], ui, q="actq", tag="uout2")
    return p


def build_l1a_b(T=TLOC, p=None):
    p = p or P("l1a_b")
    xs_d = p.dram_in("xs", [D, T])
    xn_d = p.dram_in("xn", [D, T])
    ure_d = p.dram_in("ure", [D, T])
    uim_d = p.dram_in("uim", [D, T])
    Fall_d = p.dram_in("Fall", [D, 8])
    a4_d = p.dram_in("a4", [4])
    t0c_d = p.dram_in("t0col", [128])
    ys_d = p.dram_out("ys", [D, T])
    nu_log = load_col_f32(p, p.dram_in("nu_log", [D]), D, "nul")
    th_log = load_col_f32(p, p.dram_in("theta_log", [D]), D, "thl")
    D_skip = load_col_f32(p, p.dram_in("D_skip", [D]), D, "dsk")
    CreT = load_wT_bf16(p, p.dram_in("CreT", [D, D]), D, D, "cre")
    CimT = load_wT_bf16(p, p.dram_in("CimT", [D, D]), D, D, "cim")

    ones_row = p.sbuf("ones_row", [1, 128], F32)
    p.memset(ones_row[:], 1.0, eng="pool")
    a4r = p.sbuf("a4r", [1, 4], F32)
    p.dma(a4r[0:1, :], a4_d.rearrange("(a b) -> a b", a=1), q="spq", tag="a4")
    acol = p.sbuf("acol", [128, 4], F32)
    bps = p.psum("bps", [128, 8], F32)
    p.mm(bps[:, 0:4], ones_row[0:1, :], a4r[0:1, 0:4], tag="a4bc")
    p.copy(acol[:, 0:4], bps[:, 0:4], eng="dve", tag="a4cp")
    t0col = p.sbuf("t0col_sb", [128, 1], F32)
    p.dma(t0col[:, 0:1], t0c_d.rearrange("(a b) -> a b", b=1), q="spq",
          tag="t0")

    iota_bc = p.sbuf("iota_bc", [128, T], F32)
    p.iota(iota_bc[:], [[1, T]], base=0, channel_multiplier=0, tag="iota")
    negpi = p.sbuf("negpi", [128, 1], F32)
    p.memset(negpi[:], -np.pi, eng="pool")

    hre_bf = p.sbuf("hre_bf", [128, NCH * T], BF16)
    himn_bf = p.sbuf("himn_bf", [128, NCH * T], BF16)
    work = p.sbuf("work", [128, 7 * T], F32)
    ang = work[:, 0:T]
    tmp = work[:, T:2 * T]
    Ct = work[:, 2 * T:3 * T]
    St = work[:, 3 * T:4 * T]
    ur = work[:, 4 * T:5 * T]
    ui = work[:, 5 * T:6 * T]
    tmp3 = work[:, 6 * T:7 * T]
    prm = p.sbuf("prm", [128, 2 * NCH], F32)
    Fh = p.sbuf("Fh", [128, 8], F32)
    Ucol = p.sbuf("Ucol", [128, 4], F32)

    for hc in range(NCH):
        nu = prm[:, 2 * hc:2 * hc + 1]
        th = prm[:, 2 * hc + 1:2 * hc + 2]
        p.act(nu, nu_log[hc], AF.Exp, tag="p_nu")
        p.act(th, th_log[hc], AF.Exp, tag="p_th")
        p.dma(ur, ure_d[hc * 128:(hc + 1) * 128, :], q="spq", tag="uld")
        p.dma(ui, uim_d[hc * 128:(hc + 1) * 128, :], q="spq", tag="uld2")
        p.dma(Fh[:, 0:8], Fall_d[hc * 128:(hc + 1) * 128, :], q="spq",
              tag="fld")
        p.memset(Ucol[:, 0:2], 0.0, eng="dve", tag="uz")
        for q in range(4):
            p.tt(Ucol[:, 2:3], nu, acol[:, q:q + 1], op=ALU.mult, tag="ualp")
            p.act(Ucol[:, 2:3], Ucol[:, 2:3], AF.Exp, scale=-1.0, tag="uexp")
            p.stt(Ucol[:, 0:1], Fh[:, 2 * q:2 * q + 1], Ucol[:, 2:3],
                  Ucol[:, 0:1], op0=ALU.mult, op1=ALU.add, tag="uacc")
            p.stt(Ucol[:, 1:2], Fh[:, 2 * q + 1:2 * q + 2], Ucol[:, 2:3],
                  Ucol[:, 1:2], op0=ALU.mult, op1=ALU.add, tag="uacc2")
        p.ts(tmp, iota_bc[:], 1.0, nu, op0=ALU.add, op1=ALU.mult, tag="fx1")
        p.act(tmp, tmp, AF.Exp, scale=-1.0, tag="fx2")
        p.stt(tmp3, tmp, Ucol[:, 0:1], ur, op0=ALU.mult, op1=ALU.add,
              tag="fx3")
        p.copy(ur, tmp3, eng="pool", tag="fx3c")
        p.stt(tmp3, tmp, Ucol[:, 1:2], ui, op0=ALU.mult, op1=ALU.add,
              tag="fx4")
        p.copy(ui, tmp3, eng="pool", tag="fx4c")
        p.ts(ang, iota_bc[:], t0col[:, 0:1], th, op0=ALU.add, op1=ALU.mult,
             tag="ang")
        emit_trig(p, ang, Ct, St, tmp, negpi[:, 0:1], "tr%d" % hc)
        p.tt(tmp3, ur, Ct, op=ALU.mult, tag="r1")
        p.tt(tmp, ui, St, op=ALU.mult, tag="r2")
        p.tt(hre_bf[:, hc * T:(hc + 1) * T], tmp3, tmp, op=ALU.subtract,
             tag="r3")
        p.tt(tmp3, ui, Ct, op=ALU.mult, tag="r4")
        p.tt(tmp, ur, St, op=ALU.mult, tag="r5")
        p.tt(tmp3, tmp3, tmp, op=ALU.add, tag="r6")
        p.ts(himn_bf[:, hc * T:(hc + 1) * T], tmp3, -1.0, None, op0=ALU.mult,
             tag="r7")

    pp = [p.psum("pm%d" % i, [128, 512], F32) for i in range(3)]
    xnb = p.sbuf("xnb", [128, 1024], F32)
    xb = p.sbuf("xb", [128, 1024], F32)
    rb = p.sbuf("rb", [128, 1024], F32)
    for fo in range(NCH):
        for it in range(T // 512):
            t0 = it * 512
            ps = pp[it % 2][:, 0:512]
            for c in range(NCH):
                p.mm(ps, CreT[c][:, fo * 128:(fo + 1) * 128],
                     hre_bf[:, c * T + t0:c * T + t0 + 512],
                     start=(c == 0), stop=False, tag="pr_mm")
            for c in range(NCH):
                p.mm(ps, CimT[c][:, fo * 128:(fo + 1) * 128],
                     himn_bf[:, c * T + t0:c * T + t0 + 512],
                     start=False, stop=(c == NCH - 1), tag="pi_mm")
            xnt = xnb[:, (it % 2) * 512:(it % 2) * 512 + 512]
            xt = xb[:, (it % 2) * 512:(it % 2) * 512 + 512]
            rt = rb[:, (it % 2) * 512:(it % 2) * 512 + 512]
            p.dma(xnt, xn_d[fo * 128:(fo + 1) * 128, t0:t0 + 512], q="spq",
                  tag="xnld")
            p.dma(xt, xs_d[fo * 128:(fo + 1) * 128, t0:t0 + 512], q="spq",
                  tag="xld")
            p.stt(rt, xnt, D_skip[fo], ps, op0=ALU.mult, op1=ALU.add,
                  tag="ev1")
            p.tt(rt, rt, xt, op=ALU.add, tag="ev2")
            p.dma(ys_d[fo * 128:(fo + 1) * 128, t0:t0 + 512], rt, q="actq",
                  tag="yout")
    return p



def build_l2a_pre(T=TLOC, p=None, fused=False):
    p = p or P("l2a_pre")
    cm = Common(p, T, slim=True)
    xs_d = p.dram_in("xs", [D, T])
    t0c_d = p.dram_in("t0col", [128])
    kidx_d = p.dram_in("kidx", [128])
    alt_d = p.dram_in("alt", [128])
    pswap_d = p.dram_in("pswap", [128, 128])
    wqkvT = load_wT_bf16(p, p.dram_in("wqkvT", [D, 3 * D]), D, 3 * D, "wqkv")
    bqkv = load_col_f32(p, p.dram_in("bqkv", [3 * D]), 3 * D, "bqkv")
    qf_d = p.dram_out("qf", [D, T], BF16)
    kf_d = p.dram_out("kf", [D, T], BF16)
    vf_d = p.dram_out("vf", [D, T], BF16)
    vaug_d = p.dram_out("vaug", [T, 16 * 65], BF16)
    qn_d = p.dram_out("qn", [16, T])
    kn_d = p.dram_out("kn", [16, T])
    if fused:
        vedge_d = p.dram_out("vedge", [2, D], BF16)

    xn_bf = p.sbuf("xn_bf", [128, NCH * T], BF16)
    xbuf = p.sbuf("xbuf", [128, 1024], F32)
    xslot = [0]

    def x_tile(c, it):
        s = xslot[0] % 2
        xslot[0] += 1
        sl = xbuf[:, s * 512:(s + 1) * 512]
        p.dma(sl, xs_d[c * 128:(c + 1) * 128, it * 512:(it + 1) * 512],
              q="spq", tag="xt")
        return sl

    def emit_xn(c, it, sc):
        p.copy(xn_bf[:, c * T + it * 512: c * T + (it + 1) * 512], sc,
               eng="pool", tag="xnbf")
    ln_streaming(p, cm, T, x_tile, emit_xn, "ln_")

    t0col = p.sbuf("t0col_sb", [128, 1], F32)
    p.dma(t0col[:, 0:1], t0c_d.rearrange("(a b) -> a b", b=1), q="spq", tag="t0")
    kidx = p.sbuf("kidx_sb", [128, 1], F32)
    p.dma(kidx[:, 0:1], kidx_d.rearrange("(a b) -> a b", b=1), q="spq", tag="ki")
    alt = p.sbuf("alt_sb", [128, 1], F32)
    p.dma(alt[:, 0:1], alt_d.rearrange("(a b) -> a b", b=1), q="spq", tag="al")
    pswap = p.sbuf("pswap_sb", [128, 128], BF16)
    p.dma(pswap[:], pswap_d, q="poolq", tag="ps")
    negpi = p.sbuf("negpi", [128, 1], F32)
    p.memset(negpi[:], -np.pi, eng="pool")
    iota_bc = p.sbuf("iota_bc", [128, T], F32)
    p.iota(iota_bc[:], [[1, T]], base=0, channel_multiplier=0, tag="iota")

    tb = p.sbuf("tb", [128, 4 * T], F32)
    ang = tb[:, 0:T]
    tmp = tb[:, T:2 * T]
    Ct = tb[:, 2 * T:3 * T]
    St = tb[:, 3 * T:4 * T]
    cosr = tb[:, 0:T]
    sspk = tb[:, T:2 * T]
    divc = p.sbuf("divc", [128, 1], F32)
    p.act(divc[:, 0:1], kidx[:, 0:1], AF.Exp,
          scale=-2.0 * math.log(10000.0) / 64.0, tag="div")
    p.ts(ang, iota_bc[:], t0col[:, 0:1], divc[:, 0:1], op0=ALU.add,
         op1=ALU.mult, tag="ang")
    emit_trig(p, ang, Ct, St, tmp, negpi[:, 0:1], "rt")
    p.ts(cosr, Ct, -1.0, None, op0=ALU.mult, tag="cosr")
    p.ts(sspk, St, -1.0, None, op0=ALU.mult, tag="s1")
    p.tt(sspk, sspk, alt[:, 0:1].broadcast_to([128, T]), op=ALU.mult, tag="s2")

    pp = [p.psum("mmp%d" % i, [128, 512], F32) for i in range(2)]
    ptr_bf = [p.psum("ptr%d" % i, [128, 64], BF16) for i in range(2)]
    identb2 = make_identity(p, "identb2", BF16)
    ew = p.sbuf("ew", [128, 4 * 512], F32)
    nrmst = p.sbuf("nrmst", [2, 1024], F32)
    bd2 = p.sbuf("bd2", [128, 2], F32)
    p.memset(bd2[:, 0:2], 0.0, eng="pool")
    p.memset(bd2[0:64, 0:1], 1.0, eng="pool")
    p.memset(bd2[64:128, 1:2], 1.0, eng="pool")
    vaug_sb = p.sbuf("vaug_sb", [128, 16 * 1040], BF16)
    p.memset(vaug_sb[:], 0.0, eng="pool")
    onecol = p.sbuf("onecol", [128, 1], BF16)
    p.memset(onecol[:], 1.0, eng="pool")
    for tc in range(16):
        for h in range(16):
            p.copy(vaug_sb[:, tc * 1040 + 65 * h + 64: tc * 1040 + 65 * h + 65],
                   onecol[:, 0:1], eng="pool", tag="vone")
    qkv_bf = p.sbuf("qkv_bf", [128, 512], BF16)
    nps = cm.psum_stats[0:2, 0:512]

    for part in range(3):
        for c in range(NCH):
            fo = part * NCH + c
            for it in range(T // 512):
                t0 = it * 512
                ps = pp[it % 2][:, 0:512]
                for ci in range(NCH):
                    p.mm(ps, wqkvT[ci][:, fo * 128:(fo + 1) * 128],
                         xn_bf[:, ci * T + t0: ci * T + t0 + 512],
                         start=(ci == 0), stop=(ci == NCH - 1), tag="qkv_mm")
                if part == 2:
                    p.ts(qkv_bf[:, 0:512], ps, bqkv[fo], None, op0=ALU.add,
                         tag="v_ev")
                    p.dma(vf_d[c * 128:(c + 1) * 128, t0:t0 + 512],
                          qkv_bf[:, 0:512], q="actq", tag="v_out")
                    if fused:
                        if it == 0:
                            p.dma(vedge_d[0:1, c * 128:(c + 1) * 128],
                                  qkv_bf[:, 0:1], q="actq", tag="vedge0")
                        if it == T // 512 - 1:
                            p.dma(vedge_d[1:2, c * 128:(c + 1) * 128],
                                  qkv_bf[:, 511:512], q="actq", tag="vedge1")
                    for tsb in range(4):
                        for hh in range(2):
                            h = 2 * c + hh
                            tc = it * 4 + tsb
                            tp = ptr_bf[(tsb * 2 + hh) % 2]
                            p.transpose(tp[0:128, 0:64],
                                        qkv_bf[hh * 64:(hh + 1) * 64,
                                               tsb * 128:(tsb + 1) * 128],
                                        identb2[hh * 64:(hh + 1) * 64,
                                                hh * 64:(hh + 1) * 64],
                                        tag="v_tr")
                            p.copy(vaug_sb[:, tc * 1040 + 65 * h:
                                           tc * 1040 + 65 * h + 64],
                                   tp[0:128, 0:64], eng="dve", tag="v_cp")
                else:
                    el = ew[:, 0:512]
                    t1 = ew[:, 512:1024]
                    p.ts(el, ps, bqkv[fo], 0.0, op0=ALU.add, op1=ALU.min,
                         tag="e_min")
                    p.act(el, el, AF.Exp, tag="e_exp")
                    p.ts(t1, ps, bqkv[fo], 0.0, op0=ALU.add, op1=ALU.max,
                         tag="e_max")
                    p.tt(el, el, t1, op=ALU.add, tag="e_add")
                    p.tt(t1, el, el, op=ALU.mult, eng="pool", tag="n_sq")
                    p.mm(nps[0:2, 0:512], bd2[:, 0:2], t1, tag="n_mm")
                    scl = 1.0 / 64.0 if part == 0 else 1.0
                    nst = nrmst[0:2, (it % 2) * 512:(it % 2) * 512 + 512]
                    p.act(nst, nps[0:2, 0:512], AF.Sqrt, scale=scl, tag="n_rt")
                    nd = qn_d if part == 0 else kn_d
                    p.dma(nd[2 * c:2 * c + 2, t0:t0 + 512], nst, q="spq",
                          tag="n_out")
                    elb = qkv_bf[:, 0:512]
                    p.copy(elb, el, eng="pool", tag="r_bf")
                    psh = pp[(it + 1) % 2][:, 0:512]
                    p.mm(psh, pswap[:], elb, tag="r_sw")
                    t2 = ew[:, 1024:1536]
                    t3 = ew[:, 1536:2048]
                    p.tt(t2, el, cosr[:, t0:t0 + 512], op=ALU.mult, tag="r_c")
                    p.tt(t3, psh, sspk[:, t0:t0 + 512], op=ALU.mult, tag="r_s")
                    sc2 = 0.125 if part == 0 else 1.0
                    p.tt(t2, t2, t3, op=ALU.add, tag="r_a")
                    dst = qf_d if part == 0 else kf_d
                    p.ts(qkv_bf[:, 0:512], t2, sc2, None, op0=ALU.mult,
                         tag="r_o")
                    p.dma(dst[c * 128:(c + 1) * 128, t0:t0 + 512],
                          qkv_bf[:, 0:512], q="actq", tag="r_out")

    for tc in range(16):
        p.dma(vaug_d[tc * 128:(tc + 1) * 128, :],
              vaug_sb[:, tc * 1040:(tc + 1) * 1040], q="actq", tag="va_out")
    return p


def build_l2a_flash(T=TLOC, p=None):
    p = p or P("l2a_flash")
    xs_d = p.dram_in("xs", [D, T])
    qf_d = p.dram_in("qf", [D, T], BF16)
    Kg_d = p.dram_in("Kg", [4 * D, T], BF16)
    VA_d = p.dram_in("VA", [4 * T, 16 * 65], BF16)
    qn_d = p.dram_in("qn", [16, T])
    KN_d = p.dram_in("KN", [16, 4 * T])
    vf_d = p.dram_in("vf", [D, T], BF16)
    vedl_d = p.dram_in("vedl", [D], BF16)
    vedr_d = p.dram_in("vedr", [D], BF16)
    wlep_d = p.dram_in("wlep", [D, 3])
    blep_d = p.dram_in("blep", [D])
    ys_d = p.dram_out("ys", [D, T])

    SEQT = 4 * T
    kaug = p.sbuf("kaug", [65, SEQT], BF16)
    p.memset(kaug[64:65, :], 1.0, eng="pool")
    qaug = p.sbuf("qaug", [65, T], BF16)
    p.memset(qaug[64:65, :], 0.0, eng="pool")
    qf_sb = p.sbuf("qf_sb", [128, NCH * T], BF16)
    for c in range(NCH):
        p.dma(qf_sb[:, c * T:(c + 1) * T],
              qf_d[c * 128:(c + 1) * 128, :], q="spq", tag="qld")
    qn_sb = p.sbuf("qn_sb", [16, T], F32)
    p.dma(qn_sb[0:16, :], qn_d, q="spq", tag="qnld")
    knt = p.sbuf("knt", [16, T], F32)
    kmax = p.sbuf("kmax", [16, 8], F32)
    for s in range(4):
        p.dma(knt[0:16, :], KN_d[:, s * T:(s + 1) * T], q="spq", tag="knld")
        p.red(kmax[0:16, s:s + 1], knt[0:16, :], op=ALU.max, tag="knmax")
    p.red(kmax[0:16, 4:5], kmax[0:16, 0:4], op=ALU.max, tag="knmax2")
    mneg = p.sbuf("mneg", [16, T], F32)
    p.ts(mneg[0:16, :], qn_sb[0:16, :], kmax[0:16, 4:5], -1.0,
         op0=ALU.mult, op1=ALU.mult, tag="mneg")

    vaug_h = p.sbuf("vaug_h", [128, 64 * 65], BF16)
    PT = p.sbuf("PT", [128, 2 * 512], BF16)
    lrow = p.sbuf("lrow", [1, 512], F32)
    lbc = p.sbuf("lbc", [64, 512], F32)
    ones_bf = p.sbuf("ones_bf", [1, 64], F32)
    p.memset(ones_bf[:], 1.0, eng="pool")
    attn = p.sbuf("attn", [128, NCH * T], BF16)
    psS = [p.psum("psS%d" % i, [128, 512], F32) for i in range(2)]
    psO = [p.psum("psO%d" % i, [65, 512], F32) for i in range(2)]
    psB = p.psum("psB", [64, 512], F32)

    for h in range(16):
        for s in range(4):
            p.dma(kaug[0:64, s * T:(s + 1) * T],
                  Kg_d[s * D + h * 64: s * D + (h + 1) * 64, :], q="spq",
                  tag="kld")
        p.dma(vaug_h.rearrange("p (j n) -> p j n", n=65),
              VA_d[:, 65 * h:65 * h + 65].rearrange("(j p) n -> p j n", p=128),
              q="poolq", tag="vld")
        p.copy(qaug[0:64, :],
               qf_sb[(h % 2) * 64:(h % 2) * 64 + 64,
                     (h // 2) * T:(h // 2 + 1) * T], eng="dve", tag="qcp")
        p.dma(qaug[64:65, :], mneg[h:h + 1, :], q="poolq", tag="qmn")
        for ib in range(4):
            i0 = ib * 512
            po = psO[ib % 2]
            for jt in range(SEQT // 128):
                ps = psS[jt % 2][:, 0:512]
                p.mm(ps, kaug[:, jt * 128:(jt + 1) * 128],
                     qaug[:, i0:i0 + 512], tag="s_mm")
                pt = PT[:, (jt % 2) * 512:(jt % 2) * 512 + 512]
                p.act(pt, ps, AF.Exp, tag="s_exp")
                p.mm(po[0:65, 0:512], vaug_h[:, jt * 65:(jt + 1) * 65], pt,
                     start=(jt == 0), stop=(jt == SEQT // 128 - 1),
                     tag="av_mm")
            p.reciprocal(lrow[0:1, 0:512], po[64:65, 0:512], tag="linv")
            p.mm(psB[0:64, 0:512], ones_bf[0:1, 0:64], lrow[0:1, 0:512],
                 tag="lbc_mm")
            p.copy(lbc[0:64, 0:512], psB[0:64, 0:512], eng="act", tag="lbc_cp")
            p.tt(attn[(h % 2) * 64:(h % 2) * 64 + 64,
                      (h // 2) * T + i0:(h // 2) * T + i0 + 512],
                 po[0:64, 0:512], lbc[0:64, 0:512], op=ALU.mult, tag="o_ev")

    wl = p.sbuf("wl", [128, 3 * NCH], F32)
    for c in range(NCH):
        p.dma(wl[:, 3 * c:3 * c + 3], wlep_d[c * 128:(c + 1) * 128, :],
              q="spq", tag="wlld")
    bl = load_col_f32(p, blep_d, D, "blep")
    vload = p.sbuf("vload", [128, T + 2], BF16)
    lw = p.sbuf("lw", [128, 2 * 512], F32)
    xb = p.sbuf("xb", [128, 2 * 512], F32)
    for c in range(NCH):
        p.dma(vload[:, 1:T + 1], vf_d[c * 128:(c + 1) * 128, :], q="spq",
              tag="vfld")
        p.dma(vload[:, 0:1],
              vedl_d[c * 128:(c + 1) * 128].rearrange("(a b) -> a b", b=1),
              q="spq", tag="vel")
        p.dma(vload[:, T + 1:T + 2],
              vedr_d[c * 128:(c + 1) * 128].rearrange("(a b) -> a b", b=1),
              q="spq", tag="ver")
        for it in range(T // 512):
            t0 = it * 512
            l1 = lw[:, (it % 2) * 512:(it % 2) * 512 + 512]
            p.ts(l1, vload[:, t0:t0 + 512], wl[:, 3 * c:3 * c + 1], None,
                 op0=ALU.mult, tag="lp0")
            p.stt(l1, vload[:, t0 + 1:t0 + 513], wl[:, 3 * c + 1:3 * c + 2],
                  l1, op0=ALU.mult, op1=ALU.add, tag="lp1")
            p.stt(l1, vload[:, t0 + 2:t0 + 514], wl[:, 3 * c + 2:3 * c + 3],
                  l1, op0=ALU.mult, op1=ALU.add, tag="lp2")
            p.tt(l1, l1, attn[:, c * T + t0:c * T + t0 + 512], op=ALU.add,
                 tag="lp3")
            p.ts(l1, l1, bl[c], None, op0=ALU.add, tag="lp4")
            xt = xb[:, (it % 2) * 512:(it % 2) * 512 + 512]
            p.dma(xt, xs_d[c * 128:(c + 1) * 128, t0:t0 + 512], q="spq",
                  tag="xld")
            p.tt(l1, l1, xt, op=ALU.add, tag="lp5")
            p.dma(ys_d[c * 128:(c + 1) * 128, t0:t0 + 512], l1, q="actq",
                  tag="yout")
    return p



def build_l0a_pre(T=TLOC, p=None):
    p = p or P("l0a_pre")
    cm = Common(p, T, slim=True)
    xs_d = p.dram_in("xs", [D, T])
    u_d = p.dram_out("u", [D, T])
    qf_d = p.dram_out("qf", [D, T], BF16)
    kf_d = p.dram_out("kf", [D, T], BF16)
    vaug_d = p.dram_out("vaug", [T, 528], BF16)
    sqv_d = p.dram_out("sqv", [16, T])
    qn_d = p.dram_out("qn", [16, T])
    kn_d = p.dram_out("kn", [16, T])

    wqT = load_wT_bf16(p, p.dram_in("wqT", [D, D]), D, D, "wq")
    wkT = load_wT_bf16(p, p.dram_in("wkT", [D, D]), D, D, "wk")
    wvT = load_wT_bf16(p, p.dram_in("wvT", [D, 512]), D, 512, "wv")
    wqvT = load_wT_bf16(p, p.dram_in("wqvT", [D, 512]), D, 512, "wqv")
    bq = load_col_f32(p, p.dram_in("bq", [D]), D, "bq")
    bk = load_col_f32(p, p.dram_in("bk", [D]), D, "bk")
    bv = load_col_f32(p, p.dram_in("bv", [512]), 512, "bv")
    bqv = load_col_f32(p, p.dram_in("bqv", [512]), 512, "bqv")
    ws_d = p.dram_in("ws", [1, 96])
    bs_d = p.dram_in("bs", [1])
    g1 = load_col_f32(p, p.dram_in("g1", [D]), D, "g1")
    b1 = load_col_f32(p, p.dram_in("b1", [D]), D, "b1")

    identb = make_identity(p, "identb", BF16)
    identf = make_identity(p, "identf", F32)

    ustage = p.sbuf("ustage", [128, 1024], F32)
    xbuf = p.sbuf("xbuf", [128, 1024], F32)
    xslot = [0]

    def x_tile(c, it):
        s = xslot[0] % 2
        xslot[0] += 1
        sl = xbuf[:, s * 512:(s + 1) * 512]
        p.dma(sl, xs_d[c * 128:(c + 1) * 128, it * 512:(it + 1) * 512],
              q="spq", tag="xt")
        return sl

    def emit_u(c, it, sc):
        st = ustage[:, (it % 2) * 512:(it % 2) * 512 + 512]
        p.copy(st, sc, eng="pool", tag="ucp")
        p.dma(u_d[c * 128:(c + 1) * 128, it * 512:(it + 1) * 512], st,
              q="actq", tag="uout")
    ln_streaming(p, cm, T, x_tile, emit_u, "ln1_")

    xn_bf = p.sbuf("xn_bf", [128, NCH * T], BF16)

    def u_tile(c, it):
        s = xslot[0] % 2
        xslot[0] += 1
        sl = xbuf[:, s * 512:(s + 1) * 512]
        p.dma(sl, u_d[c * 128:(c + 1) * 128, it * 512:(it + 1) * 512],
              q="spq", tag="ut")
        return sl

    def emit_xn(c, it, sc):
        p.stt(xn_bf[:, c * T + it * 512: c * T + (it + 1) * 512], sc, g1[c],
              b1[c].broadcast_to([128, 512]), op0=ALU.mult, op1=ALU.add,
              tag="xnaff")
    ln_streaming(p, cm, T, u_tile, emit_xn, "ln2_")

    pp = [p.psum("mmp%d" % i, [128, 512], F32) for i in range(2)]
    nps = cm.psum_stats[0:2, 0:512]
    nrmst = p.sbuf("nrmst", [2, 1024], F32)
    bd2 = p.sbuf("bd2", [128, 2], F32)
    p.memset(bd2[:, 0:2], 0.0, eng="pool")
    p.memset(bd2[0:64, 0:1], 1.0, eng="pool")
    p.memset(bd2[64:128, 1:2], 1.0, eng="pool")
    sq_t = p.sbuf("sq_t", [128, 512], F32)
    qkv_bf = p.sbuf("qkv_bf", [128, 512], BF16)

    for part in range(2):
        wt, bias, dst, nd, scl = ((wqT, bq, qf_d, qn_d, 0.125) if part == 0
                                  else (wkT, bk, kf_d, kn_d, 1.0))
        for c in range(NCH):
            for it in range(T // 512):
                t0 = it * 512
                ps = pp[it % 2][:, 0:512]
                for ci in range(NCH):
                    p.mm(ps, wt[ci][:, c * 128:(c + 1) * 128],
                         xn_bf[:, ci * T + t0: ci * T + t0 + 512],
                         start=(ci == 0), stop=(ci == NCH - 1), tag="p_mm")
                p.ts(qkv_bf[:, 0:512], ps, bias[c], scl, op0=ALU.add,
                     op1=ALU.mult, tag="p_ev")
                p.dma(dst[c * 128:(c + 1) * 128, t0:t0 + 512],
                      qkv_bf[:, 0:512], q="actq", tag="p_out")
                p.tt(sq_t[:, 0:512], qkv_bf[:, 0:512], qkv_bf[:, 0:512],
                 op=ALU.mult, eng="pool", tag="p_sq")
                p.mm(nps[0:2, 0:512], bd2[:, 0:2], sq_t[:, 0:512], tag="p_nm")
                nst = nrmst[0:2, (it % 2) * 512:(it % 2) * 512 + 512]
                p.act(nst, nps[0:2, 0:512], AF.Sqrt, tag="p_rt")
                p.dma(nd[2 * c:2 * c + 2, t0:t0 + 512], nst, q="spq",
                      tag="p_nout")

    ws_row = p.sbuf("ws_row", [1, 96], F32)
    p.dma(ws_row[0:1, :], ws_d, q="spq", tag="wsld")
    bs_row = load_row_f32(p, bs_d, 1, "bs")
    ws_col = p.sbuf("ws_col", [128, 3], F32)
    wpsA = cm.bcast_psum[0:64, 0:1]
    p.transpose(wpsA, ws_row[0:1, 32:96], identf[0:1, 0:1], tag="wstrA")
    p.copy(ws_col[0:64, 0:1], wpsA, eng="dve", tag="wscpA")
    wpsB = cm.bcast_psum[0:32, 1:2]
    p.transpose(wpsB, ws_row[0:1, 0:32], identf[0:1, 0:1], tag="wstrB")
    p.ts(ws_col[0:32, 1:2], wpsB, 1.0 / math.sqrt(32.0), None,
         op0=ALU.mult, tag="wssc")
    wsq_bd = p.sbuf("wsq_bd", [128, 4 * 16], BF16)
    wso_bd = p.sbuf("wso_bd", [128, 4 * 8], BF16)
    p.memset(wsq_bd[:], 0.0, eng="pool")
    p.memset(wso_bd[:], 0.0, eng="pool")
    for ch in range(4):
        for hh in range(4):
            h = 4 * ch + hh
            p.copy(wsq_bd[hh * 32:(hh + 1) * 32, ch * 16 + h - 4 * ch + 0:
                          ch * 16 + hh + 1] if False else
                   wsq_bd[hh * 32:(hh + 1) * 32, ch * 16 + hh:ch * 16 + hh + 1],
                   ws_col[0:32, 1:2], eng="pool", tag="bdq")
    for ch in range(4):
        for rr in range(2):
            p.copy(wso_bd[rr * 64:(rr + 1) * 64,
                          ch * 8 + 2 * ch + rr:ch * 8 + 2 * ch + rr + 1]
                   if False else
                   wso_bd[rr * 64:(rr + 1) * 64, ch * 8 + rr:ch * 8 + rr + 1],
                   ws_col[0:64, 0:1], eng="pool", tag="bdo")
    p.memset(wsq_bd[:], 0.0, eng="pool")
    p.memset(wso_bd[:], 0.0, eng="pool")
    for ch in range(4):
        for hh in range(4):
            h = 4 * ch + hh
            p.copy(wsq_bd[hh * 32:(hh + 1) * 32, ch * 16 + h:ch * 16 + h + 1],
                   ws_col[0:32, 1:2], eng="pool", tag="bdq2")
    for ch in range(4):
        for rr in range(2):
            r = 2 * ch + rr
            p.copy(wso_bd[rr * 64:(rr + 1) * 64, ch * 8 + r:ch * 8 + r + 1],
                   ws_col[0:64, 0:1], eng="pool", tag="bdo2")
    ones_r16 = p.sbuf("ones_r16", [1, 16], F32)
    p.memset(ones_r16[:], 1.0, eng="pool")
    bsps = cm.bcast_psum[0:16, 512:513]
    p.mm(bsps, ones_r16[0:1, 0:16], bs_row[0:1, 0:1], tag="bsbc")
    bs_col = p.sbuf("bs_col", [16, 1], F32)
    p.copy(bs_col[0:16, 0:1], bsps, eng="dve", tag="bscp")

    vaug_sb = p.sbuf("vaug_sb", [128, 16 * 528], BF16)
    p.memset(vaug_sb[:], 0.0, eng="pool")
    onecol = p.sbuf("onecol", [128, 1], BF16)
    p.memset(onecol[:], 1.0, eng="pool")
    for tc in range(16):
        p.copy(vaug_sb[:, tc * 528 + 520:tc * 528 + 521], onecol[:, 0:1],
               eng="pool", tag="vone")
    vf_sb = p.sbuf("vf_sb", [128, 4 * T], BF16)
    qvf = p.sbuf("qvf", [128, 512], BF16)
    sqv_st = p.sbuf("sqv_st", [16, 1024], F32)
    aux_ps = cm.bcast_psum[0:16, 0:512]
    vws_sb = p.sbuf("vws_sb", [8, 512], BF16)
    ptr_bf = [p.psum("ptr%d" % i, [128, 128], BF16) for i in range(2)]

    for c in range(4):
        for it in range(T // 512):
            t0 = it * 512
            ps = pp[it % 2][:, 0:512]
            for ci in range(NCH):
                p.mm(ps, wvT[ci][:, c * 128:(c + 1) * 128],
                     xn_bf[:, ci * T + t0: ci * T + t0 + 512],
                     start=(ci == 0), stop=(ci == NCH - 1), tag="v_mm")
            p.ts(vf_sb[:, c * T + t0:c * T + t0 + 512], ps, bv[c], None,
                 op0=ALU.add, tag="v_ev")
    for it in range(T // 512):
        t0 = it * 512
        for ci in range(4):
            p.mm(cm.bcast_psum[0:8, 0:512], wso_bd[:, ci * 8:(ci + 1) * 8],
                 vf_sb[:, ci * T + t0:ci * T + t0 + 512],
                 start=(ci == 0), stop=(ci == 3), tag="vws_mm")
        p.copy(vws_sb[0:8, 0:512], cm.bcast_psum[0:8, 0:512], eng="act", tag="vws_cp")
        for tsb in range(4):
            tc = it * 4 + tsb
            tp = ptr_bf[tsb % 2]
            p.transpose(tp[0:128, 0:8],
                        vws_sb[0:8, tsb * 128:(tsb + 1) * 128],
                        identb[0:8, 0:8], tag="vws_tr")
            p.copy(vaug_sb[:, tc * 528 + 512:tc * 528 + 520],
                   tp[0:128, 0:8], eng="dve", tag="vws_vcp")
            for ci in range(4):
                tpv = ptr_bf[(tsb + ci) % 2]
                p.transpose(tpv[0:128, 0:128],
                            vf_sb[:, ci * T + t0 + tsb * 128:
                                  ci * T + t0 + (tsb + 1) * 128],
                            identb[:], tag="v_tr")
                p.copy(vaug_sb[:, tc * 528 + ci * 128:tc * 528 + (ci + 1) * 128],
                       tpv[0:128, 0:128], eng="dve", tag="v_vcp")
    for tc in range(16):
        p.dma(vaug_d[tc * 128:(tc + 1) * 128, :],
              vaug_sb[:, tc * 528:(tc + 1) * 528], q="actq", tag="va_out")
    for it in range(T // 512):
        t0 = it * 512
        nst = sqv_st[0:16, (it % 2) * 512:(it % 2) * 512 + 512]
        for c in range(4):
            ps = pp[c % 2][:, 0:512]
            for ci in range(NCH):
                p.mm(ps, wqvT[ci][:, c * 128:(c + 1) * 128],
                     xn_bf[:, ci * T + t0: ci * T + t0 + 512],
                     start=(ci == 0), stop=(ci == NCH - 1), tag="qv_mm")
            p.ts(qvf[:, 0:512], ps, bqv[c], None, op0=ALU.add, tag="qv_ev")
            p.mm(aux_ps, wsq_bd[:, c * 16:(c + 1) * 16],
                 qvf[:, 0:512], tag="sqv_mm")
            if c == 0:
                p.act(nst, aux_ps, AF.Copy, tag="sqv_c0")
            else:
                p.tt(nst, nst, aux_ps, op=ALU.add, tag="sqv_acc")
        p.ts(nst, nst, bs_col[0:16, 0:1], None, op0=ALU.add, tag="sqv_bs")
        p.dma(sqv_d[:, t0:t0 + 512], nst, q="spq", tag="sqv_out")
    return p


def build_l0a_flash(T=TLOC, p=None):
    p = p or P("l0a_flash")
    qf_d = p.dram_in("qf", [D, T], BF16)
    Kg_d = p.dram_in("Kg", [4 * D, T], BF16)
    VA_d = p.dram_in("VA", [4 * T, 528], BF16)
    qn_d = p.dram_in("qn", [16, T])
    KN_d = p.dram_in("KN", [16, 4 * T])
    sqv_d = p.dram_in("sqv", [16, T])
    af_d = p.dram_out("attnf", [D, T], BF16)

    SEQT = 4 * T
    vaug = p.sbuf("vaug", [128, 64 * 528], BF16)
    p.dma(vaug.rearrange("p (j n) -> p j n", n=528),
          VA_d.rearrange("(j p) n -> p j n", p=128), q="poolq", tag="vld")
    kaug = p.sbuf("kaug", [65, SEQT], BF16)
    p.memset(kaug[64:65, :], 1.0, eng="pool")
    qaug = p.sbuf("qaug", [65, T], BF16)
    p.memset(qaug[64:65, :], 0.0, eng="pool")
    qn_sb = p.sbuf("qn_sb", [16, T], F32)
    p.dma(qn_sb[0:16, :], qn_d, q="spq", tag="qnld")
    sqv_sb = p.sbuf("sqv_sb", [16, T], F32)
    p.dma(sqv_sb[0:16, :], sqv_d, q="spq", tag="sqld")
    sqv_row = p.sbuf("sqv_row", [1, T], F32)
    sqv_cols = p.sbuf("sqv_cols", [128, 16], F32)
    knt = p.sbuf("knt", [16, T], F32)
    kmax = p.sbuf("kmax", [16, 8], F32)
    for s in range(4):
        p.dma(knt[0:16, :], KN_d[:, s * T:(s + 1) * T], q="spq", tag="knld")
        p.red(kmax[0:16, s:s + 1], knt[0:16, :], op=ALU.max, tag="knmax")
    p.red(kmax[0:16, 4:5], kmax[0:16, 0:4], op=ALU.max, tag="knmax2")
    mneg = p.sbuf("mneg", [16, T], F32)
    p.ts(mneg[0:16, :], qn_sb[0:16, :], kmax[0:16, 4:5], -1.0,
         op0=ALU.mult, op1=ALU.mult, tag="mneg")

    identf = make_identity(p, "identf", F32)
    PT = p.sbuf("PT", [128, 2 * 512], BF16)
    psS = [p.psum("psS%d" % i, [128, 512], F32) for i in range(2)]
    psO = [p.psum("psO%d" % i, [128, 512], F32) for i in range(4)]
    psaux = p.psum("psaux", [128, 128], F32)
    oti = p.psum("oti", [64, 128], F32)
    comp = p.sbuf("comp", [128, 8], F32)
    ecp = p.sbuf("ecp", [128, 8], F32)
    ccol = p.sbuf("ccol", [128, 4], F32)
    otm = p.sbuf("otm", [128, 64], F32)
    attnst = p.sbuf("attnst", [64, 2 * 128], BF16)

    for h in range(16):
        p.dma(sqv_row[0:1, :], sqv_sb[h:h + 1, :], q="poolq", tag="sqrow")
        for sb_ in range(16):
            p.transpose(psaux[0:128, 0:1],
                        sqv_row[0:1, sb_ * 128:(sb_ + 1) * 128],
                        identf[0:1, 0:1], tag="sqtr")
            p.copy(sqv_cols[:, sb_:sb_ + 1], psaux[0:128, 0:1], eng="dve",
                   tag="sqcp")
        for s in range(4):
            p.dma(kaug[0:64, s * T:(s + 1) * T],
                  Kg_d[s * D + h * 64: s * D + (h + 1) * 64, :], q="spq",
                  tag="kld")
        p.dma(qaug[0:64, :],
              qf_d[(h // 2) * 128 + (h % 2) * 64:
                   (h // 2) * 128 + (h % 2) * 64 + 64, :], q="spq", tag="qld2")
        p.dma(qaug[64:65, :], mneg[h:h + 1, :], q="poolq", tag="qmn")
        for ib in range(4):
            i0 = ib * 512
            njt = SEQT // 128
            for gj in range(njt):
                ps = psS[gj % 2][:, 0:512]
                p.mm(ps, kaug[:, gj * 128:(gj + 1) * 128],
                     qaug[:, i0:i0 + 512], tag="s_mm")
                pt = PT[:, (gj % 2) * 512:(gj % 2) * 512 + 512]
                p.act(pt, ps, AF.Exp, tag="s_exp")
                first = (gj == 0)
                last = (gj == njt - 1)
                for ic in range(4):
                    p.mm(psO[ic][:, 0:512], pt[:, ic * 128:(ic + 1) * 128],
                         vaug[:, gj * 528:gj * 528 + 512],
                         start=first, stop=last, tag="av_mm")
                    p.mm(psaux[:, ic * 32:ic * 32 + 9],
                         pt[:, ic * 128:(ic + 1) * 128],
                         vaug[:, gj * 528 + 512:gj * 528 + 521],
                         start=(first and ic == 0), stop=last, tag="aux_mm")
            for ic in range(4):
                isub = ib * 4 + ic
                p.reciprocal(ccol[:, 0:1], psaux[:, ic * 32 + 8:ic * 32 + 9],
                             tag="linv")
                p.ts(comp[:, 0:8], psaux[:, ic * 32:ic * 32 + 8],
                     ccol[:, 0:1], sqv_cols[:, isub:isub + 1],
                     op0=ALU.mult, op1=ALU.add, tag="comp")
                p.red(ccol[:, 1:2], comp[:, 0:8], op=ALU.max, negate=True,
                      tag="rmax")
                p.act(ecp[:, 0:8], comp[:, 0:8], AF.Exp, bias=ccol[:, 1:2],
                      tag="rexp")
                p.red(ccol[:, 2:3], ecp[:, 0:8], op=ALU.add, tag="rsum")
                p.reciprocal(ccol[:, 3:4], ccol[:, 2:3], tag="rrcp")
                p.ts(ecp[:, 0:8], ecp[:, 0:8], ccol[:, 3:4], ccol[:, 0:1],
                     op0=ALU.mult, op1=ALU.mult, tag="aw")
                for r in range(8):
                    if r == 0:
                        p.ts(otm[:, 0:64], psO[ic][:, 0:64], ecp[:, 0:1],
                             None, op0=ALU.mult, tag="osum0")
                    else:
                        p.stt(otm[:, 0:64], psO[ic][:, r * 64:(r + 1) * 64],
                              ecp[:, r:r + 1], otm[:, 0:64],
                              op0=ALU.mult, op1=ALU.add, tag="osum")
                p.transpose(oti[0:64, 0:128], otm[:, 0:64], identf[:],
                            tag="otr")
                ast_ = attnst[0:64, (ic % 2) * 128:(ic % 2) * 128 + 128]
                p.copy(ast_, oti[0:64, 0:128], eng="act", tag="ocp")
                p.dma(af_d[h * 64:(h + 1) * 64, isub * 128:(isub + 1) * 128],
                      ast_, q="actq", tag="aout")
    return p


def build_l0a_mid(T=TLOC, p=None):
    p = p or P("l0a_mid")
    cm = Common(p, T, slim=True)
    xs_d = p.dram_in("xs", [D, T])
    u_d = p.dram_in("u", [D, T])
    af_d = p.dram_in("attnf", [D, T], BF16)
    ys_d = p.dram_out("ys", [D, T])
    A = Arena(p, "A", 24576)
    wfT_d = p.dram_in("wfT", [D, D])
    wr1T_d = p.dram_in("wr1T", [D, 2048])
    wr2T_d = p.dram_in("wr2T", [2048, D])
    bf_c = load_col_f32(p, p.dram_in("bf", [D]), D, "bfc")
    g2 = load_col_f32(p, p.dram_in("g2", [D]), D, "g2")
    b2 = load_col_f32(p, p.dram_in("b2", [D]), D, "b2")
    br1 = load_col_f32(p, p.dram_in("br1", [2048]), 2048, "br1")
    br2 = load_col_f32(p, p.dram_in("br2", [D]), D, "br2")
    wfT = load_wT_bf16(p, wfT_d, D, D, "wf", region=A.bf16(8192, NCH * D))
    af_sb = A.bf16(0, NCH * T)
    for c in range(NCH):
        p.dma(af_sb[:, c * T:(c + 1) * T], af_d[c * 128:(c + 1) * 128, :],
              q="spq", tag="afld")
    out1 = p.sbuf("out1", [128, NCH * T], F32)
    ub = p.sbuf("ub", [128, 1024], F32)
    uslot = [0]
    pp = [p.psum("mmp%d" % i, [128, 512], F32) for i in range(3)]

    for c in range(NCH):
        for it in range(T // 512):
            t0 = it * 512
            ps = pp[it % 2][:, 0:512]
            for ci in range(NCH):
                p.mm(ps, wfT[ci][:, c * 128:(c + 1) * 128],
                     af_sb[:, ci * T + t0: ci * T + t0 + 512],
                     start=(ci == 0), stop=(ci == NCH - 1), tag="wf_mm")
            ut = ub[:, (it % 2) * 512:(it % 2) * 512 + 512]
            p.dma(ut, u_d[c * 128:(c + 1) * 128, t0:t0 + 512], q="spq",
                  tag="uld")
            p.ts(out1[:, c * T + t0:c * T + t0 + 512], ps, bf_c[c], None,
                 op0=ALU.add, tag="wf_ev")
            p.tt(out1[:, c * T + t0:c * T + t0 + 512],
                 out1[:, c * T + t0:c * T + t0 + 512], ut, op=ALU.add,
                 tag="wf_u")

    on_bf = A.bf16(0, NCH * T)
    wr1T = load_wT_bf16(p, wr1T_d, D, 2048, "wr1",
                        region=A.bf16(8192, NCH * 2048))
    wr2T = load_wT_bf16(p, wr2T_d, 2048, D, "wr2",
                        region=A.bf16(16384, 16 * D))

    def o_tile(c, it):
        return out1[:, c * T + it * 512: c * T + (it + 1) * 512]

    def emit_on(c, it, sc):
        p.stt(on_bf[:, c * T + it * 512: c * T + (it + 1) * 512], sc, g2[c],
              b2[c].broadcast_to([128, 512]), op0=ALU.mult, op1=ALU.add,
              tag="onaff")
    ln_streaming(p, cm, T, o_tile, emit_on, "ln3_")

    h1 = p.sbuf("h1", [128, 16 * 512], BF16)
    xb = p.sbuf("xb", [128, 1024], F32)
    rb = p.sbuf("rb", [128, 1024], F32)
    for it in range(T // 512):
        t0 = it * 512
        for fo in range(16):
            ps = pp[fo % 2][:, 0:512]
            for ci in range(NCH):
                p.mm(ps, wr1T[ci][:, fo * 128:(fo + 1) * 128],
                     on_bf[:, ci * T + t0: ci * T + t0 + 512],
                     start=(ci == 0), stop=(ci == NCH - 1), tag="r1_mm")
            p.ts(h1[:, fo * 512:(fo + 1) * 512], ps, br1[fo], 0.0,
                 op0=ALU.add, op1=ALU.max, tag="r1_ev")
        for fo in range(NCH):
            ps = pp[2][:, 0:512]
            for ci in range(16):
                p.mm(ps, wr2T[ci][:, fo * 128:(fo + 1) * 128],
                     h1[:, ci * 512:(ci + 1) * 512],
                     start=(ci == 0), stop=(ci == 15), tag="r2_mm")
            xt = xb[:, (it % 2) * 512:(it % 2) * 512 + 512]
            rt = rb[:, (it % 2) * 512:(it % 2) * 512 + 512]
            p.dma(xt, xs_d[fo * 128:(fo + 1) * 128, t0:t0 + 512], q="spq",
                  tag="xld")
            p.ts(rt, ps, br2[fo], None, op0=ALU.add, tag="r2_ev")
            p.tt(rt, rt, out1[:, fo * T + t0:fo * T + t0 + 512], op=ALU.add,
                 tag="r2_o1")
            p.tt(rt, rt, xt, op=ALU.add, tag="r2_x")
            p.dma(ys_d[fo * 128:(fo + 1) * 128, t0:t0 + 512], rt, q="actq",
                  tag="yout")
    return p



def build_l3a_pre(T=TLOC, p=None, fused=False):
    p = p or P("l3a_pre")
    cm = Common(p, T, slim=True)
    xs_d = p.dram_in("xs", [D, T])
    winT = load_wT_bf16(p, p.dram_in("winT", [D, D]), D, D, "win")
    wkvT = load_wT_bf16(p, p.dram_in("wkvT", [D, 256]), D, 256, "wkv")
    wbT = load_wT_bf16(p, p.dram_in("wbT", [D, 128]), D, 128, "wb")
    wqrT = load_wT_bf16(p, p.dram_in("wqrT", [D, 128]), D, 128, "wqr")
    wrtT = load_wT_f32(p, p.dram_in("wrtT", [D, 4]), D, 4, "wrt")
    bbeta = load_col_f32(p, p.dram_in("bbeta", [128]), 128, "bb")
    kn_d = p.dram_out("kn", [128, T])
    v_d = p.dram_out("v", [128, T])
    be_d = p.dram_out("be", [128, T])
    qr_d = p.dram_out("qr", [128, T])
    uw_d = p.dram_out("uw", [128, T])
    sm_d = p.dram_out("sm", [128, T])

    xn_bf = p.sbuf("xn_bf", [128, NCH * T], BF16)
    xbuf = p.sbuf("xbuf", [128, 1024], F32)
    xslot = [0]

    def x_tile(c, it):
        s = xslot[0] % 2
        xslot[0] += 1
        sl = xbuf[:, s * 512:(s + 1) * 512]
        p.dma(sl, xs_d[c * 128:(c + 1) * 128, it * 512:(it + 1) * 512],
              q="spq", tag="xt")
        return sl

    def emit_xn(c, it, sc):
        p.copy(xn_bf[:, c * T + it * 512: c * T + (it + 1) * 512], sc,
               eng="pool", tag="xnbf")
    ln_streaming(p, cm, T, x_tile, emit_xn, "ln_")

    pp = [p.psum("mmp%d" % i, [128, 512], F32) for i in range(3)]
    identf = make_identity(p, "identf", F32)
    xp_bf = p.sbuf("xp_bf", [128, NCH * T], BF16)
    xpf = p.sbuf("xpf", [128, 512], F32)
    rlog = pp[2][:, 0:16]
    rlog_sb = p.sbuf("rlog_sb", [128, 16 * (T // 512)], F32)
    sc8 = p.sbuf("sc8", [128, 8], F32)
    m8 = p.sbuf("m8", [128, 8], F32)
    mcol = p.sbuf("mcol", [128, 4], F32)
    uwtm = p.sbuf("uwtm", [128, 4], F32)
    smtm = p.sbuf("smtm", [128, 4], F32)
    sm4 = p.sbuf("sm4", [4, T], F32)
    uw4 = p.sbuf("uw4", [4, T], F32)
    p.memset(sc8[:, 4:8], -2.0, eng="pool")
    tps = cm.bcast_psum[0:4, 0:256]

    for c in range(NCH):
        for it in range(T // 512):
            t0 = it * 512
            ps = pp[it % 2][:, 0:512]
            for ci in range(NCH):
                p.mm(ps, winT[ci][:, c * 128:(c + 1) * 128],
                     xn_bf[:, ci * T + t0: ci * T + t0 + 512],
                     start=(ci == 0), stop=(ci == NCH - 1), tag="xp_mm")
            p.act(xpf[:, 0:512], ps, AF.Silu, tag="xp_silu")
            p.copy(xp_bf[:, c * T + t0:c * T + t0 + 512], xpf[:, 0:512],
                   eng="pool", tag="xp_bf")
            rsb = rlog_sb[:, it * 16:(it + 1) * 16]
            if c == 0:
                p.memset(rsb, 0.0, eng="dve", tag="rt_z")
            for sub in range(4):
                p.mm(rlog[:, sub * 4:(sub + 1) * 4],
                     xpf[:, sub * 128:(sub + 1) * 128], wrtT[c][:, 0:4],
                     start=True, stop=True, tag="rt_mm")
            p.tt(rsb, rsb, rlog[:, 0:16], op=ALU.add,
                 tag="rt_acc")
            if c == NCH - 1:
                for sub in range(4):
                    s_ = it * 4 + sub
                    rsl = rlog_sb[:, it * 16 + sub * 4:it * 16 + sub * 4 + 4]
                    p.red(mcol[:, 0:1], rsl, op=ALU.max, negate=True,
                          tag="r_max")
                    p.act(smtm[:, 0:4], rsl, AF.Exp, bias=mcol[:, 0:1],
                          tag="r_exp")
                    p.red(mcol[:, 1:2], smtm[:, 0:4], op=ALU.add, tag="r_sum")
                    p.reciprocal(mcol[:, 2:3], mcol[:, 1:2], tag="r_rcp")
                    p.ts(smtm[:, 0:4], smtm[:, 0:4], mcol[:, 2:3], None,
                         op0=ALU.mult, tag="r_nrm")
                    p.copy(sc8[:, 0:4], smtm[:, 0:4], eng="dve", tag="r_cp8")
                    p.top8(m8[:, 0:8], sc8[:, 0:8], tag="r_top")
                    p.tt(mcol[:, 0:1], m8[:, 0:1], m8[:, 1:2], op=ALU.add,
                         tag="r_ws")
                    p.ts(mcol[:, 0:1], mcol[:, 0:1], 1e-8, None, op0=ALU.add,
                         tag="r_eps")
                    p.reciprocal(mcol[:, 1:2], mcol[:, 0:1], tag="r_rcp2")
                    p.ts(uwtm[:, 0:4], smtm[:, 0:4], m8[:, 1:2], None,
                         op0=ALU.is_ge, tag="r_msk")
                    p.tt(uwtm[:, 0:4], uwtm[:, 0:4], smtm[:, 0:4],
                         op=ALU.mult, tag="r_cw")
                    p.ts(uwtm[:, 0:4], uwtm[:, 0:4], mcol[:, 1:2], None,
                         op0=ALU.mult, tag="r_cw2")
                    p.transpose(tps[0:4, 0:128], smtm[:, 0:4], identf[:],
                                tag="r_trs")
                    p.copy(sm4[0:4, s_ * 128:(s_ + 1) * 128], tps[0:4, 0:128],
                           eng="dve", tag="r_cps")
                    p.transpose(tps[0:4, 128:256], uwtm[:, 0:4], identf[:],
                                tag="r_tru")
                    p.copy(uw4[0:4, s_ * 128:(s_ + 1) * 128],
                           tps[0:4, 128:256], eng="dve", tag="r_cpu")

    kf = p.sbuf("kf", [128, T], F32)
    for (wt, nout, dst, post) in ((wkvT, 256, None, "kv"),
                                  (wbT, 128, be_d, "be"),
                                  (wqrT, 128, qr_d, "qr")):
        nch_o = nout // 128
        for c in range(nch_o):
            for it in range(T // 512):
                t0 = it * 512
                ps = pp[it % 2][:, 0:512]
                for ci in range(NCH):
                    p.mm(ps, wt[ci][:, c * 128:(c + 1) * 128],
                         xp_bf[:, ci * T + t0: ci * T + t0 + 512],
                         start=(ci == 0), stop=(ci == NCH - 1),
                         tag=post + "_mm")
                if post == "kv":
                    if c == 0:
                        p.copy(kf[:, t0:t0 + 512], ps, eng="dve", tag="k_cp")
                    else:
                        st = xpf[:, 0:512]
                        p.copy(st, ps, eng="dve", tag="v_cp")
                        p.dma(v_d[:, t0:t0 + 512], st, q="actq", tag="v_out")
                elif post == "be":
                    st = xpf[:, 0:512]
                    p.act(st, ps, AF.Sigmoid, bias=bbeta[0], tag="be_sig")
                    p.dma(be_d[:, t0:t0 + 512], st, q="actq", tag="be_out")
                else:
                    st = xpf[:, 0:512]
                    p.copy(st, ps, eng="dve", tag="qr_cp")
                    p.dma(qr_d[:, t0:t0 + 512], st, q="actq", tag="qr_out")

    E128 = p.sbuf("E128", [128, 128], F32)
    p.memset(E128[:], 0.0, eng="pool")
    onec = p.sbuf("onec", [128, 1], BF16)
    p.memset(onec[:], 1.0, eng="pool")
    for blk in range(4):
        for jj in range(32):
            pass
    it4 = p.sbuf("it4", [128, 128], F32)
    p.iota(it4[:], [[1, 128]], base=0, channel_multiplier=0, tag="e_i")
    itp = p.sbuf("itp", [128, 1], F32)
    p.iota(itp[:], [[0, 1]], base=0, channel_multiplier=1, tag="e_p")
    p.ts(it4[:], it4[:], 1.0 / 32.0, -0.499, op0=ALU.mult, op1=ALU.add,
         tag="e_s")
    p.ts(it4[:], it4[:], MAGIC, None, op0=ALU.add, tag="e_m1")
    p.ts(it4[:], it4[:], -MAGIC, None, op0=ALU.add, tag="e_m2")
    p.ts(itp[:], itp[:], 1.0 / 32.0, -0.499, op0=ALU.mult, op1=ALU.add,
         tag="e_sp")
    p.ts(itp[:], itp[:], MAGIC, None, op0=ALU.add, tag="e_m1p")
    p.ts(itp[:], itp[:], -MAGIC, None, op0=ALU.add, tag="e_m2p")
    p.ts(E128[:], it4[:], itp[:, 0:1], None, op0=ALU.is_equal, tag="e_eq")
    ksq = p.sbuf("ksq", [128, 512], F32)
    if fused:
        kq_d = p.dram_out("kq", [128, 2 * T])
        qrt = p.sbuf("qrt", [128, 1024], F32)
        kqt = p.sbuf("kqt", [128, 2048], F32)
    for it in range(T // 512):
        t0 = it * 512
        p.tt(ksq[:, 0:512], kf[:, t0:t0 + 512], kf[:, t0:t0 + 512],
             op=ALU.mult, eng="pool", tag="kn_sq")
        ps = pp[2][:, 0:512]
        p.mm(ps, E128[:], ksq[:, 0:512], tag="kn_mm")
        p.act(ksq[:, 0:512], ps, AF.Sqrt, tag="kn_rt")
        p.ts(ksq[:, 0:512], ksq[:, 0:512], 1e-6, None, op0=ALU.add,
             tag="kn_eps")
        p.reciprocal(ksq[:, 0:512], ksq[:, 0:512], tag="kn_rcp")
        p.tt(ksq[:, 0:512], ksq[:, 0:512], kf[:, t0:t0 + 512], op=ALU.mult,
             tag="kn_mul")
        if fused:
            s_ = it % 2
            qsl = qrt[:, s_ * 512:(s_ + 1) * 512]
            p.dma(qsl, qr_d[:, t0:t0 + 512], q="spq", tag="qr_ld")
            p.copy(bass.AP(kqt, s_ * 1024, [[2048, 128], [2, 512]]),
                   ksq[:, 0:512], eng="dve", tag="kq_i0")
            p.copy(bass.AP(kqt, s_ * 1024 + 1, [[2048, 128], [2, 512]]),
                   qsl, eng="dve", tag="kq_i1")
            p.dma(kq_d[:, 2 * t0:2 * t0 + 1024],
                  kqt[:, s_ * 1024:s_ * 1024 + 1024], q="actq", tag="kq_out")
        else:
            p.dma(kn_d[:, t0:t0 + 512], ksq[:, 0:512], q="actq", tag="kn_out")

    E4 = p.sbuf("E4", [4, 128], F32)
    it4b = p.sbuf("it4b", [4, 128], F32)
    p.iota(it4b[0:4, :], [[1, 128]], base=0, channel_multiplier=0, tag="e4i")
    itpb = p.sbuf("itpb", [4, 1], F32)
    p.iota(itpb[0:4, :], [[0, 1]], base=0, channel_multiplier=32, tag="e4p")
    p.ts(it4b[0:4, :], it4b[0:4, :], 1.0 / 32.0, -0.499, op0=ALU.mult,
         op1=ALU.add, tag="e4s")
    p.ts(it4b[0:4, :], it4b[0:4, :], MAGIC, None, op0=ALU.add, tag="e4m1")
    p.ts(it4b[0:4, :], it4b[0:4, :], -MAGIC, None, op0=ALU.add, tag="e4m2")
    p.ts(itpb[0:4, :], itpb[0:4, :], 1.0 / 32.0, None, op0=ALU.mult,
         tag="e4sp")
    p.ts(E4[0:4, :], it4b[0:4, :], itpb[0:4, 0:1], None, op0=ALU.is_equal,
         tag="e4eq")
    for (src, dst) in ((sm4, sm_d), (uw4, uw_d)):
        for it in range(T // 512):
            t0 = it * 512
            ps = pp[2][:, 0:512]
            p.mm(ps, E4[0:4, :], src[0:4, t0:t0 + 512], tag="b4_mm")
            st = ksq[:, 0:512]
            p.copy(st, ps, eng="dve", tag="b4_cp")
            p.dma(dst[:, t0:t0 + 512], st, q="actq", tag="b4_out")
    return p


def build_l3a_scan(TT_=SEQ, debug=False, p=None):
    p = p or P("l3a_scan")
    NC_ = min(4096, TT_)
    NCHK = TT_ // NC_
    kq_d = p.dram_in("kqA", [128, 2 * TT_])
    v_d = p.dram_in("vA", [128, TT_])
    be_d = p.dram_in("beA", [128, TT_])
    uw_d = p.dram_in("uwA", [128, TT_])
    sm_d = p.dram_in("smA", [128, TT_])
    o32_d = p.dram_out("o32", [32, TT_])

    identf = make_identity(p, "identf", F32)
    ones32 = p.sbuf("ones32", [1, 32], F32)
    p.memset(ones32[:], 1.0, eng="pool")
    ones128 = p.sbuf("ones128", [1, 128], F32)
    p.memset(ones128[:], 1.0, eng="pool")
    S = p.sbuf("S", [128, 32], F32)
    p.memset(S[:], 0.0, eng="dve")
    bulk = p.sbuf("bulk", [128, 6 * NC_], F32)
    outbuf = p.sbuf("outbuf", [128, TT_], F32)
    row_sb = p.sbuf("row_sb", [1, 256], F32)
    kq_st = p.sbuf("kq_st", [128, 2], F32)
    scr32 = p.sbuf("scr32", [128, 32], F32)
    scr32b = p.sbuf("scr32b", [128, 32], F32)
    pre32 = p.sbuf("pre32", [128, 32], F32)
    upd32 = p.sbuf("upd32", [128, 32], F32)
    d32 = p.sbuf("d32", [128, 32], F32)
    cols = p.sbuf("cols", [128, 8], F32)
    rcol = cols[:, 0:1]
    delta = cols[:, 1:2]
    Sq = cols[:, 2:3]
    sg = cols[:, 3:4]
    bo = cols[:, 4:5]
    psR = p.psum("psR", [1, 256], F32)
    psC = p.psum("psC", [128, 32], F32)
    psD = p.psum("psD", [128, 32], F32)

    lpe = [p.es.enter_context(p.nc.semaphore("lpe%d" % c))
           for c in range(NCHK)]
    ldve = [p.es.enter_context(p.nc.semaphore("ldve%d" % c))
            for c in range(NCHK)]
    lact = [p.es.enter_context(p.nc.semaphore("lact%d" % c))
            for c in range(NCHK)]
    ldma = [p.es.enter_context(p.nc.semaphore("ldma%d" % c))
            for c in range(NCHK)]
    K0 = 16
    P_, V_, A_ = 7, 7, 3

    def colv(ofs_nc, i):
        return bass.AP(bulk, ofs_nc * NC_ + i, [[6 * NC_, 128], [1, 1]])

    def kqcols(i):
        return bass.AP(bulk, 2 * i, [[6 * NC_, 128], [1, 2]])

    def obcol(c, i):
        return bass.AP(outbuf, c * NC_ + i, [[TT_, 128], [1, 1]])

    pre_counts = dict(p.counts)

    def base_waits(eng, sems, skip=None):
        for s in COMPUTE_STREAMS + DMA_STREAMS:
            if s == skip:
                continue
            n = pre_counts[s] * (16 if s in DMA_STREAMS else 1)
            if n > 0:
                eng.wait_ge(sems[s], n)

    def fn_act(eng, sems):
        base_waits(eng, sems, skip="act")
        for c in range(NCHK):
            eng.sem_inc(lact[c], K0)
        for c in range(NCHK):
            if c > 0:
                eng.wait_ge(ldve[c - 1], K0 + V_ * NC_)
                eng.wait_ge(lact[c - 1], K0 + A_ * NC_)
            eng.dma_start(out=bulk[:, 0:2 * NC_],
                          in_=kq_d[:, c * 2 * NC_:(c + 1) * 2 * NC_]
                          ).then_inc(ldma[c], 16)
            for bi, dr in enumerate((v_d, be_d, uw_d)):
                eng.dma_start(out=bulk[:, (2 + bi) * NC_:(3 + bi) * NC_],
                              in_=dr[:, c * NC_:(c + 1) * NC_]
                              ).then_inc(ldma[c], 16)
            eng.wait_ge(ldma[c], 64)
            with eng.Fori(0, NC_) as i:
                eng.wait_ge(lpe[c], K0 + i * P_ - 5)
                eng.activation(kq_st[:, 0:2], kqcols(i), AF.Copy
                               ).then_inc(lact[c], 1)
                eng.wait_ge(lpe[c], K0 + i * P_ + 2)
                eng.copy(row_sb[0:1, 0:160], psR[0:1, 0:160]
                         ).then_inc(lact[c], 1)
                eng.wait_ge(ldve[c], K0 + i * V_ + 4)
                eng.activation(upd32[:], pre32[:], AF.Tanh
                               ).then_inc(lact[c], 1)
        eng.copy(row_sb[0:1, 160:161], row_sb[0:1, 160:161]
                 ).then_inc(sems["act"], 1)

    def fn_pe(eng, sems):
        base_waits(eng, sems, skip="pe")
        for c in range(NCHK):
            eng.sem_inc(lpe[c], K0)
        for c in range(NCHK):
            with eng.Fori(0, NC_) as i:
                eng.wait_ge(lact[c], K0 + i * A_ + 1)
                eng.transpose(psR[0:1, 0:128], kq_st[:, 0:1], identf[:]
                              ).then_inc(lpe[c], 1)
                eng.transpose(psR[0:1, 128:256], kq_st[:, 1:2], identf[:]
                              ).then_inc(lpe[c], 1)
                eng.wait_ge(lact[c], K0 + i * A_ + 2)
                eng.wait_ge(ldve[c], K0 + i * V_ - 4)
                for b in range(4):
                    eng.matmul(psC[b * 32:(b + 1) * 32, 0:32],
                               ones32[0:1, 0:32],
                               row_sb[0:1, b * 32:(b + 1) * 32],
                               start=True, stop=True,
                               tile_position=(0, b * 32)).then_inc(lpe[c], 1)
                eng.wait_ge(ldve[c], K0 + i * V_)
                eng.matmul(psD[:, 0:32], ones128[0:1, 0:128],
                           row_sb[0:1, 128:160], start=True, stop=True
                           ).then_inc(lpe[c], 1)
        eng.nop().then_inc(sems["pe"], 1)

    def fn_dve(eng, sems):
        base_waits(eng, sems, skip="dve")
        for c in range(NCHK):
            eng.sem_inc(ldve[c], K0)
        for c in range(NCHK):
            eng.wait_ge(ldma[c], 64)
            with eng.Fori(0, NC_) as i:
                eng.wait_ge(lpe[c], K0 + i * P_ + 6)
                eng.wait_ge(ldve[c], K0 + i * V_ - 1)
                eng.scalar_tensor_tensor(scr32[:], S[:], 1.0, psC[:, 0:32],
                                         ALU.mult, ALU.mult, accum_out=rcol
                                         ).then_inc(ldve[c], 1)
                eng.wait_ge(ldve[c], K0 + i * V_ + 1)
                eng.tensor_tensor(delta, colv(2, i), rcol, ALU.subtract
                                  ).then_inc(ldve[c], 1)
                eng.wait_ge(ldve[c], K0 + i * V_ + 2)
                eng.tensor_scalar(scr32b[:], psC[:, 0:32], delta, None,
                                  ALU.mult).then_inc(ldve[c], 1)
                eng.wait_ge(ldve[c], K0 + i * V_ + 3)
                eng.scalar_tensor_tensor(pre32[:], S[:], colv(3, i),
                                         scr32b[:], ALU.mult, ALU.add
                                         ).then_inc(ldve[c], 1)
                eng.wait_ge(lact[c], K0 + i * A_ + 3)
                eng.tensor_tensor(d32[:], upd32[:], S[:], ALU.subtract
                                  ).then_inc(ldve[c], 1)
                eng.wait_ge(ldve[c], K0 + i * V_ + 5)
                eng.scalar_tensor_tensor(S[:], d32[:], colv(4, i), S[:],
                                         ALU.mult, ALU.add
                                         ).then_inc(ldve[c], 1)
                eng.wait_ge(ldve[c], K0 + i * V_ + 6)
                eng.wait_ge(lpe[c], K0 + i * P_ + 7)
                eng.scalar_tensor_tensor(scr32[:], S[:], 1.0, psD[:, 0:32],
                                         ALU.mult, ALU.mult,
                                         accum_out=obcol(c, i)
                                         ).then_inc(ldve[c], 1)
        eng.engine_nop().then_inc(sems["dve"], 1)

    p.rec("act", fn_act, reads=[kq_d, v_d, be_d, uw_d],
          writes=[bulk[:, :], row_sb[0:1, :], kq_st[:, :]], tag="megaact")
    p.rec("pe", fn_pe, reads=[identf[:], ones32[:], ones128[:]], writes=[],
          tag="megape")
    p.rec("dve", fn_dve, reads=[S[:]],
          writes=[S[:], outbuf[:, :]], tag="megadve")

    sqs = p.sbuf("sqs", [128, 512], F32)
    smt = p.sbuf("smt", [128, 1024], F32)
    for it, t0 in enumerate(range(0, TT_, 512)):
        w_ = min(512, TT_ - t0)
        sl = outbuf[:, t0:t0 + w_]
        st = smt[:, (it % 2) * 512:(it % 2) * 512 + w_]
        p.dma(st, sm_d[:, t0:t0 + w_], q="spq", tag="sm_ld")
        p.act(sqs[:, 0:w_], sl, AF.Silu, tag="po_silu")
        p.tt(sqs[:, 0:w_], sqs[:, 0:w_], sl, op=ALU.mult, tag="po_bo")
        p.tt(sl, sqs[:, 0:w_], st, op=ALU.mult, tag="po_sm")

    I4 = p.sbuf("I4", [128, 32], F32)
    i4f = p.sbuf("i4f", [128, 32], F32)
    i4p = p.sbuf("i4p", [128, 2], F32)
    p.iota(i4f[:], [[1, 32]], base=0, channel_multiplier=0, tag="i4i")
    p.iota(i4p[:, 0:1], [[0, 1]], base=0, channel_multiplier=1, tag="i4p")
    p.ts(i4p[:, 1:2], i4p[:, 0:1], 1.0 / 32.0, -0.499, op0=ALU.mult,
         op1=ALU.add, tag="i4s")
    p.ts(i4p[:, 1:2], i4p[:, 1:2], MAGIC, None, op0=ALU.add, tag="i4m1")
    p.ts(i4p[:, 1:2], i4p[:, 1:2], -MAGIC, None, op0=ALU.add, tag="i4m2")
    p.stt(i4p[:, 0:1], i4p[:, 1:2], -32.0, i4p[:, 0:1], op0=ALU.mult,
          op1=ALU.add, tag="i4mod")
    p.ts(I4[:], i4f[:], i4p[:, 0:1], None, op0=ALU.is_equal, tag="i4eq")
    if debug:
        dbg = p.dram_out("dbg", [128, 160])
        dsb = p.sbuf("dsb", [128, 160], F32)
        p.copy(dsb[:, 0:32], S[:], eng="dve", tag="dbg")
        p.copy(dsb[:, 32:64], scr32b[:], eng="dve", tag="dbg")
        p.copy(dsb[:, 64:96], pre32[:], eng="dve", tag="dbg")
        p.copy(dsb[:, 96:128], upd32[:], eng="dve", tag="dbg")
        p.copy(dsb[:, 128:136], cols[:, 0:8], eng="dve", tag="dbg")
        p.copy(dsb[:, 136:138], kq_st[:, 0:2], eng="dve", tag="dbg")
        p.copy(dsb[:, 138:160], psC[:, 0:22], eng="dve", tag="dbg")
        p.dma(dbg, dsb[:, 0:160], q="spq", tag="dbgout")
        i4g = p.dram_out("i4g", [128, 32])
        p.dma(i4g, I4[:], q="spq", tag="i4out")
        i4h = p.dram_out("i4h", [128, 34])
        i4hs = p.sbuf("i4hs", [128, 34], F32)
        p.copy(i4hs[:, 0:32], i4f[:], eng="dve", tag="dbg2")
        p.copy(i4hs[:, 32:34], i4p[:, 0:2], eng="dve", tag="dbg2")
        p.dma(i4h, i4hs[:, 0:34], q="spq", tag="i4out2")
        ob_d = p.dram_out("obd", [128, TT_])
        p.dma(ob_d, outbuf[:, 0:TT_], q="spq", tag="dbgout2")
    pso = p.psum("pso", [32, 512], F32)
    ost = p.sbuf("ost", [32, 1024], F32)
    tok = p.sbuf("tok", [1, 4], F32)
    p.memset(tok[0:1, 0:1], 0.0, eng="dve", tag="bar")
    p.memset(tok[0:1, 1:2], 0.0, eng="pool", tag="bar")
    p.copy(tok[0:1, 2:3], tok[0:1, 0:1], eng="act", tag="bar")

    for it, t0 in enumerate(range(0, TT_, 512)):
        w = min(512, TT_ - t0)
        st = ost[0:32, (it % 2) * 512:(it % 2) * 512 + w]
        p.mm(pso[0:32, 0:w], I4[:], outbuf[:, t0:t0 + w], tag="o_red")
        p.copy(st, pso[0:32, 0:w], eng="act", tag="o_cp")
        p.dma(o32_d[:, t0:t0 + w], st, q="spq", tag="o_out")
    return p


def build_l3a_post(T=TLOC, p=None):
    p = p or P("l3a_post")
    cm = Common(p, T, slim=True)
    xs_d = p.dram_in("xs", [D, T])
    o32_d = p.dram_in("o32", [32, T])
    woT = load_wT_f32(p, p.dram_in("woT", [32, D]), 128, 8 * D, "wo",
                      q="spq") if False else None
    woT_d = p.dram_in("woT", [32, D])
    wo_sb = p.sbuf("wo_sb", [32, D], BF16)
    p.dma(wo_sb[0:32, :], woT_d, q="poolq", tag="wold")
    wT = p.dram_in("wT", [D, D])
    ys_d = p.dram_out("ys", [D, T])

    o32_sb = p.sbuf("o32_sb", [32, T], BF16)
    p.dma(o32_sb[0:32, :], o32_d, q="poolq", tag="o32ld")

    x2 = p.sbuf("x2", [128, NCH * T], F32)
    pp = [p.psum("mmp%d" % i, [128, 512], F32) for i in range(3)]
    xbuf = p.sbuf("xbuf", [128, 1024], F32)
    for c in range(NCH):
        for it in range(T // 512):
            t0 = it * 512
            ps = pp[it % 2][:, 0:512]
            p.mm(ps, wo_sb[0:32, c * 128:(c + 1) * 128],
                 o32_sb[0:32, t0:t0 + 512], tag="wo_mm")
            xt = xbuf[:, (it % 2) * 512:(it % 2) * 512 + 512]
            p.dma(xt, xs_d[c * 128:(c + 1) * 128, t0:t0 + 512], q="spq",
                  tag="xld")
            p.tt(x2[:, c * T + t0:c * T + t0 + 512], ps, xt, op=ALU.add,
                 tag="res1")

    wf = p.sbuf("wf", [128, NCH * D], F32)
    wsgn = p.sbuf("wsgn", [128, NCH * D], BF16)
    wtiles = []
    p.dma(wf.rearrange("p (c n) -> p c n", n=D),
          wT.rearrange("(c p) n -> p c n", p=128), q="spq", tag="ldw")
    for c in range(NCH):
        sl_f = wf[:, c * D:(c + 1) * D]
        sl_s = wsgn[:, c * D:(c + 1) * D]
        p.ts(sl_s, sl_f, 0.0, None, op0=ALU.is_ge, tag="sign")
        p.ts(sl_s, sl_s, 2.0, -1.0, op0=ALU.mult, op1=ALU.add, tag="sign2")
        wtiles.append(sl_s)
    xn_bf = p.sbuf("xn_bf", [128, NCH * T], BF16)

    def x2_tile(c, it):
        return x2[:, c * T + it * 512: c * T + (it + 1) * 512]

    def emit_xn(c, it, sc):
        p.copy(xn_bf[:, c * T + it * 512: c * T + (it + 1) * 512], sc,
               eng="pool", tag="xnbf")
    ln_streaming(p, cm, T, x2_tile, emit_xn, "ln_")

    def evac(fo, it, ps):
        xsl = x2[:, fo * T + it * 512: fo * T + (it + 1) * 512]
        p.tt(xsl, ps, xsl, op=ALU.add, tag="resid")

    matmul_fm(p, wtiles, xn_bf, D, D, T, pp, evac, "l3f_")
    for c in range(NCH):
        p.dma(ys_d[c * 128:(c + 1) * 128, :], x2[:, c * T:(c + 1) * T],
              q="actq", tag="ststream")
    return p


def build_lnsum(T=TLOC, p=None):
    p = p or P("lnsum")
    cm = Common(p, T, slim=True)
    xs_d = p.dram_in("xs", [D, T])
    sln_d = p.dram_out("sln", [128, NCH])
    acc = p.sbuf("acc", [128, NCH], F32)
    rtmp = p.sbuf("rtmp", [128, 1], F32)
    p.memset(acc[:], 0.0, eng="dve")
    xbuf = p.sbuf("xbuf", [128, 1024], F32)
    xslot = [0]

    def x_tile(c, it):
        s = xslot[0] % 2
        xslot[0] += 1
        sl = xbuf[:, s * 512:(s + 1) * 512]
        p.dma(sl, xs_d[c * 128:(c + 1) * 128, it * 512:(it + 1) * 512],
              q="spq", tag="xt")
        return sl

    def emit_out(c, it, sc):
        p.red(rtmp[:], sc, op=ALU.add, tag="rsum")
        p.tt(acc[:, c:c + 1], acc[:, c:c + 1], rtmp[:], op=ALU.add, tag="racc")

    ln_streaming(p, cm, T, x_tile, emit_out, "ln_")
    p.dma(sln_d, acc[:], q="actq", tag="sout")
    return p



_PROGS = {}


def _prog(name, builder, *args):
    if name not in _PROGS:
        _PROGS[name] = builder(*args) if args else builder(TLOC)
    return _PROGS[name]


def kernel_unfused(**ins):
    asc = np.ascontiguousarray
    gi = lambda k: np.asarray(ins[k])
    x = gi("L0_attn_x")
    xs = [asc(x[c // 4, (c % 4) * TLOC:((c % 4) + 1) * TLOC, :].T)
          for c in range(8)]
    t0col = [np.full(128, (c % 4) * TLOC, np.float32) for c in range(8)]

    in_a = [{"xs": xs[c],
             "wqT": asc(gi("L0_attn_Wq").T), "wkT": asc(gi("L0_attn_Wk").T),
             "wvT": asc(gi("L0_attn_Wv").T), "wqvT": asc(gi("L0_attn_Wqv").T),
             "bq": gi("L0_attn_bq"), "bk": gi("L0_attn_bk"),
             "bv": gi("L0_attn_bv"), "bqv": gi("L0_attn_bqv"),
             "ws": gi("L0_attn_Ws"), "bs": gi("L0_attn_bs"),
             "g1": gi("L0_attn_g1"), "b1": gi("L0_attn_b1")}
            for c in range(8)]
    ra = _prog("l0a_pre", build_l0a_pre).run(in_a)

    in_b = []
    for c in range(8):
        b = c // 4
        g = lambda name, cc: np.asarray(ra.results[cc][name])
        in_b.append({
            "qf": g("qf", c),
            "Kg": np.concatenate([g("kf", b * 4 + p_) for p_ in range(4)], 0),
            "VA": np.concatenate([g("vaug", b * 4 + p_) for p_ in range(4)], 0),
            "qn": g("qn", c),
            "KN": np.concatenate([g("kn", b * 4 + p_) for p_ in range(4)], 1),
            "sqv": g("sqv", c)})
    rb = _prog("l0a_flash", build_l0a_flash).run(in_b)

    in_c = [{"xs": xs[c], "u": np.asarray(ra.results[c]["u"]),
             "attnf": np.asarray(rb.results[c]["attnf"]),
             "wfT": asc(gi("L0_attn_Wf").T), "bf": gi("L0_attn_bf"),
             "g2": gi("L0_attn_g2"), "b2": gi("L0_attn_b2"),
             "wr1T": asc(gi("L0_attn_Wr1").T), "br1": gi("L0_attn_br1"),
             "wr2T": asc(gi("L0_attn_Wr2").T), "br2": gi("L0_attn_br2")}
            for c in range(8)]
    rc = _prog("l0a_mid", build_l0a_mid).run(in_c)
    ys = [asc(np.asarray(rc.results[c]["ys"])) for c in range(8)]

    rg = _prog("lnsum", build_lnsum).run([{"xs": ys[c]} for c in range(8)])
    slns = [np.asarray(rg.results[c]["sln"]).T.reshape(D) for c in range(8)]
    gsums = {b: asc(np.stack([slns[b * 4 + q] for q in range(4)], 1))
             for b in (0, 1)}
    in_f = [{"xs": ys[c], "gsums": gsums[c // 4],
             "cn_w1": gi("L0_ffn_cn_w1"), "cn_w2": gi("L0_ffn_cn_w2"),
             "cn_w3": gi("L0_ffn_cn_w3"), "cn_w4": gi("L0_ffn_cn_w4"),
             "cn_b1": gi("L0_ffn_cn_b1"), "cn_b2": gi("L0_ffn_cn_b2"),
             "cn_b3": gi("L0_ffn_cn_b3"), "cn_b4": gi("L0_ffn_cn_b4"),
             "gc_w1": gi("L0_ffn_gc_w1"), "gc_w2": gi("L0_ffn_gc_w2"),
             "gc_w3": gi("L0_ffn_gc_w3"),
             "gc_b1": gi("L0_ffn_gc_b1"), "gc_b2": gi("L0_ffn_gc_b2"),
             "gc_b3": gi("L0_ffn_gc_b3"),
             "ar": gi("L0_ffn_adaptive_range")} for c in range(8)]
    rf = _prog("l0f", build_l0f).run(in_f)
    ys = [asc(np.asarray(rf.results[c]["ys"])) for c in range(8)]

    in_1a = [{"xs": ys[c], "t0col": t0col[c],
              "nu_log": gi("L1_attn_nu_log"),
              "theta_log": gi("L1_attn_theta_log"),
              "BreT": asc(gi("L1_attn_B_re").T),
              "BimT": asc(gi("L1_attn_B_im").T)} for c in range(8)]
    r1a = _prog("l1a_a", build_l1a_a).run(in_1a)
    in_1b = []
    for c in range(8):
        b, q = c // 4, c % 4
        Fall = np.zeros((D, 8), np.float32)
        for p_ in range(4):
            Fp = np.asarray(r1a.results[b * 4 + p_]["Fc"])
            Fall[:, 2 * p_] = Fp[:, 0]
            Fall[:, 2 * p_ + 1] = Fp[:, 1]
        a4 = np.full(4, 1e7, np.float32)
        for p_ in range(q):
            a4[p_] = 2048.0 * (q - 1 - p_)
        in_1b.append({
            "xs": ys[c], "xn": np.asarray(r1a.results[c]["xn"]),
            "ure": np.asarray(r1a.results[c]["ure"]),
            "uim": np.asarray(r1a.results[c]["uim"]),
            "Fall": Fall, "a4": a4, "t0col": t0col[c],
            "nu_log": gi("L1_attn_nu_log"),
            "theta_log": gi("L1_attn_theta_log"),
            "D_skip": gi("L1_attn_D_skip"),
            "CreT": asc(gi("L1_attn_C_re").T),
            "CimT": asc(gi("L1_attn_C_im").T)})
    r1b = _prog("l1a_b", build_l1a_b).run(in_1b)
    ys = [asc(np.asarray(r1b.results[c]["ys"])) for c in range(8)]

    in_1f = [{"xs": ys[c], "gate_wT": asc(gi("L1_ffn_gate_w").T),
              "wg": gi("L1_ffn_w_gate"), "wu": gi("L1_ffn_w_up"),
              "wd": gi("L1_ffn_w_down"), "swg": gi("L1_ffn_sw_gate"),
              "swu": gi("L1_ffn_sw_up"), "swd": gi("L1_ffn_sw_down")}
             for c in range(8)]
    r1f = _prog("l1f", build_l1f).run(in_1f)
    ys = [asc(np.asarray(r1f.results[c]["ys"])) for c in range(8)]

    kidx = ((np.arange(128) % 64) // 2).astype(np.float32)
    alt = np.where(np.arange(128) % 2 == 0, -1.0, 1.0).astype(np.float32)
    pswap = np.zeros((128, 128), np.float32)
    for i in range(128):
        pswap[i ^ 1, i] = 1.0
    in_2a = [{"xs": ys[c], "t0col": t0col[c], "kidx": kidx, "alt": alt,
              "pswap": pswap, "wqkvT": asc(gi("L2_attn_w_qkv").T),
              "bqkv": gi("L2_attn_b_qkv")} for c in range(8)]
    r2a = _prog("l2a_pre", build_l2a_pre).run(in_2a)
    in_2b = []
    for c in range(8):
        b, q = c // 4, c % 4
        g = lambda name, cc: np.asarray(r2a.results[cc][name])
        vedl = np.zeros(D, np.asarray(r2a.results[c]["vf"]).dtype)
        vedr = np.zeros(D, vedl.dtype)
        if q > 0:
            vedl[:] = g("vf", c - 1)[:, -1]
        if q < 3:
            vedr[:] = g("vf", c + 1)[:, 0]
        in_2b.append({
            "xs": ys[c], "qf": g("qf", c),
            "Kg": np.concatenate([g("kf", b * 4 + p_) for p_ in range(4)], 0),
            "VA": np.concatenate([g("vaug", b * 4 + p_) for p_ in range(4)], 0),
            "qn": g("qn", c),
            "KN": np.concatenate([g("kn", b * 4 + p_) for p_ in range(4)], 1),
            "vf": g("vf", c), "vedl": vedl, "vedr": vedr,
            "wlep": gi("L2_attn_w_lepe").reshape(D, 3),
            "blep": gi("L2_attn_b_lepe")})
    r2b = _prog("l2a_flash", build_l2a_flash).run(in_2b)
    ys = [asc(np.asarray(r2b.results[c]["ys"])) for c in range(8)]

    in_2f = [{"xs": ys[c], "router_w": gi("L2_ffn_router_w"),
              "router_b": gi("L2_ffn_router_b"),
              "w1": gi("L2_ffn_w1"), "w2": gi("L2_ffn_w2")}
             for c in range(8)]
    r2f = _prog("l2f", build_l2f).run(in_2f)
    ys = [asc(np.asarray(r2f.results[c]["ys"])) for c in range(8)]

    perm_k = [blk * 64 + n for blk in range(4) for n in range(32)]
    perm_v = [blk * 64 + 32 + n for blk in range(4) for n in range(32)]
    wkv = gi("L3_attn_W_kv")
    in_3a = [{"xs": ys[c],
              "winT": asc(gi("L3_attn_W_in").T),
              "wkvT": asc(wkv[perm_k + perm_v].T),
              "wbT": asc(gi("L3_attn_W_beta").T),
              "wqrT": asc(np.tile(gi("L3_attn_W_q"), (4, 1)).T),
              "wrtT": asc(gi("L3_attn_W_router").T),
              "bbeta": gi("L3_attn_b_beta").reshape(-1)} for c in range(8)]
    r3a = _prog("l3a_pre", build_l3a_pre).run(in_3a)
    in_3s = []
    for c in range(8):
        b = c // 4
        g = lambda name, cc: np.asarray(r3a.results[cc][name])
        m = {}
        for nm, key in (("vA", "v"), ("beA", "be"), ("uwA", "uw"),
                        ("smA", "sm")):
            m[nm] = asc(np.concatenate(
                [g(key, b * 4 + p_) for p_ in range(4)], axis=1))
        kn = np.concatenate([g("kn", b * 4 + p_) for p_ in range(4)], axis=1)
        qr = np.concatenate([g("qr", b * 4 + p_) for p_ in range(4)], axis=1)
        kq = np.empty((128, 2 * kn.shape[1]), np.float32)
        kq[:, 0::2] = kn
        kq[:, 1::2] = qr
        m["kqA"] = kq
        in_3s.append(m)
    r3s = _prog("l3a_scan", build_l3a_scan, SEQ).run(in_3s)
    in_3p = []
    for c in range(8):
        q = c % 4
        o32 = np.asarray(r3s.results[c]["o32"])[:, q * TLOC:(q + 1) * TLOC]
        in_3p.append({"xs": ys[c], "o32": asc(o32),
                      "woT": asc(gi("L3_attn_W_out").T),
                      "wT": asc(gi("L3_ffn_weight").T)})
    r3p = _prog("l3a_post", build_l3a_post).run(in_3p)

    outs = [np.asarray(r3p.results[c]["ys"]).T for c in range(8)]
    return np.ascontiguousarray(np.stack(
        [np.concatenate(outs[0:4], 0), np.concatenate(outs[4:8], 0)],
        0).astype(np.float32))



T2 = TLOC


def _fused_exchange(p, ev, ctx, pre0, pre1, post0, post1, reads, writes):
    nc = p.nc
    fl0 = nc.dram_tensor("exfl%d_0" % ev, [1, 1], I32, addr_space="Shared")
    fl1 = nc.dram_tensor("exfl%d_1" % ev, [1, 1], I32, addr_space="Shared")
    exd = p.es.enter_context(nc.semaphore("exd%d" % ev))
    exp_ = p.es.enter_context(nc.semaphore("exp%d" % ev))
    pt, nt, ft, dum = ctx["pt"], ctx["nt"], ctx["ft"], ctx["dum"]
    assert len(pre0) == len(pre1) and len(post0) == len(post1)

    def fn(eng, sems):
        dc = [0]
        with eng.register("xr%d" % ev) as rpar, \
             eng.register("xn%d" % ev) as rn, \
             eng.register("xf%d" % ev) as rfl, \
             eng.register("xc%d" % ev) as rpc:
            eng.reg_load(rpar, pt[:1, :1])
            eng.reg_load(rn, nt[:1, :1])
            with eng.If_eq(rpar, 0):
                for dst, src in pre0:
                    eng.dma_start(out=dst, in_=src).then_inc(exd, 16)
            with eng.Else():
                for dst, src in pre1:
                    eng.dma_start(out=dst, in_=src).then_inc(exd, 16)
            dc[0] += len(pre0)
            eng.wait_ge(exd, 16 * dc[0])
            with eng.If_eq(rpar, 0):
                eng.dma_start(out=fl0.ap(), in_=nt[:]).then_inc(exd, 16)
            with eng.Else():
                eng.dma_start(out=fl1.ap(), in_=nt[:]).then_inc(exd, 16)
            dc[0] += 1
            eng.wait_ge(exd, 16 * dc[0])
            eng.reg_mov(rfl, 1)
            eng.reg_mov(rpc, 0)
            with eng.While(rfl):
                eng.reg_add(rpc, rpc, 16)
                with eng.If_eq(rpar, 0):
                    eng.dma_start(out=ft[:], in_=fl1.ap()).then_inc(exp_, 16)
                with eng.Else():
                    eng.dma_start(out=ft[:], in_=fl0.ap()).then_inc(exp_, 16)
                eng.wait_ge(exp_, rpc)
                eng.reg_load(rfl, ft[:1, :1])
                eng.reg_alu(rfl, rfl, rn, ALU.subtract)
            with eng.If_eq(rpar, 0):
                for dst, src in post0:
                    eng.dma_start(out=dst, in_=src).then_inc(exd, 16)
            with eng.Else():
                for dst, src in post1:
                    eng.dma_start(out=dst, in_=src).then_inc(exd, 16)
            dc[0] += len(post0)
            eng.wait_ge(exd, 16 * dc[0])
        eng.memset(dum[:], 0.0).then_inc(sems["pool"], 1)
    return p.rec("pool", fn, reads=list(reads), writes=list(writes),
                 tag="ex%d" % ev)


def _fused_sidecopy(p, ev, ctx, plan0, plan1, reads, writes):
    exd = p.es.enter_context(p.nc.semaphore("scd%d" % ev))
    pt, dum = ctx["pt"], ctx["dum"]
    assert len(plan0) == len(plan1)

    def fn(eng, sems):
        with eng.register("sr%d" % ev) as rpar:
            eng.reg_load(rpar, pt[:1, :1])
            with eng.If_eq(rpar, 0):
                for dst, src in plan0:
                    eng.dma_start(out=dst, in_=src).then_inc(exd, 16)
            with eng.Else():
                for dst, src in plan1:
                    eng.dma_start(out=dst, in_=src).then_inc(exd, 16)
            eng.wait_ge(exd, 16 * len(plan0))
        eng.memset(dum[:], 0.0).then_inc(sems["pool"], 1)
    return p.rec("pool", fn, reads=list(reads), writes=list(writes),
                 tag="sc%d" % ev)


def build_fused(debug=False):
    p = P("fused")
    nc = p.nc

    xs_in = p.dram_in("xs", [D, 2 * T2])
    cid = p.dram_in("cid", [1, 1], I32)
    non = p.dram_in("nonce", [1, 1], I32)
    out_ys = p.dram_out("ys", [D, 2 * T2])

    W = {}

    def win(name, shape, dtype=F32):
        W[name] = p.dram_in(name, shape, dtype)
        return W[name]

    for nm, sh in (("l0p_wqT", [D, D]), ("l0p_wkT", [D, D]),
                   ("l0p_wvT", [D, 512]), ("l0p_wqvT", [D, 512]),
                   ("l0p_bq", [D]), ("l0p_bk", [D]), ("l0p_bv", [512]),
                   ("l0p_bqv", [512]), ("l0p_ws", [1, 96]), ("l0p_bs", [1]),
                   ("l0p_g1", [D]), ("l0p_b1", [D]),
                   ("l0m_wfT", [D, D]), ("l0m_bf", [D]), ("l0m_g2", [D]),
                   ("l0m_b2", [D]), ("l0m_wr1T", [D, 2048]),
                   ("l0m_br1", [2048]), ("l0m_wr2T", [2048, D]),
                   ("l0m_br2", [D]),
                   ("l0f_cn_w1", [1024, 1024]), ("l0f_cn_w2", [1024, 512]),
                   ("l0f_cn_w3", [512, 256]), ("l0f_cn_w4", [256, 1]),
                   ("l0f_cn_b1", [1024]), ("l0f_cn_b2", [512]),
                   ("l0f_cn_b3", [256]), ("l0f_cn_b4", [1]),
                   ("l0f_gc_w1", [1024, 512]), ("l0f_gc_w2", [512, 256]),
                   ("l0f_gc_w3", [256, 1]), ("l0f_gc_b1", [512]),
                   ("l0f_gc_b2", [256]), ("l0f_gc_b3", [1]),
                   ("l0f_ar", [1]),
                   ("l1_nu_log", [D]), ("l1_theta_log", [D]),
                   ("l1_BreT", [D, D]), ("l1_BimT", [D, D]),
                   ("l1_CreT", [D, D]), ("l1_CimT", [D, D]),
                   ("l1_D_skip", [D]),
                   ("l1f_gate_wT", [1024, 8]), ("l1f_wg", [8, 1024, 1408]),
                   ("l1f_wu", [8, 1024, 1408]), ("l1f_wd", [8, 1408, 1024]),
                   ("l1f_swg", [1024, 2816]), ("l1f_swu", [1024, 2816]),
                   ("l1f_swd", [2816, 1024]),
                   ("l2_wqkvT", [D, 3 * D]), ("l2_bqkv", [3 * D]),
                   ("l2_kidx", [128]), ("l2_alt", [128]),
                   ("l2_pswap", [128, 128]),
                   ("l2_wlep", [D, 3]), ("l2_blep", [D]),
                   ("l2f_router_w", [1024, 8]), ("l2f_router_b", [8]),
                   ("l2f_w1", [8, 1024, 8192]), ("l2f_w2", [8, 4096, 1024]),
                   ("l3_winT", [D, D]), ("l3_wkvT", [D, 256]),
                   ("l3_wbT", [D, 128]), ("l3_wqrT", [D, 128]),
                   ("l3_wrtT", [D, 4]), ("l3_bbeta", [128]),
                   ("l3_woT", [32, D]), ("l3_wT", [D, D]),
                   ("t0A", [128]), ("t0B", [128]), ("a4A", [4]), ("a4B", [4])):
        win(nm, sh)

    def tmp(name, shape, dtype=F32):
        return p.dram_tmp(name, shape, dtype)

    ys = {k: tmp("ys%d" % k, [D, 2 * T2]) for k in range(6)}
    loc = {}
    for i, I in enumerate("AB"):
        loc["u" + I] = tmp("u" + I, [D, T2])
        loc["qf" + I] = tmp("qf" + I, [D, T2], BF16)
        loc["kf" + I] = tmp("kf" + I, [D, T2], BF16)
        loc["vaug" + I] = tmp("vaug" + I, [T2, 528], BF16)
        loc["sqv" + I] = tmp("sqv" + I, [16, T2])
        loc["qn" + I] = tmp("qn" + I, [16, T2])
        loc["kn" + I] = tmp("kn" + I, [16, T2])
        loc["attnf" + I] = tmp("attnf" + I, [D, T2], BF16)
        loc["sln" + I] = tmp("sln" + I, [128, NCH])
        loc["xn1" + I] = tmp("xn1" + I, [D, T2])
        loc["ure" + I] = tmp("ure" + I, [D, T2])
        loc["uim" + I] = tmp("uim" + I, [D, T2])
        loc["Fc" + I] = tmp("Fc" + I, [D, 2])
        loc["qf2" + I] = tmp("qf2" + I, [D, T2], BF16)
        loc["kf2" + I] = tmp("kf2" + I, [D, T2], BF16)
        loc["vf2" + I] = tmp("vf2" + I, [D, T2], BF16)
        loc["vaug2" + I] = tmp("vaug2" + I, [T2, 1040], BF16)
        loc["qn2" + I] = tmp("qn2" + I, [16, T2])
        loc["kn2" + I] = tmp("kn2" + I, [16, T2])
        loc["vedge" + I] = tmp("vedge" + I, [2, D], BF16)
        loc["vedl" + I] = tmp("vedl" + I, [D], BF16)
        loc["vedr" + I] = tmp("vedr" + I, [D], BF16)
        loc["kq3" + I] = tmp("kq3" + I, [128, 2 * T2])
        loc["qr3" + I] = tmp("qr3" + I, [128, T2])
        loc["kn3" + I] = tmp("kn3" + I, [128, T2])
        loc["v3" + I] = tmp("v3" + I, [128, T2])
        loc["be3" + I] = tmp("be3" + I, [128, T2])
        loc["uw3" + I] = tmp("uw3" + I, [128, T2])
        loc["sm3" + I] = tmp("sm3" + I, [128, T2])
        loc["o32" + I] = tmp("o32" + I, [32, T2])
    Kg0 = tmp("Kg0", [4 * D, T2], BF16)
    VA0 = tmp("VA0", [4 * T2, 528], BF16)
    KN0 = tmp("KN0", [16, 4 * T2])
    gq = tmp("gq", [128, 32])
    Fq = [tmp("Fq%d" % i, [D, 2]) for i in range(4)]
    Fall = tmp("Fall", [D, 8])
    Kg2 = tmp("Kg2", [4 * D, T2], BF16)
    VA2 = tmp("VA2", [4 * T2, 1040], BF16)
    KN2 = tmp("KN2", [16, 4 * T2])
    zed = tmp("zed", [D], BF16)
    kqF = tmp("kqF", [128, 2 * SEQ])
    vF = tmp("vF", [128, SEQ])
    beF = tmp("beF", [128, SEQ])
    uwF = tmp("uwF", [128, SEQ])
    smF = tmp("smF", [128, SEQ])
    o32F = tmp("o32F", [32, SEQ])

    def shpair(name, shape, dtype=F32):
        return (nc.dram_tensor("sh_%s_0" % name, shape, dtype,
                               addr_space="Shared").ap(),
                nc.dram_tensor("sh_%s_1" % name, shape, dtype,
                               addr_space="Shared").ap())

    sh = {}
    for I in "AB":
        sh["kf" + I] = shpair("kf" + I, [D, T2], BF16)
        sh["vaug" + I] = shpair("vaug" + I, [T2, 528], BF16)
        sh["kn" + I] = shpair("kn" + I, [16, T2])
        sh["sln" + I] = shpair("sln" + I, [128, NCH])
        sh["Fc" + I] = shpair("Fc" + I, [D, 2])
        sh["kf2" + I] = shpair("kf2" + I, [D, T2], BF16)
        sh["vaug2" + I] = shpair("vaug2" + I, [T2, 1040], BF16)
        sh["kn2" + I] = shpair("kn2" + I, [16, T2])
        sh["vedge" + I] = shpair("vedge" + I, [2, D], BF16)
        sh["kq3" + I] = shpair("kq3" + I, [128, 2 * T2])
        sh["v3" + I] = shpair("v3" + I, [128, T2])
        sh["be3" + I] = shpair("be3" + I, [128, T2])
        sh["uw3" + I] = shpair("uw3" + I, [128, T2])
        sh["sm3" + I] = shpair("sm3" + I, [128, T2])

    pt = p.sbuf("ex_pt", [1, 1], I32)
    nt = p.sbuf("ex_nt", [1, 1], I32)
    ft = p.sbuf("ex_ft", [1, 1], I32)
    dum = p.sbuf("ex_dum", [1, 1], F32)
    ctx = {"pt": pt, "nt": nt, "ft": ft, "dum": dum}
    p.dma(pt[:], cid, q="poolq", tag="ld_cid")
    p.dma(nt[:], non, q="poolq", tag="ld_nonce")
    with p.phase("init0"):
        zb = p.sbuf("zb", [1, 1024], BF16)
        p.memset(zb[:], 0.0, eng="pool", tag="zb")
        p.dma(zed, zb[0:1, :], q="poolq", tag="zed")

    def xslice(t, i):
        return t[:, i * T2:(i + 1) * T2]

    for i, I in enumerate("AB"):
        with p.phase("p0p" + I, wire={
                "xs": xslice(xs_in, i), "u": loc["u" + I],
                "qf": loc["qf" + I], "kf": loc["kf" + I],
                "vaug": loc["vaug" + I], "sqv": loc["sqv" + I],
                "qn": loc["qn" + I], "kn": loc["kn" + I],
                "wqT": W["l0p_wqT"], "wkT": W["l0p_wkT"],
                "wvT": W["l0p_wvT"], "wqvT": W["l0p_wqvT"],
                "bq": W["l0p_bq"], "bk": W["l0p_bk"], "bv": W["l0p_bv"],
                "bqv": W["l0p_bqv"], "ws": W["l0p_ws"], "bs": W["l0p_bs"],
                "g1": W["l0p_g1"], "b1": W["l0p_b1"]}):
            build_l0a_pre(T2, p=p)

    def gather_plans(kf_n, vaug_n, kn_n, Kg, VA, KN, vaw):
        pre = {0: [], 1: []}
        post = {0: [], 1: []}
        for s in (0, 1):
            qA, qB = 2 * s, 2 * s + 1
            pA, pB = 2 * (1 - s), 2 * (1 - s) + 1
            for i, (qt, pt_) in enumerate(((qA, pA), (qB, pB))):
                I = "AB"[i]
                pre[s] += [
                    (sh[kf_n + I][s], loc[kf_n + I]),
                    (sh[vaug_n + I][s], loc[vaug_n + I]),
                    (sh[kn_n + I][s], loc[kn_n + I]),
                    (Kg[qt * D:(qt + 1) * D, :], loc[kf_n + I]),
                    (VA[qt * T2:(qt + 1) * T2, :], loc[vaug_n + I]),
                    (KN[:, qt * T2:(qt + 1) * T2], loc[kn_n + I]),
                ]
                post[s] += [
                    (Kg[pt_ * D:(pt_ + 1) * D, :], sh[kf_n + I][1 - s]),
                    (VA[pt_ * T2:(pt_ + 1) * T2, :], sh[vaug_n + I][1 - s]),
                    (KN[:, pt_ * T2:(pt_ + 1) * T2], sh[kn_n + I][1 - s]),
                ]
        reads = [loc[kf_n + I] for I in "AB"] + \
                [loc[vaug_n + I] for I in "AB"] + [loc[kn_n + I] for I in "AB"]
        writes = [Kg[:, :], VA[:, :], KN[:, :]]
        return pre[0], pre[1], post[0], post[1], reads, writes

    _fused_exchange(p, 1, ctx, *gather_plans("kf", "vaug", "kn",
                                             Kg0, VA0, KN0, 528))

    for i, I in enumerate("AB"):
        with p.phase("p0f" + I, wire={
                "qf": loc["qf" + I], "Kg": Kg0, "VA": VA0,
                "qn": loc["qn" + I], "KN": KN0, "sqv": loc["sqv" + I],
                "attnf": loc["attnf" + I]}):
            build_l0a_flash(T2, p=p)

    for i, I in enumerate("AB"):
        with p.phase("p0m" + I, wire={
                "xs": xslice(xs_in, i), "u": loc["u" + I],
                "attnf": loc["attnf" + I], "ys": xslice(ys[0], i),
                "wfT": W["l0m_wfT"], "bf": W["l0m_bf"], "g2": W["l0m_g2"],
                "b2": W["l0m_b2"], "wr1T": W["l0m_wr1T"],
                "br1": W["l0m_br1"], "wr2T": W["l0m_wr2T"],
                "br2": W["l0m_br2"]}):
            build_l0a_mid(T2, p=p)

    for i, I in enumerate("AB"):
        with p.phase("p0s" + I, wire={"xs": xslice(ys[0], i),
                                      "sln": loc["sln" + I]}):
            build_lnsum(T2, p=p)

    pre = {0: [], 1: []}
    post = {0: [], 1: []}
    for s in (0, 1):
        for i, I in enumerate("AB"):
            qt, pt_ = 2 * s + i, 2 * (1 - s) + i
            pre[s] += [(sh["sln" + I][s], loc["sln" + I]),
                       (gq[:, qt * 8:(qt + 1) * 8], loc["sln" + I])]
            post[s] += [(gq[:, pt_ * 8:(pt_ + 1) * 8], sh["sln" + I][1 - s])]
    _fused_exchange(p, 2, ctx, pre[0], pre[1], post[0], post[1],
                    [loc["slnA"], loc["slnB"]], [gq[:, :]])

    for i, I in enumerate("AB"):
        with p.phase("p0c" + I, wire={
                "xs": xslice(ys[0], i), "ys": xslice(ys[1], i), "gsums4": gq,
                **{k: W["l0f_" + k] for k in
                   ("cn_w1", "cn_w2", "cn_w3", "cn_w4", "cn_b1", "cn_b2",
                    "cn_b3", "cn_b4", "gc_w1", "gc_w2", "gc_w3", "gc_b1",
                    "gc_b2", "gc_b3", "ar")}}):
            build_l0f(T2, p=p, fused=True)

    for i, I in enumerate("AB"):
        with p.phase("p1a" + I, wire={
                "xs": xslice(ys[1], i), "xn": loc["xn1" + I],
                "ure": loc["ure" + I], "uim": loc["uim" + I],
                "Fc": loc["Fc" + I], "t0col": W["t0A" if i == 0 else "t0B"],
                "nu_log": W["l1_nu_log"], "theta_log": W["l1_theta_log"],
                "BreT": W["l1_BreT"], "BimT": W["l1_BimT"]}):
            build_l1a_a(T2, p=p)

    pre = {0: [], 1: []}
    post = {0: [], 1: []}
    for s in (0, 1):
        for i, I in enumerate("AB"):
            qt, pt_ = 2 * s + i, 2 * (1 - s) + i
            pre[s] += [(sh["Fc" + I][s], loc["Fc" + I]),
                       (Fq[qt], loc["Fc" + I])]
            post[s] += [(Fq[pt_], sh["Fc" + I][1 - s])]
    _fused_exchange(p, 3, ctx, pre[0], pre[1], post[0], post[1],
                    [loc["FcA"], loc["FcB"]], [f for f in Fq])

    with p.phase("fall"):
        fstage = p.sbuf("fstage", [128, 64], F32)
        fout = p.sbuf("fout", [128, 64], F32)
        for q in range(4):
            p.dma(fstage.rearrange("p (q c n) -> p q c n", q=1, c=8, n=2
                                   )[:, 0, :, :] if False else
                  bass.AP(fstage, q * 16, [[64, 128], [1, 16]]),
                  Fq[q].rearrange("(c p) n -> p c n", p=128), q="spq",
                  tag="f_ld")
        for q in range(4):
            for n in range(2):
                p.copy(bass.AP(fout, 2 * q + n, [[64, 128], [8, 8]]),
                       bass.AP(fstage, q * 16 + n, [[64, 128], [2, 8]]),
                       eng="dve", tag="f_as")
        p.dma(Fall.rearrange("(c p) n -> p c n", p=128),
              bass.AP(fout, 0, [[64, 128], [8, 8], [1, 8]]), q="actq",
              tag="f_st")

    for i, I in enumerate("AB"):
        with p.phase("p1b" + I, wire={
                "xs": xslice(ys[1], i), "xn": loc["xn1" + I],
                "ure": loc["ure" + I], "uim": loc["uim" + I],
                "Fall": Fall, "a4": W["a4A" if i == 0 else "a4B"],
                "t0col": W["t0A" if i == 0 else "t0B"],
                "ys": xslice(ys[2], i),
                "nu_log": W["l1_nu_log"], "theta_log": W["l1_theta_log"],
                "D_skip": W["l1_D_skip"], "CreT": W["l1_CreT"],
                "CimT": W["l1_CimT"]}):
            build_l1a_b(T2, p=p)

    for i, I in enumerate("AB"):
        with p.phase("p1f" + I, wire={
                "xs": xslice(ys[2], i), "ys": xslice(ys[3], i),
                "gate_wT": W["l1f_gate_wT"], "wg": W["l1f_wg"],
                "wu": W["l1f_wu"], "wd": W["l1f_wd"], "swg": W["l1f_swg"],
                "swu": W["l1f_swu"], "swd": W["l1f_swd"]}):
            build_l1f(T2, p=p)

    for i, I in enumerate("AB"):
        with p.phase("p2p" + I, wire={
                "xs": xslice(ys[3], i), "t0col": W["t0A" if i == 0 else "t0B"],
                "kidx": W["l2_kidx"], "alt": W["l2_alt"],
                "pswap": W["l2_pswap"], "wqkvT": W["l2_wqkvT"],
                "bqkv": W["l2_bqkv"], "qf": loc["qf2" + I],
                "kf": loc["kf2" + I], "vf": loc["vf2" + I],
                "vaug": loc["vaug2" + I], "qn": loc["qn2" + I],
                "kn": loc["kn2" + I], "vedge": loc["vedge" + I]}):
            build_l2a_pre(T2, p=p, fused=True)

    pre0, pre1, post0, post1, reads, writes = gather_plans(
        "kf2", "vaug2", "kn2", Kg2, VA2, KN2, 1040)
    for s, (pre_l, post_l) in ((0, (pre0, post0)), (1, (pre1, post1))):
        pre_l += [(sh["vedge" + I][s], loc["vedge" + I]) for I in "AB"]
        pre_l += [(loc["vedrA"], loc["vedgeB"][0:1, :]),
                  (loc["vedlB"], loc["vedgeA"][1:2, :])]
    post0 += [(loc["vedlA"], zed), (loc["vedrB"], sh["vedgeA"][1][0:1, :])]
    post1 += [(loc["vedlA"], sh["vedgeB"][0][1:2, :]), (loc["vedrB"], zed)]
    reads += [loc["vedgeA"], loc["vedgeB"]]
    writes += [loc["vedlA"], loc["vedrA"], loc["vedlB"], loc["vedrB"]]
    _fused_exchange(p, 4, ctx, pre0, pre1, post0, post1, reads, writes)

    for i, I in enumerate("AB"):
        with p.phase("p2f" + I, wire={
                "xs": xslice(ys[3], i), "qf": loc["qf2" + I], "Kg": Kg2,
                "VA": VA2, "qn": loc["qn2" + I], "KN": KN2,
                "vf": loc["vf2" + I], "vedl": loc["vedl" + I],
                "vedr": loc["vedr" + I], "wlep": W["l2_wlep"],
                "blep": W["l2_blep"], "ys": xslice(ys[4], i)}):
            build_l2a_flash(T2, p=p)

    for i, I in enumerate("AB"):
        with p.phase("p2m" + I, wire={
                "xs": xslice(ys[4], i), "ys": xslice(ys[5], i),
                "router_w": W["l2f_router_w"], "router_b": W["l2f_router_b"],
                "w1": W["l2f_w1"], "w2": W["l2f_w2"]}):
            build_l2f(T2, p=p)

    for i, I in enumerate("AB"):
        with p.phase("p3p" + I, wire={
                "xs": xslice(ys[5], i), "kq": loc["kq3" + I],
                "qr": loc["qr3" + I], "kn": loc["kn3" + I],
                "v": loc["v3" + I], "be": loc["be3" + I],
                "uw": loc["uw3" + I], "sm": loc["sm3" + I],
                "winT": W["l3_winT"], "wkvT": W["l3_wkvT"],
                "wbT": W["l3_wbT"], "wqrT": W["l3_wqrT"],
                "wrtT": W["l3_wrtT"], "bbeta": W["l3_bbeta"]}):
            build_l3a_pre(T2, p=p, fused=True)

    pre = {0: [], 1: []}
    post = {0: [], 1: []}
    for s in (0, 1):
        for i, I in enumerate("AB"):
            qt, pt_ = 2 * s + i, 2 * (1 - s) + i
            pre[s] += [(sh["kq3" + I][s], loc["kq3" + I]),
                       (kqF[:, 2 * qt * T2:2 * (qt + 1) * T2],
                        loc["kq3" + I])]
            post[s] += [(kqF[:, 2 * pt_ * T2:2 * (pt_ + 1) * T2],
                         sh["kq3" + I][1 - s])]
            for nm, full in (("v3", vF), ("be3", beF), ("uw3", uwF),
                             ("sm3", smF)):
                pre[s] += [(sh[nm + I][s], loc[nm + I]),
                           (full[:, qt * T2:(qt + 1) * T2], loc[nm + I])]
                post[s] += [(full[:, pt_ * T2:(pt_ + 1) * T2],
                             sh[nm + I][1 - s])]
    _fused_exchange(p, 5, ctx, pre[0], pre[1], post[0], post[1],
                    [loc[nm + I] for nm in ("kq3", "v3", "be3", "uw3", "sm3")
                     for I in "AB"],
                    [kqF[:, :], vF[:, :], beF[:, :], uwF[:, :], smF[:, :]])

    with p.phase("scan", wire={"kqA": kqF, "vA": vF, "beA": beF,
                               "uwA": uwF, "smA": smF, "o32": o32F}):
        build_l3a_scan(SEQ, p=p)

    _fused_sidecopy(
        p, 6, ctx,
        [(loc["o32A"], o32F[:, 0:T2]), (loc["o32B"], o32F[:, T2:2 * T2])],
        [(loc["o32A"], o32F[:, 2 * T2:3 * T2]),
         (loc["o32B"], o32F[:, 3 * T2:4 * T2])],
        [o32F[:, :]], [loc["o32A"], loc["o32B"]])

    for i, I in enumerate("AB"):
        with p.phase("p3o" + I, wire={
                "xs": xslice(ys[5], i), "o32": loc["o32" + I],
                "woT": W["l3_woT"], "wT": W["l3_wT"],
                "ys": xslice(out_ys, i)}):
            build_l3a_post(T2, p=p)

    if debug:
        for k in range(6):
            dbg = p.dram_out("dbg%d" % k, [D, 2 * T2])
            p.dma(dbg, ys[k], q="spq", tag="dbg%d" % k)
    return p


def kernel_fused(debug=False, **ins):
    asc = np.ascontiguousarray
    gi = lambda k: np.asarray(ins[k])
    x = gi("L0_attn_x")
    key = "fused_dbg" if debug else "fused"
    if key not in _PROGS:
        _PROGS[key] = build_fused(debug=debug)
    prog = _PROGS[key]

    perm_k = [blk * 64 + n for blk in range(4) for n in range(32)]
    perm_v = [blk * 64 + 32 + n for blk in range(4) for n in range(32)]
    kidx = ((np.arange(128) % 64) // 2).astype(np.float32)
    alt = np.where(np.arange(128) % 2 == 0, -1.0, 1.0).astype(np.float32)
    pswap = np.zeros((128, 128), np.float32)
    for i in range(128):
        pswap[i ^ 1, i] = 1.0
    wk = {
        "l0p_wqT": asc(gi("L0_attn_Wq").T), "l0p_wkT": asc(gi("L0_attn_Wk").T),
        "l0p_wvT": asc(gi("L0_attn_Wv").T),
        "l0p_wqvT": asc(gi("L0_attn_Wqv").T),
        "l0p_bq": gi("L0_attn_bq"), "l0p_bk": gi("L0_attn_bk"),
        "l0p_bv": gi("L0_attn_bv"), "l0p_bqv": gi("L0_attn_bqv"),
        "l0p_ws": gi("L0_attn_Ws"), "l0p_bs": gi("L0_attn_bs"),
        "l0p_g1": gi("L0_attn_g1"), "l0p_b1": gi("L0_attn_b1"),
        "l0m_wfT": asc(gi("L0_attn_Wf").T), "l0m_bf": gi("L0_attn_bf"),
        "l0m_g2": gi("L0_attn_g2"), "l0m_b2": gi("L0_attn_b2"),
        "l0m_wr1T": asc(gi("L0_attn_Wr1").T), "l0m_br1": gi("L0_attn_br1"),
        "l0m_wr2T": asc(gi("L0_attn_Wr2").T), "l0m_br2": gi("L0_attn_br2"),
        "l1_nu_log": gi("L1_attn_nu_log"),
        "l1_theta_log": gi("L1_attn_theta_log"),
        "l1_BreT": asc(gi("L1_attn_B_re").T),
        "l1_BimT": asc(gi("L1_attn_B_im").T),
        "l1_CreT": asc(gi("L1_attn_C_re").T),
        "l1_CimT": asc(gi("L1_attn_C_im").T),
        "l1_D_skip": gi("L1_attn_D_skip"),
        "l1f_gate_wT": asc(gi("L1_ffn_gate_w").T), "l1f_wg": gi("L1_ffn_w_gate"),
        "l1f_wu": gi("L1_ffn_w_up"), "l1f_wd": gi("L1_ffn_w_down"),
        "l1f_swg": gi("L1_ffn_sw_gate"), "l1f_swu": gi("L1_ffn_sw_up"),
        "l1f_swd": gi("L1_ffn_sw_down"),
        "l2_wqkvT": asc(gi("L2_attn_w_qkv").T), "l2_bqkv": gi("L2_attn_b_qkv"),
        "l2_kidx": kidx, "l2_alt": alt, "l2_pswap": pswap,
        "l2_wlep": gi("L2_attn_w_lepe").reshape(D, 3),
        "l2_blep": gi("L2_attn_b_lepe"),
        "l2f_router_w": gi("L2_ffn_router_w"),
        "l2f_router_b": gi("L2_ffn_router_b"),
        "l2f_w1": gi("L2_ffn_w1"), "l2f_w2": gi("L2_ffn_w2"),
        "l3_winT": asc(gi("L3_attn_W_in").T),
        "l3_wkvT": asc(gi("L3_attn_W_kv")[perm_k + perm_v].T),
        "l3_wbT": asc(gi("L3_attn_W_beta").T),
        "l3_wqrT": asc(np.tile(gi("L3_attn_W_q"), (4, 1)).T),
        "l3_wrtT": asc(gi("L3_attn_W_router").T),
        "l3_bbeta": gi("L3_attn_b_beta").reshape(-1),
        "l3_woT": asc(gi("L3_attn_W_out").T),
        "l3_wT": asc(gi("L3_ffn_weight").T),
    }
    for k in ("l0f_cn_w1", "l0f_cn_w2", "l0f_cn_w3", "l0f_cn_w4",
              "l0f_cn_b1", "l0f_cn_b2", "l0f_cn_b3", "l0f_cn_b4",
              "l0f_gc_w1", "l0f_gc_w2", "l0f_gc_w3",
              "l0f_gc_b1", "l0f_gc_b2", "l0f_gc_b3"):
        wk[k] = gi("L0_ffn_" + k[4:])
    wk["l0f_ar"] = gi("L0_ffn_adaptive_range")

    nonce = int(np.random.randint(1, 2**31 - 1))
    in_maps = []
    for c in range(8):
        s = c % 2
        b = (c // 2) % 2
        qA, qB = 2 * s, 2 * s + 1
        a4A = np.full(4, 1e7, np.float32)
        for p_ in range(qA):
            a4A[p_] = 2048.0 * (qA - 1 - p_)
        a4B = np.full(4, 1e7, np.float32)
        for p_ in range(qB):
            a4B[p_] = 2048.0 * (qB - 1 - p_)
        m = dict(wk)
        m["xs"] = asc(x[b, s * 2 * T2:(s + 1) * 2 * T2, :].T)
        m["cid"] = np.array([[s]], np.int32)
        m["nonce"] = np.array([[nonce]], np.int32)
        m["t0A"] = np.full(128, qA * T2, np.float32)
        m["t0B"] = np.full(128, qB * T2, np.float32)
        m["a4A"] = a4A
        m["a4B"] = a4B
        in_maps.append(m)
    res = prog.run(in_maps)
    out = np.empty((2, SEQ, D), np.float32)
    for c in range(4):
        s, b = c % 2, (c // 2) % 2
        out[b, s * 2 * T2:(s + 1) * 2 * T2, :] = \
            np.asarray(res.results[c]["ys"]).T
    return out, res


def kernel(**ins):
    out, _ = kernel_fused(**ins)
    return np.ascontiguousarray(out)
